# version 08
# baseline (speedup 1.0000x reference)
import numpy as np
import concourse.bass as bass
import concourse.mybir as mybir
from concourse.bass_utils import run_bass_kernel_spmd
from concourse.ap import AP
from concourse.bass_core_rust import IndirectOffsetOnAxis
from contextlib import ExitStack

F32 = mybir.dt.float32
BF16 = mybir.dt.bfloat16
U32 = mybir.dt.uint32
AF = mybir.ActivationFunctionType
ALU = mybir.AluOpType
AX = mybir.AxisListType

P = 128
NT = 16
D = 1024
KC = 8
H = 16
DH = 64
M = 256
E = 8
TOK = 2048
STAGE = 4
DEBUG = 0
TRACE = None


def build(stage):
    nc = bass.Bass("TRN2", target_bir_lowering=False)

    xin = nc.dram_tensor("xin", [TOK, D], F32, kind="ExternalInput").ap()
    wq_d = nc.dram_tensor("wq", [D, D], F32, kind="ExternalInput").ap()
    wk_d = nc.dram_tensor("wk", [D, D], F32, kind="ExternalInput").ap()
    wv_d = nc.dram_tensor("wv", [D, D], F32, kind="ExternalInput").ap()
    om_d = nc.dram_tensor("om", [M, DH], F32, kind="ExternalInput").ap()
    xout = nc.dram_tensor("xout", [TOK, D], F32, kind="ExternalOutput").ap()
    dbg = {}
    if DEBUG:
        for nm, shp, dt in [("d_xnf", [P, KC * TOK], BF16), ("d_q", [P, KC * TOK], BF16),
                            ("d_v", [P, NT * D], BF16), ("d_kvl", [P, H * M], BF16),
                            ("d_nq", [P, H * NT], F32), ("d_nk", [P, H * NT], F32),
                            ("d_sqe", [P, H * NT], F32)]:
            dbg[nm] = nc.dram_tensor(nm, shp, dt, kind="ExternalOutput").ap()
        if stage >= 3:
            for nm, shp, dt in [("d_k", [P, 384], BF16), ("d_p", [P, 4096], F32),
                                ("d_s", [P, 4096], BF16), ("d_y", [P, NT * D], BF16),
                                ("d_yf", [P, NT * D], BF16),
                                ("d_uss", [P, NT * D], BF16),
                                ("d_w", [P * 64, 256], BF16)]:
                dbg[nm] = nc.dram_tensor(nm, shp, dt, kind="ExternalOutput").ap()
    kvloc_d = nc.dram_tensor("kvloc_d", [65, H * M], BF16, kind="Internal").ap()
    lnb_d = nc.dram_tensor("lnb_d", [P, 2 * D], BF16, kind="Internal").ap()
    cb_d = nc.dram_tensor("cb_d", [P, 256], F32, kind="Internal").ap()
    ozb_d = nc.dram_tensor("ozb_d", [P, 1040], F32, kind="Internal").ap()
    kvagg_d = nc.dram_tensor("kvagg_d", [65, H * M], BF16, kind="Internal").ap()
    if stage >= 2:
        wg_d = nc.dram_tensor("wg", [D, D], F32, kind="ExternalInput").ap()
        bg_d = nc.dram_tensor("bg", [D], F32, kind="ExternalInput").ap()
        we_d = nc.dram_tensor("we", [E * D, D], F32, kind="ExternalInput").ap()
        be_d = nc.dram_tensor("be", [E, D], F32, kind="ExternalInput").ap()
    if stage >= 3:
        ldt_d = nc.dram_tensor("ldt", [P, 1], F32, kind="ExternalInput").ap()
        are_d = nc.dram_tensor("are", [P, 32], F32, kind="ExternalInput").ap()
        aim_d = nc.dram_tensor("aim", [P, 32], F32, kind="ExternalInput").ap()
        cre_d = nc.dram_tensor("cre", [P, 32], F32, kind="ExternalInput").ap()
        cim_d = nc.dram_tensor("cim", [P, 32], F32, kind="ExternalInput").ap()
        ddb_d = nc.dram_tensor("ddb", [P, 1], F32, kind="ExternalInput").ap()
        uidx_d = nc.dram_tensor("uidx", [P, P], U32, kind="ExternalInput").ap()
        ussidx_d = nc.dram_tensor("ussidx", [P, 8], U32, kind="ExternalInput").ap()
        yidx_d = nc.dram_tensor("yidx", [P, P], U32, kind="ExternalInput").ap()
        xg_loc = nc.dram_tensor("xg_loc", [P, NT * D], BF16, kind="Internal").ap()
        xg_all_t = nc.dram_tensor("xg_all", [8 * P, NT * D], BF16, kind="Internal")
        xg_all = xg_all_t.ap()
        xg2k = xg_all.rearrange("a (b w) -> (a b) w", w=TOK)
        xg128 = xg_all.rearrange("a (b w) -> (a b) w", w=P)
        k_dram = nc.dram_tensor("k_dram", [P, 384], BF16, kind="Internal").ap()
        w_dram = nc.dram_tensor("w_dram", [P * 64, 256], BF16, kind="Internal").ap()
        pp_dram = nc.dram_tensor("pp_dram", [P * 256, 64], BF16, kind="Internal").ap()
        p_dram = nc.dram_tensor("p_dram", [P, 4096], F32, kind="Internal").ap()
        s_dram = nc.dram_tensor("s_dram", [P, 4096], BF16, kind="Internal").ap()
        y_dram = nc.dram_tensor("y_dram", [P, NT * D], BF16, kind="Internal").ap()
        yfin_dram = nc.dram_tensor("yfin", [P, NT * D], BF16, kind="Internal").ap()
        yg_all_t = nc.dram_tensor("yg_all", [8 * P, NT * D], BF16, kind="Internal")
        yg_all = yg_all_t.ap()
        yg128 = yg_all.rearrange("a (b w) -> (a b) w", w=P)

    ctx = ExitStack()
    with ctx:
        sem_names = ["qS", "qG", "cT", "cV", "cS", "cP", "car", "tA", "tB", "tC"]
        SM = {n: ctx.enter_context(nc.semaphore(n)) for n in sem_names}
        C = {n: 0 for n in sem_names}

        def inc(n, instr, k=1):
            if TRACE is not None:
                TRACE.append(("inc", str(instr.ins.engine), n, k,
                              getattr(instr.ins, "name", "?")))
            instr.then_inc(SM[n], k)
            C[n] += k
            return instr

        def w(eng, n, v):
            if TRACE is not None:
                TRACE.append(("wait", str(eng.engine), n, v, "w"))
            if v > 0:
                eng.wait_ge(SM[n], v)

        def sb(name, shape, dt):
            return ctx.enter_context(nc.sbuf_tensor(name, shape, dt))

        X = sb("X", [P, NT * D], F32)
        XNF = sb("XNF", [P, KC * TOK], BF16)
        Q = sb("Q", [P, KC * TOK], BF16)
        V = sb("V", [P, NT * D], BF16)
        AR1 = sb("AR1", [P, 8192], BF16)
        KVL = sb("KVL", [P, H * M], BF16)
        SCR2 = sb("SCR2", [P, TOK], BF16)
        SQF = sb("SQF", [P, TOK], F32)
        OZB = sb("OZB", [P, 16 * 65], F32)
        NQ = sb("NQ", [P, H * NT], F32)
        NK = sb("NK", [P, H * NT], F32)
        SQE = sb("SQE", [P, H * NT], F32)
        WSTJ = sb("WSTJ", [P, 2 * P], F32)
        ONESB = sb("ONESB", [P, P], BF16)
        IDENT = sb("IDENT", [P, P], BF16)
        IDENTF = sb("IDENTF", [P, P], F32)
        OMT2 = sb("OMT2", [P, M], BF16)
        OM32 = sb("OM32", [P, 2 * DH], F32)
        OMB = sb("OMB", [P, 2 * DH], BF16)
        BNB = sb("BNB", [P, 12], F32)
        MVB = sb("MVB", [P, 4], F32)
        MVB2 = sb("MVB2", [P, 2], F32)
        RSTD = sb("RSTD", [P, 4], F32)
        ZT = sb("ZT", [P, 4], F32)
        EPSC = sb("EPSC", [P, 1], F32)
        ZD = sb("ZD", [P, 4 * NT], F32)
        BEBT = sb("BEBT", [P, E * KC], F32)
        IDXB = sb("IDXB", [P, 8], U32)
        UIDXS = sb("UIDXS", [P, P], U32)
        USIDX = sb("USIDX", [P, 8], U32)

        KBUF = [AR1[:, 0:TOK], AR1[:, TOK:2 * TOK]]
        KVF = AR1[:, 0:H * 2 * 65]
        PQB = [AR1[:, 2 * TOK:3 * TOK], AR1[:, 3 * TOK:4 * TOK]]
        WBV = AR1[:, :]
        WBJ = [AR1[:, 4096:4096 + D], AR1[:, 5120:5120 + D]]
        PKALL = AR1[:, 6144:6144 + 8 * M]
        XNT = [SCR2[:, 0:D], SCR2[:, D:2 * D]]
        SQH = SCR2[:, :]

        block = ctx.enter_context(nc.Block())

        PT0 = ctx.enter_context(nc.psum_tensor("PT0", [P, P], BF16))
        PT1 = ctx.enter_context(nc.psum_tensor("PT1", [P, P], BF16))
        PA0 = ctx.enter_context(nc.psum_tensor("PA0", [P, 512], F32))
        PA1 = ctx.enter_context(nc.psum_tensor("PA1", [P, 512], F32))
        PK0 = ctx.enter_context(nc.psum_tensor("PK0", [P, M], F32))
        PK1 = ctx.enter_context(nc.psum_tensor("PK1", [P, M], F32))
        PKV = ctx.enter_context(nc.psum_tensor("PKV", [64, M], F32))
        PPS = ctx.enter_context(nc.psum_tensor("PPS", [1, M], F32))
        PA = [PA0, PA1]
        PT = [PT0, PT1]
        PH4 = [PK0, PK1, PA0, PA1]

        def dummy_mm(tn):
            return tn.matmul(PA0[0:1, 0:1], ONESB[0:1, 0:1], ONESB[0:1, 0:1],
                             start=True, stop=True)

        def dummy_ps(tn):
            return tn.matmul(PPS[0:1, 0:1], ONESB[0:1, 0:1], ONESB[0:1, 0:1],
                             start=True, stop=True)

        def dummy_tr(tn):
            return tn.transpose(PT0[0:1, 0:1], ONESB[0:1, 0:1], IDENT[0:1, 0:1])

        @block.sync
        def _(sync):
            xt = xin.rearrange("(t p) d -> t p d", p=P)
            for t in range(NT):
                w(sync, "cP", 1)
                inc("qS", sync.dma_start(out=X[:, t * D:(t + 1) * D], in_=xt[t]), 16)
            omr = om_d.rearrange("(c p) d -> c p d", p=P)
            for c in range(2):
                w(sync, "cP", 1)
                inc("qS", sync.dma_start(out=OM32[:, c * DH:(c + 1) * DH], in_=omr[c]), 16)
        qs_x = [16 * (t + 1) for t in range(NT)]
        qs_om = C["qS"]

        @block.gpsimd
        def _(g):
            g.memset(KVL[64:P, :], 0.0)
            g.memset(EPSC[:, :], 1e-5)
            inc("cP", g.memset(ONESB[:, :], 1.0))
            w(g, "cP", 1)
            inc("cP", g.affine_select(IDENT[:, :], ONESB[:, :], pattern=[[-1, P]],
                                      compare_op=ALU.is_equal, fill=0.0, base=0,
                                      channel_multiplier=1))
            w(g, "cP", 2)
            inc("cP", g.tensor_copy(IDENTF[:, :], IDENT[:, :]))
            w(g, "qS", qs_om)
            inc("cP", g.tensor_copy(OMB[:, :], OM32[:, :]))
        cp_const = C["cP"]

        @block.tensor
        def _(tn):
            w(tn, "cP", cp_const)
            inc("cT", tn.transpose(PT0[0:64, :], OMB[:, 0:DH], IDENT[:, :]))
            inc("cT", tn.transpose(PT1[0:64, :], OMB[:, DH:2 * DH], IDENT[:, :]))
            inc("cT", dummy_mm(tn))
        ct_om = C["cT"]

        @block.vector
        def _(v):
            w(v, "cT", ct_om - 1)
            v.tensor_copy(OMT2[0:64, 0:P], PT0[0:64, :])
            w(v, "cT", ct_om)
            inc("cV", v.tensor_copy(OMT2[0:64, P:2 * P], PT1[0:64, :]))
        cv_omt = C["cV"]

        @block.gpsimd
        def _(g):
            w(g, "cV", cv_omt)
            inc("qG", g.dma_start(out=OMT2[64:P, :], in_=OMT2[0:64, :]), 16)
        qg_omt2 = C["qG"]

        def emit_ln_to_fm(perm=None):
            vb, sbb, tb = C["tA"], C["tB"], C["tC"]
            ng = NT * KC

            def rstd_ms2(t):
                return sbb + (2 * t + 2 if t <= 1 else 10 * t - 6)

            def ev_ms2(t2, c2):
                return sbb + (10 * t2 + 5 + c2 if t2 <= 14 else 153 + c2)

            pL = C["cP"]

            @block.gpsimd
            def _(g):
                for t in range(NT):
                    w(g, "tA", vb + 3 * t + 3)
                    sl = (t % 2) * D
                    inc("qG", g.dma_start(out=lnb_d[:, sl:sl + D],
                                          in_=XNT[t % 2][:, :]), 16)
                    w(g, "qG", C["qG"])
                    inc("qG", g.dma_start(out=XNT[t % 2][:, :],
                                          in_=lnb_d[:, sl:sl + D]), 16)
                    w(g, "qG", C["qG"])
                    inc("cP", g.tensor_copy(ZT[:, 1:2], ZT[:, 0:1]))

            @block.tensor
            def _(tn):
                w(tn, "cV", cv_omt)
                for t in range(NT):
                    w(tn, "cP", pL + t + 1)
                    for c in range(KC):
                        gidx = t * 8 + c
                        if gidx >= 2:
                            t2, c2 = (gidx - 2) // 8, (gidx - 2) % 8
                            w(tn, "tB", ev_ms2(t2, c2))
                        inc("tC", tn.transpose(
                            PT[gidx % 2][:, :],
                            XNT[t % 2][:, c * P:(c + 1) * P],
                            IDENT[:, :] if perm is None else perm))
                w(tn, "tB", ev_ms2(15, 6))
                inc("tC", dummy_tr(tn))

            @block.vector
            def _(v):
                for t in range(NT):
                    w(v, "qS", qs_x[t])
                    if t >= 2:
                        w(v, "tB", rstd_ms2(t - 2))
                    v.bn_stats(BNB[:, 0:6], X[:, t * D:t * D + 512])
                    inc("tA", v.bn_stats(BNB[:, 6:12], X[:, t * D + 512:(t + 1) * D]))
                    w(v, "tA", vb + 3 * t + 1)
                    inc("tA", v.bn_aggr(MVB[:, 2 * (t % 2):2 * (t % 2) + 2], BNB[:, :]))
                    w(v, "tA", vb + 3 * t + 2)
                    w(v, "tB", rstd_ms2(t))
                    if t >= 2:
                        w(v, "tC", tb + 8 * (t - 2) + 8)
                    inc("tA", v.tensor_scalar(
                        XNT[t % 2], X[:, t * D:(t + 1) * D],
                        MVB[:, 2 * (t % 2):2 * (t % 2) + 1],
                        RSTD[:, t % 2:t % 2 + 1],
                        ALU.subtract, ALU.mult))

            def rstd_ms(t):
                return sbb + (2 * t + 2 if t <= 1 else 10 * t - 6)

            def ev_ms(t2, c2):
                return sbb + (10 * t2 + 5 + c2 if t2 <= 14 else 153 + c2)

            @block.scalar
            def _(s):
                w(s, "cP", cp_const)

                def lnexp(t):
                    w(s, "tA", vb + 3 * t + 2)
                    mvv = MVB[:, 2 * (t % 2) + 1:2 * (t % 2) + 2]
                    inc("tB", s.activation(RSTD[:, t % 2:t % 2 + 1], mvv,
                                           AF.Ln, bias=EPSC[:, 0:1]))
                    w(s, "tB", rstd_ms(t) - 1)
                    inc("tB", s.activation(RSTD[:, t % 2:t % 2 + 1],
                                           RSTD[:, t % 2:t % 2 + 1],
                                           AF.Exp, scale=-0.5))

                lnexp(0)
                lnexp(1)
                for t in range(NT):
                    for c in range(KC):
                        gidx = t * 8 + c
                        w(s, "tC", tb + min(gidx + 2, ng + 1))
                        inc("tB", s.activation(
                            XNF[:, c * TOK + t * P:c * TOK + (t + 1) * P],
                            PT[gidx % 2][:, :], AF.Copy))
                    if t + 2 < NT:
                        lnexp(t + 2)

            assert C["tA"] == vb + 3 * NT
            assert C["tB"] == sbb + 10 * NT
            assert C["tC"] == tb + ng + 1
            return C["tB"]

        def emit_col_weights(wdram, ready_sig, mmdone):
            qb, pb = C["qS"], C["cP"]

            @block.sync
            def _(sync):
                for jc in range(KC):
                    for kc in range(KC):
                        gi = jc * KC + kc
                        if gi >= 2:
                            w(sync, "cP", pb + gi - 1)
                        else:
                            w(sync, "cP", pb)
                        src = AP(wdram.tensor, (kc * P) * D + jc * P,
                                 [[D, P], [1, P]])
                        inc("qS", sync.dma_start(
                            out=WSTJ[:, (gi % 2) * P:(gi % 2 + 1) * P], in_=src), 16)

            @block.gpsimd
            def _(g):
                if ready_sig is not None:
                    w(g, ready_sig[0], ready_sig[1])
                for jc in range(KC):
                    if mmdone is not None and jc >= 2:
                        w(g, "cT", mmdone(jc - 2))
                    for kc in range(KC):
                        gi = jc * KC + kc
                        w(g, "qS", qb + 16 * (gi + 1))
                        inc("cP", g.tensor_copy(
                            WBJ[jc % 2][:, kc * P:(kc + 1) * P],
                            WSTJ[:, (gi % 2) * P:(gi % 2 + 1) * P]))
            assert C["cP"] == pb + 64
            return pb

        def emit_attn(xnf_ready_tB):
            qb0, pb0 = C["qS"], C["cP"]

            @block.sync
            def _(sync):
                for gi in range(KC * KC):
                    if gi >= 2:
                        w(sync, "cP", pb0 + gi - 1)
                    else:
                        w(sync, "cP", cp_const)
                    kc, jj = gi // KC, gi % KC
                    src = AP(wv_d.tensor, (kc * P) * D + jj * P, [[D, P], [1, P]])
                    inc("qS", sync.dma_start(
                        out=WSTJ[:, (gi % 2) * P:(gi % 2 + 1) * P], in_=src), 16)

            @block.gpsimd
            def _(g):
                for gi in range(KC * KC):
                    kc, jj = gi // KC, gi % KC
                    w(g, "qS", qb0 + 16 * (gi + 1))
                    inc("cP", g.tensor_copy(
                        WBV[:, kc * D + jj * P:kc * D + (jj + 1) * P],
                        WSTJ[:, (gi % 2) * P:(gi % 2 + 1) * P]))
            wv_ready = C["cP"]

            tb0, sb0 = C["cT"], C["cS"]
            ngv = NT * 2

            @block.tensor
            def _(tn):
                w(tn, "cP", wv_ready)
                w(tn, "tB", xnf_ready_tB)
                for gidx in range(ngv):
                    t, nt = gidx // 2, gidx % 2
                    if gidx >= 2:
                        w(tn, "cS", sb0 + gidx - 1)
                    pa = PA[gidx % 2]
                    for kc in range(KC):
                        mm = tn.matmul(
                            pa[:, :],
                            XNF[:, kc * TOK + t * P:kc * TOK + (t + 1) * P],
                            WBV[:, kc * D + nt * 512:kc * D + (nt + 1) * 512],
                            start=(kc == 0), stop=(kc == KC - 1))
                    inc("cT", mm)
                w(tn, "cS", sb0 + 31)
                inc("cT", dummy_mm(tn))

            @block.scalar
            def _(s):
                for gidx in range(ngv):
                    t, nt = gidx // 2, gidx % 2
                    w(s, "cT", tb0 + min(gidx + 2, ngv + 1))
                    inc("cS", s.activation(
                        V[:, t * D + nt * 512:t * D + (nt + 1) * 512],
                        PA[gidx % 2][:, :], AF.Copy))
            v_done_cS = C["cS"]
            assert C["cT"] == tb0 + ngv + 1

            tb1 = C["cT"]
            wq_pb = emit_col_weights(wq_d, ("cT", tb1),
                                     mmdone=lambda jc: tb1 + 38 * jc + 4)
            sb1, vb1 = C["cS"], C["cV"]

            def q_drain(jc, g):
                if g < 32 - 1:
                    return tb1 + 38 * jc + 5 + g + 2
                return tb1 + 38 * jc + 38

            @block.tensor
            def _(tn):
                w(tn, "cS", v_done_cS)
                for jc in range(KC):
                    TJ = tb1 + 38 * jc
                    SJ = sb1 + 8 * jc
                    VJ = vb1 + 32 * jc
                    w(tn, "cP", wq_pb + 8 * (jc + 1))
                    w(tn, "cS", SJ)
                    w(tn, "cV", VJ)
                    for nt in range(4):
                        gidx = jc * 4 + nt
                        if gidx >= 2:
                            g2 = gidx - 2
                            w(tn, "cS", sb1 + 8 * (g2 // 4) + 2 * (g2 % 4) + 2)
                        pa = PA[nt % 2]
                        for kc in range(KC):
                            mm = tn.matmul(
                                pa[:, :],
                                WBJ[jc % 2][:, kc * P:(kc + 1) * P],
                                XNF[:, kc * TOK + nt * 512:kc * TOK + (nt + 1) * 512],
                                start=(kc == 0), stop=(kc == KC - 1))
                        inc("cT", mm)
                    inc("cT", dummy_tr(tn))
                    w(tn, "cS", SJ + 8)
                    for g in range(32):
                        hh, t = g // NT, g % NT
                        if g >= 4:
                            w(tn, "cV", VJ + g - 3)
                        elif g >= 2:
                            w(tn, "cS", SJ + 2 * g + 2)
                        inc("cT", tn.matmul(
                            PH4[g % 4][:, 0:1],
                            SQH[64 * hh:64 * hh + 64, t * P:(t + 1) * P],
                            ONESB[64 * hh:64 * hh + 64, 0:1],
                            start=True, stop=True))
                    inc("cT", dummy_tr(tn))

            @block.scalar
            def _(s):
                for jc in range(KC):
                    TJ = tb1 + 38 * jc
                    SJ = sb1 + 8 * jc
                    for nt in range(4):
                        gidx = jc * 4 + nt
                        if nt < 3:
                            w(s, "cT", TJ + nt + 2)
                        else:
                            w(s, "cT", TJ + 5)
                        if jc > 0 and nt == 0:
                            w(s, "cT", TJ)
                        pa = PA[nt % 2]
                        inc("cS", s.activation(
                            Q[:, jc * TOK + nt * 512:jc * TOK + (nt + 1) * 512],
                            pa[:, :], AF.Copy))
                        inc("cS", s.activation(
                            SQH[:, nt * 512:(nt + 1) * 512], pa[:, :], AF.Square))

            @block.vector
            def _(v):
                for jc in range(KC):
                    TJ = tb1 + 38 * jc
                    VJ = vb1 + 32 * jc
                    for g in range(32):
                        hh, t = g // NT, g % NT
                        h = 2 * jc + hh
                        w(v, "cT", q_drain(jc, g))
                        inc("cV", v.tensor_scalar(
                            NQ[:, h * NT + t:h * NT + t + 1],
                            PH4[g % 4][:, 0:1], -0.5, None, ALU.mult))

            assert C["cT"] == tb1 + 38 * KC
            assert C["cS"] == sb1 + 8 * KC
            assert C["cV"] == vb1 + 32 * KC
            nq_done_cV = C["cV"]

            sqr = C["cP"]

            @block.gpsimd
            def _(g):
                w(g, "cV", nq_done_cV)
                inc("qG", g.dma_start(out=cb_d[:, :], in_=NQ[:, :]), 16)
                w(g, "qG", C["qG"])
                inc("qG", g.dma_start(out=NQ[:, :], in_=cb_d[:, :]), 16)
                w(g, "qG", C["qG"])
                inc("cP", g.tensor_copy(ZT[:, 1:2], ZT[:, 0:1]))

            @block.scalar
            def _(s):
                w(s, "cP", sqr + 1)
                inc("cS", s.activation(SQE[:, :], NQ[:, :], AF.Exp))
            sqe_mid_cS = C["cS"]

            @block.gpsimd
            def _(g):
                w(g, "cS", sqe_mid_cS)
                inc("qG", g.dma_start(out=cb_d[:, :], in_=SQE[:, :]), 16)
                w(g, "qG", C["qG"])
                inc("qG", g.dma_start(out=SQE[:, :], in_=cb_d[:, :]), 16)
                w(g, "qG", C["qG"])

            @block.scalar
            def _(s):
                w(s, "qG", C["qG"])
                inc("cS", s.activation(ZT[0:1, 2:3], ZT[0:1, 0:1], AF.Copy))
            sqe_done_cS = C["cS"]

            tb2 = C["cT"]
            sb2, vb2 = C["cS"], C["cV"]
            qbK, pbK = C["qS"], C["cP"]
            if TRACE is not None:
                print("BASES tb2=%d sb2=%d vb2=%d qbK=%d pbK=%d" % (tb2, sb2, vb2, qbK, pbK))
            def k_norm_drain(jc, g):
                if g < 32 - 1:
                    return tb2 + 97 * jc + 5 + g + 2
                return tb2 + 97 * jc + 37 + 1

            OBASE = [0, 6, 12, 21, 27, 36, 42, 51]
            KVPOS = [21, 36, 51, 60]

            def phi_drain(jc, hf, oct_i, i):
                base = tb2 + 97 * jc + 37 + OBASE[2 * hf + oct_i]
                if i < 3:
                    return base + i + 2
                return base + 6

            def kv_drain(jc, hf):
                return tb2 + 97 * jc + 37 + KVPOS[hf]

            def cast_pos(gi):
                jc = gi // KC
                relays_before = 9 * max(0, jc - 1) if jc >= 2 else 0
                return pbK + gi + 1 + relays_before

            def nk_relay_pos(jc2):
                casts_before = 8 * min(KC, jc2 + 2)
                return pbK + casts_before + 9 * jc2 + 1

            def relay_pos(jc2, oct_i):
                casts_before = 8 * min(KC, jc2 + 2)
                return pbK + casts_before + 9 * jc2 + 1 + oct_i + 1

            @block.sync
            def _(sync):
                for jc in range(KC):
                    for kc in range(KC):
                        gi = jc * KC + kc
                        if gi >= 2:
                            w(sync, "cP", cast_pos(gi - 2))
                        else:
                            w(sync, "cP", pbK)
                        src = AP(wk_d.tensor, (kc * P) * D + jc * P,
                                 [[D, P], [1, P]])
                        inc("qS", sync.dma_start(
                            out=WSTJ[:, (gi % 2) * P:(gi % 2 + 1) * P], in_=src), 16)

            @block.gpsimd
            def _(g):
                w(g, "cT", tb2)

                def dma_hop():
                    g.tensor_copy(ZT[:, 1:2], ZT[:, 0:1])

                def emit_relay(jc2):
                    w(g, "cV", vb2 + 41 * jc2 + 33)
                    nkc = NK[:, 32 * jc2:32 * jc2 + 32]
                    inc("qG", g.dma_start(out=cb_d[:, 0:32], in_=nkc), 16)
                    w(g, "qG", C["qG"])
                    inc("qG", g.dma_start(out=nkc, in_=cb_d[:, 0:32]), 16)
                    w(g, "qG", C["qG"])
                    inc("cP", g.tensor_copy(ZT[:, 1:2], ZT[:, 0:1]))
                    for oct_i in range(8):
                        w(g, "cS", sb2 + 40 * jc2 + 8 + oct_i * 4 + 4)
                        half = (oct_i % 2) * 1024
                        inc("qG", g.dma_start(
                            out=lnb_d[:, 0:1024],
                            in_=PKALL[:, half:half + 1024]), 16)
                        w(g, "qG", C["qG"])
                        inc("qG", g.dma_start(
                            out=PKALL[:, half:half + 1024],
                            in_=lnb_d[:, 0:1024]), 16)
                        w(g, "qG", C["qG"])
                        inc("cP", g.tensor_copy(ZT[:, 1:2], ZT[:, 0:1]))

                for jc in range(KC):
                    if jc >= 2:
                        w(g, "cT", tb2 + 97 * (jc - 2) + 4)
                        emit_relay(jc - 2)
                    for kc in range(KC):
                        gi = jc * KC + kc
                        w(g, "qS", qbK + 16 * (gi + 1))
                        inc("cP", g.tensor_copy(
                            WBJ[jc % 2][:, kc * P:(kc + 1) * P],
                            WSTJ[:, (gi % 2) * P:(gi % 2 + 1) * P]))
                emit_relay(KC - 2)
                emit_relay(KC - 1)

            @block.tensor
            def _(tn):
                w(tn, "cV", nq_done_cV)
                w(tn, "qG", qg_omt2)
                for jc in range(KC):
                    TJ = tb2 + 97 * jc
                    SJ = sb2 + 40 * jc
                    VJ = vb2 + 41 * jc
                    w(tn, "cP", cast_pos(jc * KC + KC - 1))
                    w(tn, "cS", SJ)
                    w(tn, "cV", VJ)
                    for nt in range(4):
                        gidx = jc * 4 + nt
                        if gidx >= 2:
                            g2 = gidx - 2
                            w(tn, "cS", sb2 + 40 * (g2 // 4) + 2 * (g2 % 4) + 2)
                        pa = PA[nt % 2]
                        for kc in range(KC):
                            mm = tn.matmul(
                                pa[:, :],
                                WBJ[jc % 2][:, kc * P:(kc + 1) * P],
                                XNF[:, kc * TOK + nt * 512:kc * TOK + (nt + 1) * 512],
                                start=(kc == 0), stop=(kc == KC - 1))
                        inc("cT", mm)
                    inc("cT", dummy_tr(tn))
                    w(tn, "cS", SJ + 8)
                    for g in range(32):
                        hh, t = g // NT, g % NT
                        if g >= 4:
                            w(tn, "cV", VJ + g - 3)
                        elif g >= 2:
                            w(tn, "cS", SJ + 2 * g + 2)
                        inc("cT", tn.matmul(
                            PH4[g % 4][:, 0:1],
                            SQH[64 * hh:64 * hh + 64, t * P:(t + 1) * P],
                            ONESB[64 * hh:64 * hh + 64, 0:1],
                            start=True, stop=True))
                    def emit_oct(o):
                        hf, oct_i = o // 2, o % 2
                        hh = hf // 2
                        half = hf % 2
                        for i in range(4):
                            tt = oct_i * 4 + i
                            t = half * 8 + tt
                            if o == 0:
                                w(tn, "cV", VJ + 29 + i)
                            else:
                                w(tn, "cP", relay_pos(jc, o - 1))
                            inc("cT", tn.matmul(
                                PH4[i][:, 0:M],
                                KBUF[jc % 2][64 * hh:64 * hh + 64,
                                             t * P:(t + 1) * P],
                                OMT2[64 * hh:64 * hh + 64, :],
                                start=True, stop=True))
                        inc("cT", dummy_tr(tn))
                        inc("cT", dummy_tr(tn))

                    def emit_kv(hfk):
                        hh = hfk // 2
                        h = 2 * jc + hh
                        half = hfk % 2
                        w(tn, "cP", relay_pos(jc, 2 * hfk + 1))
                        if hfk >= 1:
                            w(tn, "cV", VJ + 33 + 2 * hfk)
                        for tt in range(8):
                            t = half * 8 + tt
                            tn.matmul(PKV[:, :],
                                      V[:, t * D + h * DH:t * D + (h + 1) * DH],
                                      PKALL[:, tt * M:(tt + 1) * M],
                                      start=(tt == 0), stop=(tt == 7))
                        for tt in range(8):
                            mm = tn.matmul(PPS[:, :], ONESB[:, 0:1],
                                           PKALL[:, tt * M:(tt + 1) * M],
                                           start=(tt == 0), stop=(tt == 7))
                        inc("cT", mm)
                        inc("cT", dummy_tr(tn))
                        inc("cT", dummy_tr(tn))

                    emit_oct(0); emit_oct(1); emit_oct(2); emit_kv(0)
                    emit_oct(3); emit_oct(4); emit_kv(1)
                    emit_oct(5); emit_oct(6); emit_kv(2)
                    emit_oct(7); emit_kv(3)
                inc("cT", dummy_mm(tn))

            @block.scalar
            def _(s):
                for jc in range(KC):
                    TJ = tb2 + 97 * jc
                    SJ = sb2 + 40 * jc
                    VJ = vb2 + 41 * jc
                    for nt in range(4):
                        if nt < 3:
                            w(s, "cT", TJ + nt + 2)
                        else:
                            w(s, "cT", TJ + 5)
                        if jc > 0 and nt == 0:
                            w(s, "cT", TJ)
                        pa = PA[nt % 2]
                        inc("cS", s.activation(
                            KBUF[jc % 2][:, nt * 512:(nt + 1) * 512],
                            pa[:, :], AF.Copy))
                        inc("cS", s.activation(
                            SQH[:, nt * 512:(nt + 1) * 512], pa[:, :], AF.Square))
                    w(s, "cP", nk_relay_pos(jc))
                    for hf in range(4):
                        hh = hf // 2
                        h = 2 * jc + hh
                        half = hf % 2
                        for oct_i in range(2):
                            for i in range(4):
                                tt = oct_i * 4 + i
                                t = half * 8 + tt
                                w(s, "cT", phi_drain(jc, hf, oct_i, i))
                                if hf > 0 and oct_i == 0 and i == 0:
                                    w(s, "cT", kv_drain(jc, hf - 1))
                                inc("cS", s.activation(
                                    PKALL[:, tt * M:(tt + 1) * M],
                                    PH4[i][:, 0:M], AF.Exp,
                                    bias=NK[:, h * NT + t:h * NT + t + 1]))

            @block.vector
            def _(v):
                for jc in range(KC):
                    TJ = tb2 + 97 * jc
                    SJ = sb2 + 40 * jc
                    VJ = vb2 + 41 * jc
                    for g in range(32):
                        hh, t = g // NT, g % NT
                        h = 2 * jc + hh
                        w(v, "cT", k_norm_drain(jc, g))
                        inc("cV", v.tensor_scalar(
                            NK[:, h * NT + t:h * NT + t + 1],
                            PH4[g % 4][:, 0:1], -0.5, None, ALU.mult))
                    inc("cV", v.tensor_copy(ZT[:, 1:2], ZT[:, 0:1]))
                    for hf in range(4):
                        hh = hf // 2
                        h = 2 * jc + hh
                        w(v, "cT", kv_drain(jc, hf))
                        if hf % 2 == 0:
                            v.tensor_copy(KVL[0:64, h * M:(h + 1) * M], PKV[:, :])
                            inc("cV", v.tensor_copy(
                                KVL[64:65, h * M:(h + 1) * M], PPS[:, :]))
                        else:
                            v.tensor_tensor(KVL[0:64, h * M:(h + 1) * M],
                                            KVL[0:64, h * M:(h + 1) * M],
                                            PKV[:, :], ALU.add)
                            inc("cV", v.tensor_tensor(
                                KVL[64:65, h * M:(h + 1) * M],
                                KVL[64:65, h * M:(h + 1) * M],
                                PPS[:, :], ALU.add))
                        inc("cV", v.tensor_copy(ZT[:, 1:2], ZT[:, 0:1]))

            assert C["cT"] == tb2 + 97 * KC + 1
            assert C["cS"] == sb2 + 40 * KC
            assert C["cV"] == vb2 + 41 * KC
            kv_done_cV = C["cV"]

            qg0 = C["qG"]

            @block.gpsimd
            def _(g):
                w(g, "cV", kv_done_cV)
                inc("qG", g.dma_start(out=kvloc_d[:, :], in_=KVL[0:65, :]), 16)
                w(g, "qG", qg0 + 16)
                inc("car", g.collective_compute(
                    "AllReduce", ALU.add,
                    replica_groups=[[0, 1, 2, 3], [4, 5, 6, 7]],
                    ins=[kvloc_d], outs=[kvagg_d]))
                w(g, "car", 1)
                inc("qG", g.dma_start(out=KVL[0:65, :], in_=kvagg_d[:, :]), 16)
            kv_in_cQG = C["qG"]

            tb3, vb3 = C["cT"], C["cV"]

            @block.tensor
            def _(tn):
                w(tn, "qG", kv_in_cQG)
                for g in range(H * 2):
                    h, mh = g // 2, g % 2
                    if g >= 2:
                        w(tn, "cV", vb3 + g - 1)
                    inc("cT", tn.transpose(
                        PT[g % 2][:, :],
                        KVL[:, h * M + mh * P:h * M + (mh + 1) * P],
                        IDENT[:, :]))
                inc("cT", dummy_mm(tn))

            @block.vector
            def _(v):
                for g in range(H * 2):
                    h, mh = g // 2, g % 2
                    w(v, "cT", tb3 + min(g + 2, H * 2 + 1))
                    inc("cV", v.tensor_copy(
                        KVF[:, h * 130 + mh * 65:h * 130 + (mh + 1) * 65],
                        PT[g % 2][0:P, 0:65]))
            assert C["cT"] == tb3 + H * 2 + 1
            assert C["cV"] == vb3 + H * 2
            kvf_done_cV = C["cV"]

            @block.gpsimd
            def _(g):
                w(g, "cV", kvf_done_cV)
                kv_ap = AP(kvagg_d.tensor, 0, [[2080, P], [1, 2080]])
                inc("qG", g.dma_start(out=kv_ap, in_=KVF[:, :]), 16)
                w(g, "qG", C["qG"])
                inc("qG", g.dma_start(out=KVF[:, :], in_=kv_ap), 16)
                w(g, "qG", C["qG"])
                inc("cP", g.tensor_copy(ZT[:, 1:2], ZT[:, 0:1]))
            kvfb_cP = C["cP"]

            tb4, sb4, vb4 = C["cT"], C["cS"], C["cV"]
            pgB = C["cP"]

            @block.gpsimd
            def _(g):
                for h in range(H):
                    w(g, "cS", sb4 + 24 * h + 8)
                    for mh in range(2):
                        qp_ap = AP(kvagg_d.tensor, 0, [[2048, P], [1, 2048]])
                        inc("qG", g.dma_start(out=qp_ap, in_=PQB[mh][:, :]), 16)
                        w(g, "qG", C["qG"])
                        inc("qG", g.dma_start(out=PQB[mh][:, :], in_=qp_ap), 16)
                        w(g, "qG", C["qG"])
                    inc("cP", g.tensor_copy(ZT[:, 1:2], ZT[:, 0:1]))
                    w(g, "cS", sb4 + 24 * h + 24)
                    inc("qG", g.dma_start(out=ozb_d[:, :], in_=OZB[:, :]), 16)
                    w(g, "qG", C["qG"])
                    inc("qG", g.dma_start(out=OZB[:, :], in_=ozb_d[:, :]), 16)
                    w(g, "qG", C["qG"])
                    inc("cP", g.tensor_copy(ZT[:, 1:2], ZT[:, 0:1]))
            ozz = AP(OZB[:, :].tensor, 64, [[16 * 65, P], [65, NT]])
            PO4 = [PK0, PK1, PA0, PA1]

            @block.tensor
            def _(tn):
                w(tn, "cV", kvf_done_cV)
                w(tn, "cP", kvfb_cP)
                w(tn, "cS", sqe_done_cS)
                for h in range(H):
                    TH = tb4 + 28 * h
                    SH = sb4 + 24 * h
                    qrow = 64 * (h % 2)
                    qcol = TOK * (h // 2)
                    if h >= 1:
                        w(tn, "cS", SH)
                    for g in range(8):
                        mh, nt = g // 4, g % 4
                        if g >= 2:
                            w(tn, "cS", SH + g - 1)
                        pa = PA[g % 2]
                        inc("cT", tn.matmul(
                            pa[:, :],
                            OMT2[qrow:qrow + 64, mh * P:(mh + 1) * P],
                            Q[qrow:qrow + 64,
                              qcol + nt * 512:qcol + (nt + 1) * 512],
                            start=True, stop=True))
                    inc("cT", dummy_tr(tn))
                    inc("cT", dummy_tr(tn))
                    w(tn, "cS", SH + 8)
                    w(tn, "cP", pgB + 2 * h + 1)
                    for t in range(NT):
                        if t >= 4:
                            w(tn, "cS", SH + 8 + t - 3)
                        po = PO4[t % 4]
                        tn.matmul(po[:, 0:65],
                                  PQB[0][:, t * P:(t + 1) * P],
                                  KVF[:, h * 130:h * 130 + 65],
                                  start=True, stop=False)
                        inc("cT", tn.matmul(
                            po[:, 0:65],
                            PQB[1][:, t * P:(t + 1) * P],
                            KVF[:, h * 130 + 65:h * 130 + 130],
                            start=False, stop=True))
                    inc("cT", dummy_tr(tn))
                    inc("cT", dummy_tr(tn))

            @block.scalar
            def _(s):
                for h in range(H):
                    TH = tb4 + 28 * h
                    SH = sb4 + 24 * h
                    for g in range(8):
                        mh, nt = g // 4, g % 4
                        w(s, "cT", TH + (g + 2 if g < 7 else 10))
                        inc("cS", s.activation(
                            PQB[mh][:, nt * 512:(nt + 1) * 512],
                            PA[g % 2][:, :], AF.Exp))
                    for t in range(NT):
                        nxt = TH + 10 + t + 2
                        if t == NT - 1:
                            nxt = TH + 28
                        w(s, "cT", nxt)
                        if t == 0 and h >= 1:
                            w(s, "cV", vb4 + 5 * (h - 1) + 5)
                        inc("cS", s.activation(
                            OZB[:, t * 65:(t + 1) * 65], PO4[t % 4][:, 0:65],
                            AF.Copy))

            @block.vector
            def _(v):
                for h in range(H):
                    SH = sb4 + 24 * h
                    VH = vb4 + 5 * h
                    w(v, "cP", pgB + 2 * h + 2)
                    sqs = SQE[:, h * NT:(h + 1) * NT]
                    inc("cV", v.tensor_tensor(ZD[:, 0:NT], ozz, sqs, ALU.mult))
                    w(v, "cV", VH + 1)
                    inc("cV", v.tensor_scalar(ZD[:, NT:2 * NT], ZD[:, 0:NT],
                                              1e-6 * M, None, ALU.add))
                    w(v, "cV", VH + 2)
                    inc("cV", v.reciprocal(ZD[:, 2 * NT:3 * NT], ZD[:, NT:2 * NT]))
                    w(v, "cV", VH + 3)
                    inc("cV", v.tensor_tensor(ZD[:, 3 * NT:4 * NT],
                                              ZD[:, 2 * NT:3 * NT], sqs, ALU.mult))
                    w(v, "cV", VH + 4)
                    for t in range(NT):
                        ins = v.scalar_tensor_tensor(
                            X[:, t * D + h * DH:t * D + (h + 1) * DH],
                            OZB[:, t * 65:t * 65 + 64],
                            ZD[:, 3 * NT + t:3 * NT + t + 1],
                            X[:, t * D + h * DH:t * D + (h + 1) * DH],
                            ALU.mult, ALU.add)
                        if t == NT - 1:
                            inc("cV", ins)

            assert C["cT"] == tb4 + 28 * H
            assert C["cS"] == sb4 + 24 * H
            assert C["cV"] == vb4 + 5 * H
            return C["cV"]


        def emit_ffn(first):
            lnF = emit_ln_to_fm()
            sPrev, vPrev = C["cS"], C["cV"]
            qF, pF = C["qS"], C["cP"]
            WGB = Q[:, 0:KC * D]
            BGB = KVL[:, 0:D]
            LGF = SQF[:, 1024:2048]
            EXB = V[:, 4096:5120]
            WEL = V[:, 5120:6144]
            HEB = [V[:, 0:1024], V[:, 1024:2048]]
            NLM = NQ[:, 0:16]
            SCT = NQ[:, 16:32]
            PCOL = NQ[:, 32:48]
            EIDT = NQ[:, 48:64]
            MX8 = NK[:, 0:8]
            TMPC = NK[:, 8:9]
            MSK = SQE[:, 0:128]
            PH = [PA0, PA1, PK0, PK1]

            def wgcast(c):
                return pF + 2 + c

            rbase = pF + 1 + KC
            pE = rbase + 33

            def wecast(i):
                return pE + i + (i // 8) + 1

            def werelay(u):
                return pE + 9 * u + 9

            NU = 2 * E * KC

            def usrc(u):
                return (u // 8) % 8, u % 8, u // 64

            @block.sync
            def _(sync):
                w(sync, "cV", vPrev)
                inc("qS", sync.dma_start(out=SQF[0:1, 0:1024], in_=bg_d[:]), 16)
                beap = AP(be_d.tensor, 0, [[1, P], [D, E], [P, KC]])
                w(sync, "cV", vPrev)
                inc("qS", sync.dma_start(out=BEBT[:, :], in_=beap,
                                         allow_slow_non_contiguous=True), 16)
                for c in range(KC):
                    if c == 0:
                        w(sync, "cP", pF + 1)
                    elif c >= 2:
                        w(sync, "cP", wgcast(c - 2))
                    else:
                        w(sync, "cP", pF + 1)
                    src = AP(wg_d.tensor, (c * P) * D, [[D, P], [1, D]])
                    inc("qS", sync.dma_start(
                        out=SQF[:, (c % 2) * 1024:(c % 2) * 1024 + 1024],
                        in_=src), 16)
                for i in range(NU * KC):
                    u, c = i // 8, i % 8
                    e, fb, th = usrc(u)
                    if i >= 2:
                        w(sync, "cP", wecast(i - 2))
                    else:
                        w(sync, "cP", pF)
                    src = AP(we_d.tensor, (e * D + c * P) * D + fb * P,
                             [[D, P], [1, P]])
                    inc("qS", sync.dma_start(
                        out=WSTJ[:, (i % 2) * P:(i % 2 + 1) * P], in_=src), 16)

            tR, sR, vR = C["cT"], C["cS"], C["cV"]
            vE = vR + 4 * NT + 1

            @block.gpsimd
            def _(g):
                w(g, "qS", qF + 16)
                inc("cP", g.tensor_copy(BGB[0:1, :], SQF[0:1, 0:1024]))
                for c in range(KC):
                    w(g, "qS", qF + 48 + 16 * c)
                    inc("cP", g.tensor_copy(
                        WGB[:, c * D:(c + 1) * D],
                        SQF[:, (c % 2) * 1024:(c % 2) * 1024 + 1024]))
                for t in range(NT):
                    w(g, "cS", sR + 3 * t + 2)
                    inc("cP", g.tensor_copy(ZT[:, 1:2], ZT[:, 0:1]))
                    w(g, "cV", vR + 4 * t + 1)
                    inc("qG", g.dma_start(out=cb_d[:, 0:1],
                                          in_=NLM[:, t:t + 1],
                                          allow_slow_non_contiguous=True), 16)
                    w(g, "qG", C["qG"])
                    inc("qG", g.dma_start(out=NLM[:, t:t + 1],
                                          in_=cb_d[:, 0:1],
                                          allow_slow_non_contiguous=True), 16)
                    w(g, "qG", C["qG"])
                    inc("cP", g.tensor_copy(ZT[:, 1:2], ZT[:, 0:1]))
                w(g, "cS", sR + 3 * NT)
                inc("qG", g.dma_start(out=cb_d[:, 0:16], in_=SCT[:, :]), 16)
                w(g, "qG", C["qG"])
                inc("qG", g.dma_start(out=SCT[:, :], in_=cb_d[:, 0:16]), 16)
                w(g, "qG", C["qG"])
                inc("cP", g.tensor_copy(ZT[:, 1:2], ZT[:, 0:1]))
                for u in range(NU):
                    for c in range(KC):
                        i = u * KC + c
                        w(g, "qS", qF + 160 + 16 * (i + 1))
                        if u >= 1:
                            w(g, "cT", tR + 64 + 14 * (u - 1) + 2)
                        inc("cP", g.tensor_copy(
                            WEL[:, (i % 8) * P:(i % 8 + 1) * P],
                            WSTJ[:, (i % 2) * P:(i % 2 + 1) * P]))
                    w(g, "cS", sR + 3 * NT + 2 * u + 2)
                    inc("cP", g.tensor_copy(ZT[:, 1:2], ZT[:, 0:1]))

            tE = tR + 4 * NT

            @block.tensor
            def _(tn):
                w(tn, "cS", sPrev)
                w(tn, "cV", vPrev)
                w(tn, "tB", lnF)
                for t in range(NT):
                    for h2 in range(2):
                        if t == 0 and h2 == 0:
                            w(tn, "cP", wgcast(KC - 1))
                        if t >= 1:
                            w(tn, "cS", sR + 3 * (t - 1) + h2 + 1)
                        pa = PA[h2]
                        for c in range(KC):
                            tn.matmul(
                                pa[:, :],
                                XNF[:, c * TOK + t * P:c * TOK + (t + 1) * P],
                                WGB[:, c * D + h2 * 512:c * D + (h2 + 1) * 512],
                                start=(c == 0), stop=False)
                        inc("cT", tn.matmul(
                            pa[:, :], ONESB[0:1, 0:P],
                            BGB[0:1, h2 * 512:(h2 + 1) * 512],
                            start=False, stop=True))
                    inc("cT", dummy_ps(tn))
                    inc("cT", dummy_ps(tn))
                for u in range(NU):
                    e, fb, th = usrc(u)
                    TU = tE + 14 * u
                    for c in range(KC):
                        i = u * KC + c
                        w(tn, "cP", wecast(i))
                        if c == 0:
                            for tg in range(2):
                                if u == 0:
                                    w(tn, "cS", sR + 3 * NT)
                                else:
                                    w(tn, "cS", sR + 3 * NT + 2 * (u - 1) + tg + 1)
                        for tg in range(2):
                            mm = tn.matmul(
                                PA[tg][:, :],
                                WEL[:, (i % 8) * P:(i % 8 + 1) * P],
                                XNF[:, c * TOK + th * 1024 + tg * 512:
                                    c * TOK + th * 1024 + (tg + 1) * 512],
                                start=(c == 0), stop=(c == KC - 1))
                            if c == KC - 1:
                                inc("cT", mm)
                    inc("cT", dummy_ps(tn))
                    inc("cT", dummy_ps(tn))
                    w(tn, "cP", werelay(u))
                    for gg in range(KC):
                        if gg >= 2:
                            w(tn, "cV", vE + KC * u + gg - 1)
                        inc("cT", tn.transpose(
                            PT[gg % 2][:, :],
                            HEB[u % 2][:, gg * P:(gg + 1) * P], IDENT[:, :]))
                    inc("cT", dummy_ps(tn))
                    inc("cT", dummy_ps(tn))

            @block.scalar
            def _(s):
                for t in range(NT):
                    w(s, "cT", tR + 4 * t + 2)
                    inc("cS", s.activation(LGF[:, 0:512], PA0[:, :], AF.Copy))
                    w(s, "cT", tR + 4 * t + 4)
                    inc("cS", s.activation(LGF[:, 512:1024], PA1[:, :], AF.Copy))
                    w(s, "cP", rbase + 2 * t + 2)
                    inc("cS", s.activation(
                        EXB[:, :], LGF[:, :], AF.Exp,
                        bias=NLM[:, t:t + 1],
                        accum_out=SCT[:, t:t + 1]))
                for u in range(NU):
                    e, fb, th = usrc(u)
                    TU = tE + 14 * u
                    if u == 0:
                        w(s, "qS", qF + 32)
                    for tg in range(2):
                        if tg < 1:
                            w(s, "cT", TU + 2)
                        else:
                            w(s, "cT", TU + 4)
                        if u >= 2 and tg == 0:
                            w(s, "cT", tE + 14 * (u - 2) + 14)
                        inc("cS", s.activation(
                            HEB[u % 2][:, tg * 512:(tg + 1) * 512],
                            PA[tg][:, :], AF.Relu,
                            bias=BEBT[:, e * KC + fb:e * KC + fb + 1]))

            @block.vector
            def _(v):
                for t in range(NT):
                    w(v, "cP", rbase + 2 * t + 1)
                    inc("cV", v.reduce_max(NLM[:, t:t + 1], LGF[:, :],
                                           axis=AX.X, negate=True))
                    inc("cV", v.max(MX8[:, :], LGF[:, :]))
                    w(v, "cV", vR + 4 * t + 2)
                    inc("cV", v.max_index(IDXB[:, 0:8], MX8[:, :], LGF[:, :]))
                    w(v, "cV", vR + 4 * t + 3)
                    inc("cV", v.tensor_single_scalar(
                        IDXB[:, 1:2], IDXB[:, 0:1], 7, ALU.bitwise_and))
                    w(v, "cV", vR + 4 * t + 4)
                    v.tensor_copy(EIDT[:, t:t + 1], IDXB[:, 1:2])
                w(v, "cP", pE)
                for t in range(NT):
                    v.reciprocal(PCOL[:, t:t + 1], SCT[:, t:t + 1])
                for t in range(NT):
                    for e in range(E):
                        v.scalar_tensor_tensor(
                            MSK[:, t * 8 + e:t * 8 + e + 1], EIDT[:, t:t + 1],
                            float(e), PCOL[:, t:t + 1],
                            ALU.is_equal, ALU.mult)
                inc("cV", v.tensor_copy(ZT[:, 1:2], ZT[:, 0:1]))
                for u in range(NU):
                    e, fb, th = usrc(u)
                    TU = tE + 14 * u
                    for gg in range(KC):
                        t = th * KC + gg
                        if gg < KC - 1:
                            w(v, "cT", TU + 4 + gg + 2)
                        else:
                            w(v, "cT", TU + 14)
                        inc("cV", v.scalar_tensor_tensor(
                            X[:, t * D + fb * P:t * D + (fb + 1) * P],
                            PT[gg % 2][:, :],
                            MSK[:, t * 8 + e:t * 8 + e + 1],
                            X[:, t * D + fb * P:t * D + (fb + 1) * P],
                            ALU.mult, ALU.add))

            assert C["cT"] == tE + 14 * NU
            assert C["cS"] == sR + 3 * NT + 2 * NU
            assert C["cV"] == vE + KC * NU
            assert C["cP"] == pE + 9 * NU
            return C["cV"]


        def emit_ssm():
            TPI = 6.283185307179586
            REVP = OMT2[:, 0:P]
            prevS, prevV = C["cS"], C["cV"]

            @block.gpsimd
            def _(g):
                w(g, "cS", prevS)
                w(g, "cV", prevV)
                inc("cP", g.affine_select(
                    REVP, ONESB[:, :], pattern=[[1, P]],
                    compare_op=ALU.is_equal, fill=0.0, base=-127,
                    channel_multiplier=1))
            lnS = emit_ln_to_fm(perm=REVP)
            PRB = V[:, 0:4096]
            SSTK = V[:, 4096:8192]
            YB = V[:, 0:2048]
            YG = V[:, 2048:4096]
            UGS = [V[:, 8192:8320], V[:, 8320:8448]]
            UTS = [V[:, 8448:8576], V[:, 8576:8704]]
            PLT = [V[:, 8704:8832], V[:, 8832:8960]]
            KDT = [V[:, 8960:9216], V[:, 9216:9472]]
            WTT = [V[0:64, 9472:9728], V[0:64, 9728:9984]]
            STT = [V[0:64, 10880:10944], V[0:64, 10944:11008]]
            KCH = V[:, 9856:9888]
            KCH0 = V[:, 9888:9889]
            ZBL = V[:, 2048:2176]
            YSB = [V[:, 9984:10112], V[:, 10112:10240]]
            PSB = [SQF[0:64, 0:64], SQF[0:64, 64:128]]
            WTMPA = V[:, 0:1024]
            WTMPB = V[:, 1024:2048]
            CCT = V[:, 10240:12288]
            CCTB = V[:, 12288:14336]
            T1B = V[:, 14336:16384]
            USS = Q[:, :]
            LDTC = NQ[:, 64:65]
            ARE = NQ[:, 96:128]
            AIM = NQ[:, 128:160]
            CRE = NQ[:, 160:192]
            CIM = NQ[:, 192:224]
            DCOL = NQ[:, 66:67]
            EA = NQ[:, 224:256]
            DAR = NK[:, 0:32]
            DAI = NK[:, 32:64]
            YRC = NK[:, 64:96]
            KRC = NK[:, 96:128]
            XRD = NK[:, 128:160]
            COSA = NK[:, 160:192]
            K1C = NK[:, 192:224]
            PI2T = NK[:, 224:256]
            ERE2 = SQE[:, 0:32]
            SINV = SQE[:, 32:64]
            COSV = SQE[:, 64:96]
            LK = SQE[:, 96:160]
            CC2F = SQE[:, 160:224]
            LRE = OZB[:, 1024:1025]
            LREV = SQE[:, 224:256]
            LIMV = NQ[:, 0:32]
            LTB = KVL[:, 1024:1088]
            CCA = KVL[:, 1088:1152]
            CCB = KVL[:, 1152:1216]
            PW = SQF[:, 0:2048]
            CM1 = OZB[:, 0:32]
            CM2 = OZB[:, 32:64]
            H1 = OZB[:, 0:1024]
            H2 = OZB[:, 0:512]
            H3 = NK[:, 0:256]
            H4 = OZB[:, 0:128]
            H5 = OZB[:, 128:192]
            H6 = OZB[:, 192:224]
            KZC = OZB[:, 224:225]

            sP, vP, tP, cPb, qSb, qGb = (C["cS"], C["cV"], C["cT"],
                                         C["cP"], C["qS"], C["qG"])

            @block.sync
            def _(sync):
                for dst, src in [(LDTC, ldt_d[:, :]), (ARE, are_d[:, :]),
                                 (AIM, aim_d[:, :]), (CRE, cre_d[:, :]),
                                 (CIM, cim_d[:, :]), (DCOL, ddb_d[:, :]),
                                 (UIDXS[:, :], uidx_d[:, :]),
                                 (USIDX[:, :], ussidx_d[:, :])]:
                    w(sync, "cV", vP)
                    inc("qS", sync.dma_start(out=dst, in_=src), 16)

            @block.gpsimd
            def _(g):
                w(g, "tB", lnS)
                inc("qG", g.dma_start(out=xg_loc[:, :], in_=XNF[:, :]), 16)
                w(g, "qG", C["qG"])
                inc("car", g.collective_compute(
                    "AllGather", ALU.bypass,
                    replica_groups=[[0, 1, 2, 3, 4, 5, 6, 7]],
                    ins=[xg_loc],
                    outs=[xg_all.rearrange("(r p) w -> r p w", p=P)]))
                w(g, "car", C["car"])
                for sl in range(8):
                    w(g, "qS", qSb + 16 * 8)
                    inc("qG", g.indirect_dma_start(
                        out=USS[:, sl * TOK:(sl + 1) * TOK],
                        out_offset=None,
                        in_=xg2k,
                        in_offset=IndirectOffsetOnAxis(
                            ap=USIDX[:, sl:sl + 1], axis=0)), 16)
                g.memset(PI2T[:, :], TPI / 4.0)
                g.memset(ZBL[:, :], 0.0)
                inc("cP", g.memset(SSTK[:, :], 0.0))
                inc("cP", g.tensor_copy(ZT[:, 1:2], ZT[:, 0:1]))
            gUSS = C["qG"]
            gC = gUSS
            pMS = C["cP"]

            cb0 = C["cS"]

            DTB = OZB[:, 256:288]

            @block.scalar
            def _(s):
                w(s, "qS", qSb + 16 * 6)
                inc("cS", s.activation(NQ[:, 65:66], LDTC, AF.Exp))
                inc("cS", s.activation(EA, ARE, AF.Exp))
                s.activation(ZT[:, 2:3], ZT[:, 0:1], AF.Copy)
                s.activation(ZT[:, 2:3], ZT[:, 0:1], AF.Copy)
                inc("cS", s.activation(DTB, ONESB[:, 0:32], AF.Copy,
                                       scale=NQ[:, 65:66]))

            rel0 = C["cP"]

            @block.gpsimd
            def _(g):
                w(g, "cS", cb0 + 3)
                inc("qG", g.dma_start(out=p_dram[:, 100:132], in_=EA), 16)
                inc("qG", g.dma_start(out=p_dram[:, 132:164], in_=DTB), 16)
                w(g, "qG", C["qG"])
                inc("qG", g.dma_start(out=EA, in_=p_dram[:, 100:132]), 16)
                inc("qG", g.dma_start(out=DTB, in_=p_dram[:, 132:164]), 16)
                w(g, "qG", C["qG"])
                inc("cP", g.tensor_copy(ZT[:, 1:2], ZT[:, 0:1]))

            vb0 = C["cV"]
            vpos = {}

            @block.vector
            def _(v):
                def B(op):
                    inc("cV", op)
                    w(v, "cV", C["cV"])

                w(v, "cP", rel0 + 1)
                for _ in range(6):
                    v.tensor_copy(CM1, EA)
                B(v.scalar_tensor_tensor(DAR, EA, -1.0, DTB, ALU.mult, ALU.mult))
                B(v.tensor_tensor(DAI, AIM, DTB, ALU.mult))
                B(v.tensor_single_scalar(YRC, DAI, 1.0 / TPI, ALU.mult))
                B(v.tensor_single_scalar(KRC, YRC, 0.5, ALU.is_ge))
                B(v.tensor_single_scalar(K1C, YRC, 1.5, ALU.is_ge))
                B(v.tensor_tensor(KRC, KRC, K1C, ALU.add))
                B(v.tensor_single_scalar(K1C, YRC, -0.5, ALU.is_le))
                B(v.tensor_tensor(KRC, KRC, K1C, ALU.subtract))
                B(v.tensor_single_scalar(K1C, YRC, -1.5, ALU.is_le))
                B(v.tensor_tensor(KRC, KRC, K1C, ALU.subtract))
                B(v.scalar_tensor_tensor(XRD, KRC, -TPI, DAI, ALU.mult, ALU.add))
                B(v.scalar_tensor_tensor(COSA, XRD, -1.0, PI2T, ALU.mult, ALU.add))
                B(v.tensor_single_scalar(K1C, COSA, TPI / 2.0, ALU.is_gt))
                if DEBUG:
                    v.tensor_copy(KVL[:, 2256:2288], KRC)
                    v.tensor_copy(KVL[:, 2288:2320], YRC)
                    v.tensor_copy(KVL[:, 2000:2032], EA)
                    v.tensor_copy(KVL[:, 2032:2064], DAR)
                    v.tensor_copy(KVL[:, 2064:2096], DAI)
                    v.tensor_copy(KVL[:, 2096:2128], XRD)
                B(v.scalar_tensor_tensor(COSA, K1C, -TPI, COSA,
                                         ALU.mult, ALU.add))
                vpos["cosa"] = C["cV"]

            rel1 = C["cP"]

            @block.gpsimd
            def _(g):
                w(g, "cV", vpos["cosa"])
                inc("cP", g.tensor_copy(ZT[:, 1:2], ZT[:, 0:1]))

            @block.scalar
            def _(s):
                w(s, "cP", rel1 + 1)
                inc("cS", s.activation(SINV, XRD, AF.Sin))
                inc("cS", s.activation(COSV, COSA, AF.Sin))
                inc("cS", s.activation(ERE2, DAR, AF.Exp))

            rel2 = C["cP"]

            @block.gpsimd
            def _(g):
                w(g, "cS", cb0 + 6)
                inc("qG", g.dma_start(out=p_dram[:, 0:96], in_=SQE[:, 0:96]), 16)
                w(g, "qG", C["qG"])
                inc("qG", g.dma_start(out=SQE[:, 0:96], in_=p_dram[:, 0:96]), 16)
                w(g, "qG", C["qG"])
                inc("cP", g.tensor_copy(ZT[:, 1:2], ZT[:, 0:1]))

            vb1 = C["cV"]
            gW = C["qG"]
            gpos = {}

            def gk_pos(kind, k):
                off = {"k": 4, "pp": 5, "wre": 6, "wim": 7}[kind]
                return gW + 16 * (off + 4 * k)

            @block.vector
            def _(v):
                def B(op):
                    inc("cV", op)
                    w(v, "cV", C["cV"])

                w(v, "cP", rel2 + 1)
                for _ in range(6):
                    v.tensor_copy(CM1, CRE)
                B(v.tensor_tensor(LREV, ERE2, COSV, ALU.mult))
                B(v.tensor_tensor(LIMV, ERE2, SINV, ALU.mult))
                if DEBUG:
                    v.tensor_copy(KVL[:, 2128:2160], LREV)
                    v.tensor_copy(KVL[:, 2160:2192], LIMV)
                    v.tensor_copy(KVL[:, 2192:2224], ERE2)
                    v.tensor_copy(KVL[:, 2224:2256], SINV)
                QR = OZB[:, 288:320]
                QI = OZB[:, 320:352]
                IA2 = OZB[:, 352:384]
                NR = OZB[:, 384:416]
                NI = OZB[:, 416:448]
                AR2 = OZB[:, 448:480]
                B(v.tensor_single_scalar(AR2, EA, -1.0, ALU.mult))
                B(v.tensor_single_scalar(NR, LREV, 1.0, ALU.subtract))
                B(v.tensor_copy(NI, LIMV))
                B(v.tensor_tensor(CM1, AR2, AR2, ALU.mult))
                B(v.tensor_tensor(CM2, AIM, AIM, ALU.mult))
                B(v.tensor_tensor(IA2, CM1, CM2, ALU.add))
                B(v.reciprocal(IA2, IA2))
                B(v.tensor_tensor(CM1, NR, AR2, ALU.mult))
                B(v.tensor_tensor(CM2, NI, AIM, ALU.mult))
                B(v.tensor_tensor(QR, CM1, CM2, ALU.add))
                B(v.tensor_tensor(QR, QR, IA2, ALU.mult))
                B(v.tensor_tensor(CM1, NI, AR2, ALU.mult))
                B(v.tensor_tensor(CM2, NR, AIM, ALU.mult))
                B(v.tensor_tensor(QI, CM1, CM2, ALU.subtract))
                B(v.tensor_tensor(QI, QI, IA2, ALU.mult))
                B(v.tensor_tensor(CM1, CRE, QR, ALU.mult))
                B(v.tensor_tensor(CM2, CIM, QI, ALU.mult))
                B(v.tensor_tensor(CC2F[:, 0:32], CM1, CM2, ALU.subtract))
                B(v.tensor_single_scalar(CC2F[:, 0:32], CC2F[:, 0:32], 2.0,
                                         ALU.mult))
                B(v.tensor_tensor(CM1, CRE, QI, ALU.mult))
                B(v.tensor_tensor(CM2, CIM, QR, ALU.mult))
                B(v.tensor_tensor(CC2F[:, 32:64], CM1, CM2, ALU.add))
                B(v.tensor_single_scalar(CC2F[:, 32:64], CC2F[:, 32:64], -2.0,
                                         ALU.mult))
                B(v.tensor_copy(CCA, CC2F))
                B(v.tensor_copy(CCB[:, 0:32], CC2F[:, 32:64]))
                B(v.tensor_single_scalar(CCB[:, 32:64], CC2F[:, 0:32], -1.0,
                                         ALU.mult))
                B(v.tensor_copy(CCT[:, 0:64], CCA))
                B(v.tensor_copy(CCTB[:, 0:64], CCB))
                wdt = 64
                while wdt < 2048:
                    B(v.tensor_copy(CCT[:, wdt:2 * wdt], CCT[:, 0:wdt]))
                    B(v.tensor_copy(CCTB[:, wdt:2 * wdt], CCTB[:, 0:wdt]))
                    wdt *= 2
                B(v.tensor_tensor(KZC[:, 0:1], CC2F[:, 0:1], CC2F[:, 1:2],
                                  ALU.add))
                for n in range(2, 32):
                    B(v.tensor_tensor(KZC[:, 0:1], KZC[:, 0:1],
                                      CC2F[:, n:n + 1], ALU.add))
                B(v.tensor_copy(KCH0[:, 0:1], KZC[:, 0:1]))
                vpos["k0"] = C["cV"]

                def cmul_into(dre, dim_, sre, sim):
                    B(v.tensor_tensor(CM1, sre, LREV, ALU.mult))
                    B(v.tensor_tensor(CM2, sim, LIMV, ALU.mult))
                    B(v.tensor_tensor(dre, CM1, CM2, ALU.subtract))
                    B(v.tensor_tensor(CM1, sre, LIMV, ALU.mult))
                    B(v.tensor_tensor(CM2, sim, LREV, ALU.mult))
                    B(v.tensor_tensor(dim_, CM1, CM2, ALU.add))

                for k in range(8):
                    if k >= 1:
                        w(v, "qG", gk_pos("pp", k - 1))
                    if k == 0:
                        B(v.tensor_copy(PW[:, 0:32], LREV))
                        B(v.tensor_copy(PW[:, 32:64], LIMV))
                    else:
                        cmul_into(PW[:, 0:32], PW[:, 32:64],
                                  LK[:, 0:32], LK[:, 32:64])
                    for l in range(1, 32):
                        cmul_into(PW[:, l * 64:l * 64 + 32],
                                  PW[:, l * 64 + 32:l * 64 + 64],
                                  PW[:, (l - 1) * 64:(l - 1) * 64 + 32],
                                  PW[:, (l - 1) * 64 + 32:(l - 1) * 64 + 64])
                    B(v.tensor_copy(LK, PW[:, 31 * 64:32 * 64]))
                    nl = 32 if k < 7 else 31
                    if k >= 1:
                        w(v, "qG", gk_pos("wim", k - 1))
                    WV, WO = 16384, 1040
                    B(v.tensor_tensor(T1B, PW, CCT, ALU.mult))
                    sA = AP(T1B.tensor, T1B.offset, [[WV, P], [64, 32], [1, 32]])
                    sBp = AP(T1B.tensor, T1B.offset + 32,
                             [[WV, P], [64, 32], [1, 32]])
                    B(v.tensor_tensor(H1, sA, sBp, ALU.add))
                    if k >= 1:
                        w(v, "qG", gk_pos("wre", k - 1))
                    h1t = AP(H1.tensor, H1.offset,
                             [[WO, P], [1, 32], [32, nl]])
                    B(v.tensor_copy(WTMPA[:, 0:32 * nl], h1t))
                    h1a = AP(H1.tensor, H1.offset, [[WO, P], [32, 32], [1, 16]])
                    h1b = AP(H1.tensor, H1.offset + 16,
                             [[WO, P], [32, 32], [1, 16]])
                    B(v.tensor_tensor(H2, h1a, h1b, ALU.add))
                    h2a = AP(H2.tensor, H2.offset, [[WO, P], [16, 32], [1, 8]])
                    h2b = AP(H2.tensor, H2.offset + 8,
                             [[WO, P], [16, 32], [1, 8]])
                    B(v.tensor_tensor(H3, h2a, h2b, ALU.add))
                    h3a = AP(H3.tensor, H3.offset, [[256, P], [8, 32], [1, 4]])
                    h3b = AP(H3.tensor, H3.offset + 4,
                             [[256, P], [8, 32], [1, 4]])
                    B(v.tensor_tensor(H4, h3a, h3b, ALU.add))
                    h4a = AP(H4.tensor, H4.offset, [[WO, P], [4, 32], [1, 2]])
                    h4b = AP(H4.tensor, H4.offset + 2,
                             [[WO, P], [4, 32], [1, 2]])
                    B(v.tensor_tensor(H5, h4a, h4b, ALU.add))
                    h5a = AP(H5.tensor, H5.offset, [[WO, P], [2, 32], [1, 1]])
                    h5b = AP(H5.tensor, H5.offset + 1,
                             [[WO, P], [2, 32], [1, 1]])
                    B(v.tensor_tensor(H6, h5a, h5b, ALU.add))
                    if k >= 1:
                        w(v, "qG", gk_pos("k", k - 1))
                    B(v.tensor_copy(KCH[:, 0:32], H6[:, 0:32]))
                    vpos[("kwre", k)] = C["cV"]
                    B(v.tensor_tensor(T1B, PW, CCTB, ALU.mult))
                    B(v.tensor_tensor(H1, sA, sBp, ALU.add))
                    B(v.tensor_copy(WTMPB[:, 0:32 * nl], h1t))
                    vpos[("wim", k)] = C["cV"]
                B(v.tensor_copy(LTB, PW[:, 31 * 64:32 * 64]))
                vpos["ltb"] = C["cV"]

            @block.gpsimd
            def _(g):
                w(g, "cV", vpos["k0"])
                inc("qG", g.dma_start(
                    out=AP(k_dram.tensor, 127, [[384, P], [1, 1]]),
                    in_=KCH0[:, 0:1], allow_slow_non_contiguous=True), 16)
                inc("qG", g.dma_start(
                    out=AP(k_dram.tensor, 0, [[384, P], [1, 127]]),
                    in_=ZBL[:, 0:127]), 16)
                inc("qG", g.dma_start(
                    out=AP(w_dram.tensor, 0, [[64 * 256, P], [256, 64]]),
                    in_=CCA, allow_slow_non_contiguous=True), 16)
                for k in range(8):
                    nl = 32 if k < 7 else 31
                    w(g, "cV", vpos[("kwre", k)])
                    inc("qG", g.dma_start(
                        out=AP(k_dram.tensor, 127 + 32 * k + 1,
                               [[384, P], [1, nl]]),
                        in_=KCH[:, 0:nl]), 16)
                    gpos[("k", k)] = C["qG"]
                    jj = 32 * ((k + 4) % 8)
                    w(g, "cV", vpos[("kwre", k)])
                    inc("qG", g.dma_start(
                        out=AP(pp_dram.tensor, jj * 64,
                               [[256 * 64, P], [64, 32], [1, 64]]),
                        in_=PW[:, 0:2048]), 16)
                    gpos[("pp", k)] = C["qG"]
                    w(g, "cV", vpos[("kwre", k)])
                    inc("qG", g.dma_start(
                        out=AP(w_dram.tensor, 32 * k + 1,
                               [[64 * 256, P], [256, 32], [1, nl]]),
                        in_=WTMPA[:, 0:32 * nl]), 16)
                    gpos[("wre", k)] = C["qG"]
                    w(g, "cV", vpos[("wim", k)])
                    inc("qG", g.dma_start(
                        out=AP(w_dram.tensor, (32 * 256) + 32 * k + 1,
                               [[64 * 256, P], [256, 32], [1, nl]]),
                        in_=WTMPB[:, 0:32 * nl]), 16)
                    gpos[("wim", k)] = C["qG"]
                    assert gpos[("wim", k)] == gk_pos("wim", k)
            gK = C["qG"]
            consts_cV = C["cV"]

            t1T, s1S, g1G = C["cT"], C["cS"], C["qG"]

            @block.sync
            def _(sync):
                for ch in range(P):
                    for jb in range(2):
                        if ch >= 2:
                            w(sync, "cT", t1T + 6 * (ch - 2) + 4)
                        else:
                            w(sync, "qG", gK)
                        src = AP(pp_dram.tensor, (ch * 256 + jb * 128) * 64,
                                 [[64, P], [1, 64]])
                        inc("qS", sync.dma_start(
                            out=PLT[ch % 2][:, jb * 64:(jb + 1) * 64],
                            in_=src), 16)

            @block.gpsimd
            def _(g):
                w(g, "qG", gK)
                for ch in range(P):
                    if ch >= 2:
                        w(g, "cT", t1T + 6 * (ch - 2) + 1)
                    else:
                        w(g, "qG", gK)
                    inc("qG", g.indirect_dma_start(
                        out=UGS[ch % 2][:, :], out_offset=None,
                        in_=xg128,
                        in_offset=IndirectOffsetOnAxis(
                            ap=UIDXS[:, ch:ch + 1], axis=0)), 16)
                    if ch >= 1:
                        w(g, "cS", s1S + 2 * (ch - 1) + 2)
                        inc("qG", g.dma_start(
                            out=AP(p_dram.tensor, (ch - 1) * 4096,
                                   [[64, 64], [1, 64]]),
                            in_=PSB[(ch - 1) % 2][:, :]), 16)
                w(g, "cS", s1S + 2 * (P - 1) + 2)
                inc("qG", g.dma_start(
                    out=AP(p_dram.tensor, (P - 1) * 4096,
                           [[64, 64], [1, 64]]),
                    in_=PSB[(P - 1) % 2][:, :]), 16)

            @block.tensor
            def _(tn):
                w(tn, "cV", consts_cV)
                for ch in range(P):
                    w(tn, "qG", g1G + 16 * (2 * ch + 1))
                    if ch >= 2:
                        w(tn, "cS", s1S + 2 * (ch - 2) + 1)
                    inc("cT", tn.transpose(PT[ch % 2][:, :], UGS[ch % 2][:, :],
                                           IDENT[:, :]))
                    inc("cT", dummy_ps(tn))
                    inc("cT", dummy_ps(tn))
                    w(tn, "cS", s1S + 2 * ch + 1)
                    w(tn, "qS", qSb + 16 * (8 + 2 * ch + 2))
                    if ch >= 1:
                        w(tn, "cS", s1S + 2 * (ch - 1) + 2)
                    tn.matmul(PKV[:, 0:64], PLT[ch % 2][:, 0:64],
                              UTS[ch % 2][:, 0:64], start=True, stop=False)
                    inc("cT", tn.matmul(PKV[:, 0:64], PLT[ch % 2][:, 64:128],
                                        UTS[ch % 2][:, 64:128],
                                        start=False, stop=True))
                    inc("cT", dummy_ps(tn))
                    inc("cT", dummy_ps(tn))

            @block.scalar
            def _(s):
                for ch in range(P):
                    w(s, "cT", t1T + 6 * ch + 3)
                    inc("cS", s.activation(UTS[ch % 2][:, :],
                                           PT[ch % 2][:, :], AF.Copy))
                    w(s, "cT", t1T + 6 * ch + 6)
                    if ch >= 2:
                        w(s, "qG", g1G + 16 * (2 * (ch - 2) + 2))
                    inc("cS", s.activation(PSB[ch % 2][:, :],
                                           PKV[:, 0:64], AF.Copy))
            p1_done_qG = C["qG"]

            g2G = C["qG"]

            @block.gpsimd
            def _(g):
                w(g, "qG", p1_done_qG)
                inc("qG", g.dma_start(out=PRB[:, :], in_=p_dram[:, :]), 16)
            rec_in_qG = C["qG"]
            vR2 = C["cV"]

            @block.vector
            def _(v):
                w(v, "qG", rec_in_qG)
                w(v, "cP", pMS)
                A1 = SQF[:, 0:32]
                B1 = SQF[:, 32:64]
                D1 = SQF[:, 64:96]
                WV = 16384

                def scol(buf, call, im):
                    return AP(buf.tensor, buf.offset + 32 * 64 * im + call,
                              [[WV, P], [64, 32]])

                def B2(op):
                    inc("cV", op)
                    w(v, "cV", C["cV"])

                for b in range(2):
                    for c in range(1, 32):
                        call = b * 32 + c
                        B2(v.tensor_tensor(A1, scol(SSTK, call - 1, 0),
                                           LTB[:, 0:32], ALU.mult))
                        B2(v.tensor_tensor(B1, scol(SSTK, call - 1, 1),
                                           LTB[:, 32:64], ALU.mult))
                        B2(v.tensor_tensor(D1, A1, B1, ALU.subtract))
                        B2(v.tensor_tensor(scol(SSTK, call, 0), D1,
                                           scol(PRB, call - 1, 0), ALU.add))
                        B2(v.tensor_tensor(A1, scol(SSTK, call - 1, 0),
                                           LTB[:, 32:64], ALU.mult))
                        B2(v.tensor_tensor(B1, scol(SSTK, call - 1, 1),
                                           LTB[:, 0:32], ALU.mult))
                        B2(v.tensor_tensor(D1, A1, B1, ALU.add))
                        B2(v.tensor_tensor(scol(SSTK, call, 1), D1,
                                           scol(PRB, call - 1, 1), ALU.add))
                inc("cV", v.tensor_copy(ZT[:, 1:2], ZT[:, 0:1]))

            @block.gpsimd
            def _(g):
                w(g, "cV", vR2 + 62 * 8 + 1)
                inc("qG", g.dma_start(out=s_dram[:, :], in_=SSTK[:, :]), 16)
            s_done_qG = C["qG"]

            t2T, s2S, g2G2, q2S = C["cT"], C["cS"], C["qG"], C["qS"]

            def YT2(ch, ib):
                off = 10368 + 128 * (2 * (ch % 2) + ib)
                return V[0:64, off:off + 128]

            @block.sync
            def _(sync):
                for ch in range(P):
                    if ch >= 2:
                        w(sync, "cT", t2T + 11 * (ch - 2) + 11)
                    else:
                        w(sync, "qG", s_done_qG)
                    inc("qS", sync.dma_start(
                        out=KDT[ch % 2][:, 0:128],
                        in_=AP(k_dram.tensor, ch * 384, [[1, P], [1, P]])), 16)
                    inc("qS", sync.dma_start(
                        out=KDT[ch % 2][:, 128:256],
                        in_=AP(k_dram.tensor, ch * 384 + 128,
                               [[1, P], [1, P]])), 16)
                    inc("qS", sync.dma_start(
                        out=WTT[ch % 2][:, :],
                        in_=AP(w_dram.tensor, ch * 64 * 256,
                               [[256, 64], [1, 256]])), 16)
                    inc("qS", sync.dma_start(
                        out=STT[ch % 2][:, :],
                        in_=AP(s_dram.tensor, ch * 4096,
                               [[64, 64], [1, 64]])), 16)

            @block.gpsimd
            def _(g):
                w(g, "qG", s_done_qG)
                for ch in range(P):
                    if ch >= 2:
                        w(g, "cT", t2T + 11 * (ch - 2) + 1)
                    else:
                        w(g, "qG", s_done_qG)
                    inc("qG", g.indirect_dma_start(
                        out=UGS[ch % 2][:, :], out_offset=None,
                        in_=xg128,
                        in_offset=IndirectOffsetOnAxis(
                            ap=UIDXS[:, ch:ch + 1], axis=0)), 16)
                    w(g, "cS", s2S + 5 * ch + 5)
                    for ib in range(2):
                        inc("qG", g.dma_start(
                            out=AP(y_dram.tensor, ch * 16384 + ib * 128,
                                   [[256, 64], [1, 128]]),
                            in_=YT2(ch, ib)), 16)

            @block.tensor
            def _(tn):
                w(tn, "qG", s_done_qG)
                for ch in range(P):
                    w(tn, "qG", g2G2 + 16 * (3 * ch + 1))
                    if ch >= 1:
                        w(tn, "cS", s2S + 5 * (ch - 1) + 5)
                    inc("cT", tn.transpose(PT[ch % 2][:, :], UGS[ch % 2][:, :],
                                           IDENT[:, :]))
                    inc("cT", dummy_ps(tn))
                    inc("cT", dummy_ps(tn))
                    w(tn, "cS", s2S + 5 * ch + 1)
                    w(tn, "qS", q2S + 16 * 4 * (ch + 1))
                    tn.matmul(PK0[:, 0:64], KDT[ch % 2][:, 0:128],
                              UTS[ch % 2][:, 0:64], start=True, stop=False)
                    inc("cT", tn.matmul(PK0[:, 0:64], WTT[ch % 2][0:64, 0:128],
                                        STT[ch % 2][:, :],
                                        start=False, stop=True))
                    tn.matmul(PK1[:, 0:64], KDT[ch % 2][:, 128:256],
                              UTS[ch % 2][:, 0:64], start=True, stop=False)
                    tn.matmul(PK1[:, 0:64], KDT[ch % 2][:, 0:128],
                              UTS[ch % 2][:, 64:128], start=False, stop=False)
                    inc("cT", tn.matmul(PK1[:, 0:64],
                                        WTT[ch % 2][0:64, 128:256],
                                        STT[ch % 2][:, :],
                                        start=False, stop=True))
                    inc("cT", dummy_ps(tn))
                    inc("cT", dummy_ps(tn))
                    w(tn, "cS", s2S + 5 * ch + 3)
                    inc("cT", tn.transpose(PT[ch % 2][0:64, :],
                                           YSB[ch % 2][:, 0:64], IDENT[:, :]))
                    inc("cT", tn.transpose(PT[(ch + 1) % 2][0:64, :],
                                           YSB[ch % 2][:, 64:128], IDENT[:, :]))
                    inc("cT", dummy_ps(tn))
                    inc("cT", dummy_ps(tn))

            @block.scalar
            def _(s):
                for ch in range(P):
                    w(s, "cT", t2T + 11 * ch + 3)
                    inc("cS", s.activation(UTS[ch % 2][:, :],
                                           PT[ch % 2][:, :], AF.Copy))
                    if ch >= 2:
                        w(s, "cT", t2T + 11 * (ch - 2) + 11)
                    w(s, "cT", t2T + 11 * ch + 5)
                    inc("cS", s.activation(YSB[ch % 2][:, 0:64],
                                           PK0[:, 0:64], AF.Copy))
                    w(s, "cT", t2T + 11 * ch + 7)
                    inc("cS", s.activation(YSB[ch % 2][:, 64:128],
                                           PK1[:, 0:64], AF.Copy))
                    w(s, "cT", t2T + 11 * ch + 9)
                    inc("cS", s.activation(YT2(ch, 0), PT[ch % 2][0:64, :],
                                           AF.Copy))
                    w(s, "cT", t2T + 11 * ch + 11)
                    inc("cS", s.activation(YT2(ch, 1), PT[(ch + 1) % 2][0:64, :],
                                           AF.Copy))
            y_done_qG = C["qG"]

            g3G, v3V, s3S = C["qG"], C["cV"], C["cS"]

            @block.gpsimd
            def _(g):
                w(g, "qG", y_done_qG)
                for blk in range(8):
                    if blk >= 2:
                        w(g, "cS", s3S + (blk - 2) + 1)
                    inc("qG", g.dma_start(
                        out=YB[:, 0:2048],
                        in_=y_dram[:, blk * 2048:(blk + 1) * 2048]), 16)
                    w(g, "cS", s3S + blk + 1)
                    inc("qG", g.dma_start(
                        out=yfin_dram[:, blk * 2048:(blk + 1) * 2048],
                        in_=YG[:, 0:2048]), 16)
                w(g, "qG", C["qG"])
                inc("car", g.collective_compute(
                    "AllGather", ALU.bypass,
                    replica_groups=[[0, 1, 2, 3, 4, 5, 6, 7]],
                    ins=[yfin_dram],
                    outs=[yg_all.rearrange("(r p) w -> r p w", p=P)]))
            du_car = C["car"]

            @block.vector
            def _(v):
                for blk in range(8):
                    w(v, "qG", g3G + 16 * (2 * blk + 1))
                    rev = AP(USS.tensor, USS.offset + blk * 2048 + 127,
                             [[P * 128, P], [128, 16], [-1, 128]])
                    inc("cV", v.tensor_scalar(SQF[:, 0:2048], rev, DCOL, None,
                                              ALU.mult))
                    w(v, "cV", C["cV"])
                    inc("cV", v.tensor_tensor(SQF[:, 0:2048], SQF[:, 0:2048],
                                              YB[:, 0:2048], ALU.add))

            @block.scalar
            def _(s):
                for blk in range(8):
                    w(s, "cV", v3V + 2 * blk + 2)
                    inc("cS", s.activation(YG[:, 0:2048], SQF[:, 0:2048],
                                           AF.Gelu))

            t4T, v4V, g4G = C["cT"], C["cV"], C["qG"]

            @block.sync
            def _(sync):
                w(sync, "qG", y_done_qG)
                inc("qS", sync.dma_start(out=UIDXS[:, :], in_=yidx_d[:, :]), 16)
            yidx_qS = C["qS"]

            @block.gpsimd
            def _(g):
                w(g, "car", du_car)
                w(g, "qS", yidx_qS)
                for gidx in range(P):
                    if gidx >= 2:
                        w(g, "cT", t4T + 3 * (gidx - 2) + 1)
                    else:
                        w(g, "qS", yidx_qS)
                    inc("qG", g.indirect_dma_start(
                        out=UGS[gidx % 2][:, :], out_offset=None,
                        in_=yg128,
                        in_offset=IndirectOffsetOnAxis(
                            ap=UIDXS[:, gidx:gidx + 1], axis=0)), 16)

            @block.tensor
            def _(tn):
                for gidx in range(P):
                    w(tn, "qG", g4G + 16 * (gidx + 1))
                    if gidx >= 2:
                        w(tn, "cV", v4V + gidx - 1)
                    inc("cT", tn.transpose(PT[gidx % 2][:, :],
                                           UGS[gidx % 2][:, :], IDENT[:, :]))
                    inc("cT", dummy_ps(tn))
                    inc("cT", dummy_ps(tn))

            @block.vector
            def _(v):
                for gidx in range(P):
                    cbk, t = gidx // 16, gidx % 16
                    w(v, "cT", t4T + 3 * gidx + 3)
                    inc("cV", v.tensor_tensor(
                        X[:, t * D + cbk * P:t * D + (cbk + 1) * P],
                        X[:, t * D + cbk * P:t * D + (cbk + 1) * P],
                        PT[gidx % 2][:, :], ALU.add))
            return C["cV"]

        ln1_tB = emit_ln_to_fm()
        attn_done_cV = emit_attn(ln1_tB)
        x_done = ("cV", attn_done_cV)
        if stage >= 2:
            ffn1_done_cV = emit_ffn(first=True)
            x_done = ("cV", ffn1_done_cV)
        if stage >= 3:
            ssm_done_cV = emit_ssm()
            x_done = ("cV", ssm_done_cV)
        if stage >= 4:
            ffn2_done_cV = emit_ffn(first=False)
            x_done = ("cV", ffn2_done_cV)

        @block.gpsimd
        def _(g):
            w(g, x_done[0], x_done[1])
            xo = xout.rearrange("(t p) d -> t p d", p=P)
            for t in range(NT):
                inc("qG", g.dma_start(out=xo[t], in_=X[:, t * D:(t + 1) * D]), 16)
            if DEBUG:
                for nm, src in [("d_xnf", XNF), ("d_q", Q), ("d_v", V),
                                ("d_kvl", KVL[:, :]), ("d_nq", NQ), ("d_nk", NK),
                                ("d_sqe", SQE)]:
                    sap = src if isinstance(src, AP) else src[:, :]
                    inc("qG", g.dma_start(out=dbg[nm][:, :], in_=sap), 16)
                if stage >= 3:
                    for nm, src in [("d_k", k_dram), ("d_p", p_dram),
                                    ("d_s", s_dram), ("d_y", y_dram),
                                    ("d_yf", yfin_dram), ("d_w", w_dram)]:
                        inc("qG", g.dma_start(out=dbg[nm][:, :], in_=src[:, :]),
                            16)
                    inc("qG", g.dma_start(out=dbg["d_uss"][:, :],
                                          in_=Q[:, :]), 16)
            w(g, "qG", C["qG"])

    return nc


def kernel(**inputs):
    x = np.ascontiguousarray(np.asarray(inputs["L0_attn_a_x"], dtype=np.float32))
    wq = np.ascontiguousarray(np.asarray(inputs["L0_attn_a_Wq"], dtype=np.float32))
    wk = np.ascontiguousarray(np.asarray(inputs["L0_attn_a_Wk"], dtype=np.float32))
    wv = np.ascontiguousarray(np.asarray(inputs["L0_attn_a_Wv"], dtype=np.float32))
    om = np.ascontiguousarray(np.asarray(inputs["L0_attn_a_omega"], dtype=np.float32))
    wg = np.ascontiguousarray(np.asarray(inputs["L0_ffn_Wg"], dtype=np.float32))
    bg = np.ascontiguousarray(np.asarray(inputs["L0_ffn_bg"], dtype=np.float32))
    we = np.ascontiguousarray(
        np.asarray(inputs["L0_ffn_We"], dtype=np.float32).reshape(E * D, D))
    be = np.ascontiguousarray(np.asarray(inputs["L0_ffn_be"], dtype=np.float32))

    nc = build(STAGE)
    in_maps = []
    for r in range(8):
        b, qq = r // 4, r % 4
        in_maps.append({
            "xin": np.ascontiguousarray(x[b, qq * TOK:(qq + 1) * TOK, :]),
            "wq": wq, "wk": wk, "wv": wv, "om": om,
        })
        if STAGE >= 2:
            in_maps[-1].update({"wg": wg, "bg": bg, "we": we, "be": be})
        if STAGE >= 3:
            ch0 = r * 128
            uidx = np.zeros((128, 128), dtype=np.uint32)
            for comb in range(128):
                jb2, call = comb // 64, comb % 64
                bb2, qq2, cq2 = call // 32, (call % 32) // 8, call % 8
                s2 = bb2 * 4 + qq2
                for ch in range(128):
                    uidx[comb, ch] = (128 * s2 + ch) * 128 + r * 16 + cq2 * 2 + jb2
            ussidx = np.zeros((128, 8), dtype=np.uint32)
            for p2 in range(128):
                for s2 in range(8):
                    ussidx[p2, s2] = (128 * s2 + p2) * 8 + r
            yidx = np.zeros((128, 128), dtype=np.uint32)
            for chl in range(128):
                for g2 in range(128):
                    cb2, t2 = g2 // 16, g2 % 16
                    yidx[chl, g2] = (cb2 * 128 + chl) * 128 + (b * 64 + qq * 16 + t2)
            ldtf = np.asarray(inputs["L0_attn_b_log_dt"], dtype=np.float32)
            aref = np.asarray(inputs["L0_attn_b_A_real"], dtype=np.float32)
            aimf = np.asarray(inputs["L0_attn_b_A_imag"], dtype=np.float32)
            cref = np.asarray(inputs["L0_attn_b_C_real"], dtype=np.float32)
            cimf = np.asarray(inputs["L0_attn_b_C_imag"], dtype=np.float32)
            ddf = float(np.asarray(inputs["L0_attn_b_D"], dtype=np.float32))
            in_maps[-1].update({
                "ldt": np.ascontiguousarray(ldtf[ch0:ch0 + 128, None]),
                "are": np.ascontiguousarray(aref[ch0:ch0 + 128]),
                "aim": np.ascontiguousarray(aimf[ch0:ch0 + 128]),
                "cre": np.ascontiguousarray(cref[ch0:ch0 + 128]),
                "cim": np.ascontiguousarray(cimf[ch0:ch0 + 128]),
                "ddb": np.full((128, 1), ddf, dtype=np.float32),
                "uidx": uidx, "ussidx": ussidx, "yidx": yidx,
            })
    res = run_bass_kernel_spmd(nc, in_maps, core_ids=list(range(8)))
    out = np.empty((2, 8192, D), dtype=np.float32)
    for r in range(8):
        b, qq = r // 4, r % 4
        out[b, qq * TOK:(qq + 1) * TOK, :] = np.asarray(res.results[r]["xout"])
    if DEBUG:
        kernel.dbg = [res.results[r] for r in range(8)]
    return out


# version 09
# speedup vs baseline: 1.0106x; 1.0106x over previous
import numpy as np
import concourse.bass as bass
import concourse.mybir as mybir
from concourse.bass_utils import run_bass_kernel_spmd
from concourse.ap import AP
from concourse.bass_core_rust import IndirectOffsetOnAxis
from contextlib import ExitStack

F32 = mybir.dt.float32
BF16 = mybir.dt.bfloat16
U32 = mybir.dt.uint32
AF = mybir.ActivationFunctionType
ALU = mybir.AluOpType
AX = mybir.AxisListType

P = 128
NT = 16
D = 1024
KC = 8
H = 16
DH = 64
M = 256
E = 8
TOK = 2048
STAGE = 4
DEBUG = 0
TRACE = None


def build(stage):
    nc = bass.Bass("TRN2", target_bir_lowering=False)

    xin = nc.dram_tensor("xin", [TOK, D], F32, kind="ExternalInput").ap()
    wq_d = nc.dram_tensor("wq", [D, D], F32, kind="ExternalInput").ap()
    wk_d = nc.dram_tensor("wk", [D, D], F32, kind="ExternalInput").ap()
    wv_d = nc.dram_tensor("wv", [D, D], F32, kind="ExternalInput").ap()
    om_d = nc.dram_tensor("om", [M, DH], F32, kind="ExternalInput").ap()
    xout = nc.dram_tensor("xout", [TOK, D], F32, kind="ExternalOutput").ap()
    dbg = {}
    if DEBUG:
        for nm, shp, dt in [("d_xnf", [P, KC * TOK], BF16), ("d_q", [P, KC * TOK], BF16),
                            ("d_v", [P, NT * D], BF16), ("d_kvl", [P, H * M], BF16),
                            ("d_nq", [P, H * NT], F32), ("d_nk", [P, H * NT], F32),
                            ("d_sqe", [P, H * NT], F32)]:
            dbg[nm] = nc.dram_tensor(nm, shp, dt, kind="ExternalOutput").ap()
        if stage >= 3:
            for nm, shp, dt in [("d_k", [P, 384], BF16), ("d_p", [P, 4096], F32),
                                ("d_s", [P, 4096], BF16), ("d_y", [P, NT * D], BF16),
                                ("d_yf", [P, NT * D], BF16),
                                ("d_uss", [P, NT * D], BF16),
                                ("d_w", [P * 64, 256], BF16)]:
                dbg[nm] = nc.dram_tensor(nm, shp, dt, kind="ExternalOutput").ap()
    kvloc_d = nc.dram_tensor("kvloc_d", [65, H * M], BF16, kind="Internal").ap()
    lnb_d = nc.dram_tensor("lnb_d", [P, 2 * D], BF16, kind="Internal").ap()
    cb_d = nc.dram_tensor("cb_d", [P, 256], F32, kind="Internal").ap()
    ozb_d = nc.dram_tensor("ozb_d", [P, 1040], F32, kind="Internal").ap()
    kvagg_d = nc.dram_tensor("kvagg_d", [65, H * M], BF16, kind="Internal").ap()
    if stage >= 2:
        wg_d = nc.dram_tensor("wg", [D, D], F32, kind="ExternalInput").ap()
        bg_d = nc.dram_tensor("bg", [D], F32, kind="ExternalInput").ap()
        we_d = nc.dram_tensor("we", [E * D, D], F32, kind="ExternalInput").ap()
        be_d = nc.dram_tensor("be", [E, D], F32, kind="ExternalInput").ap()
    if stage >= 3:
        ldt_d = nc.dram_tensor("ldt", [P, 1], F32, kind="ExternalInput").ap()
        are_d = nc.dram_tensor("are", [P, 32], F32, kind="ExternalInput").ap()
        aim_d = nc.dram_tensor("aim", [P, 32], F32, kind="ExternalInput").ap()
        cre_d = nc.dram_tensor("cre", [P, 32], F32, kind="ExternalInput").ap()
        cim_d = nc.dram_tensor("cim", [P, 32], F32, kind="ExternalInput").ap()
        ddb_d = nc.dram_tensor("ddb", [P, 1], F32, kind="ExternalInput").ap()
        uidx_d = nc.dram_tensor("uidx", [P, P], U32, kind="ExternalInput").ap()
        ussidx_d = nc.dram_tensor("ussidx", [P, 8], U32, kind="ExternalInput").ap()
        yidx_d = nc.dram_tensor("yidx", [P, P], U32, kind="ExternalInput").ap()
        xg_loc = nc.dram_tensor("xg_loc", [P, NT * D], BF16, kind="Internal").ap()
        xg_all_t = nc.dram_tensor("xg_all", [8 * P, NT * D], BF16, kind="Internal")
        xg_all = xg_all_t.ap()
        xg2k = xg_all.rearrange("a (b w) -> (a b) w", w=TOK)
        xg128 = xg_all.rearrange("a (b w) -> (a b) w", w=P)
        k_dram = nc.dram_tensor("k_dram", [P, 384], BF16, kind="Internal").ap()
        w_dram = nc.dram_tensor("w_dram", [P * 64, 256], BF16, kind="Internal").ap()
        pp_dram = nc.dram_tensor("pp_dram", [P * 256, 64], BF16, kind="Internal").ap()
        p_dram = nc.dram_tensor("p_dram", [P, 4096], F32, kind="Internal").ap()
        s_dram = nc.dram_tensor("s_dram", [P, 4096], BF16, kind="Internal").ap()
        y_dram = nc.dram_tensor("y_dram", [P, NT * D], BF16, kind="Internal").ap()
        yfin_dram = nc.dram_tensor("yfin", [P, NT * D], BF16, kind="Internal").ap()
        yg_all_t = nc.dram_tensor("yg_all", [8 * P, NT * D], BF16, kind="Internal")
        yg_all = yg_all_t.ap()
        yg128 = yg_all.rearrange("a (b w) -> (a b) w", w=P)

    ctx = ExitStack()
    with ctx:
        sem_names = ["qS", "qG", "cT", "cV", "cS", "cP", "car", "tA", "tB", "tC"]
        SM = {n: ctx.enter_context(nc.semaphore(n)) for n in sem_names}
        C = {n: 0 for n in sem_names}

        def inc(n, instr, k=1):
            if TRACE is not None:
                TRACE.append(("inc", str(instr.ins.engine), n, k,
                              getattr(instr.ins, "name", "?")))
            instr.then_inc(SM[n], k)
            C[n] += k
            return instr

        def w(eng, n, v):
            if TRACE is not None:
                TRACE.append(("wait", str(eng.engine), n, v, "w"))
            if v > 0:
                eng.wait_ge(SM[n], v)

        def sb(name, shape, dt):
            return ctx.enter_context(nc.sbuf_tensor(name, shape, dt))

        X = sb("X", [P, NT * D], F32)
        XNF = sb("XNF", [P, KC * TOK], BF16)
        Q = sb("Q", [P, KC * TOK], BF16)
        V = sb("V", [P, NT * D], BF16)
        AR1 = sb("AR1", [P, 8192], BF16)
        KVL = sb("KVL", [P, H * M], BF16)
        SCR2 = sb("SCR2", [P, TOK], BF16)
        SQF = sb("SQF", [P, TOK], F32)
        OZB = sb("OZB", [P, 16 * 65], F32)
        NQ = sb("NQ", [P, H * NT], F32)
        NK = sb("NK", [P, H * NT], F32)
        SQE = sb("SQE", [P, H * NT], F32)
        WSTJ = sb("WSTJ", [P, 2 * P], F32)
        ONESB = sb("ONESB", [P, P], BF16)
        IDENT = sb("IDENT", [P, P], BF16)
        IDENTF = sb("IDENTF", [P, P], F32)
        OMT2 = sb("OMT2", [P, M], BF16)
        OM32 = sb("OM32", [P, 2 * DH], F32)
        OMB = sb("OMB", [P, 2 * DH], BF16)
        BNB = sb("BNB", [P, 12], F32)
        MVB = sb("MVB", [P, 4], F32)
        MVB2 = sb("MVB2", [P, 2], F32)
        RSTD = sb("RSTD", [P, 4], F32)
        ZT = sb("ZT", [P, 4], F32)
        EPSC = sb("EPSC", [P, 1], F32)
        ZD = sb("ZD", [P, 4 * NT], F32)
        BEBT = sb("BEBT", [P, E * KC], F32)
        IDXB = sb("IDXB", [P, 8], U32)
        UIDXS = sb("UIDXS", [P, P], U32)
        USIDX = sb("USIDX", [P, 8], U32)

        KBUF = [AR1[:, 0:TOK], AR1[:, TOK:2 * TOK]]
        KVF = AR1[:, 0:H * 2 * 65]
        PQB = [AR1[:, 2 * TOK:3 * TOK], AR1[:, 3 * TOK:4 * TOK]]
        WBV = AR1[:, :]
        WBJ = [AR1[:, 4096:4096 + D], AR1[:, 5120:5120 + D]]
        PKALL = AR1[:, 6144:6144 + 8 * M]
        XNT = [SCR2[:, 0:D], SCR2[:, D:2 * D]]
        SQH = SCR2[:, :]

        block = ctx.enter_context(nc.Block())

        PT0 = ctx.enter_context(nc.psum_tensor("PT0", [P, P], BF16))
        PT1 = ctx.enter_context(nc.psum_tensor("PT1", [P, P], BF16))
        PA0 = ctx.enter_context(nc.psum_tensor("PA0", [P, 512], F32))
        PA1 = ctx.enter_context(nc.psum_tensor("PA1", [P, 512], F32))
        PK0 = ctx.enter_context(nc.psum_tensor("PK0", [P, M], F32))
        PK1 = ctx.enter_context(nc.psum_tensor("PK1", [P, M], F32))
        PKV = ctx.enter_context(nc.psum_tensor("PKV", [64, M], F32))
        PPS = ctx.enter_context(nc.psum_tensor("PPS", [1, M], F32))
        PA = [PA0, PA1]
        PT = [PT0, PT1]
        PH4 = [PK0, PK1, PA0, PA1]

        def dummy_mm(tn):
            return tn.matmul(PA0[0:1, 0:1], ONESB[0:1, 0:1], ONESB[0:1, 0:1],
                             start=True, stop=True)

        def dummy_ps(tn):
            return tn.matmul(PPS[0:1, 0:1], ONESB[0:1, 0:1], ONESB[0:1, 0:1],
                             start=True, stop=True)

        def dummy_tr(tn):
            return tn.transpose(PT0[0:1, 0:1], ONESB[0:1, 0:1], IDENT[0:1, 0:1])

        @block.sync
        def _(sync):
            xt = xin.rearrange("(t p) d -> t p d", p=P)
            for t in range(NT):
                w(sync, "cP", 1)
                inc("qS", sync.dma_start(out=X[:, t * D:(t + 1) * D], in_=xt[t]), 16)
            omr = om_d.rearrange("(c p) d -> c p d", p=P)
            for c in range(2):
                w(sync, "cP", 1)
                inc("qS", sync.dma_start(out=OM32[:, c * DH:(c + 1) * DH], in_=omr[c]), 16)
        qs_x = [16 * (NT + 2)] * NT
        qs_om = C["qS"]

        @block.gpsimd
        def _(g):
            g.memset(KVL[64:P, :], 0.0)
            g.memset(EPSC[:, :], 1e-5)
            inc("cP", g.memset(ONESB[:, :], 1.0))
            w(g, "cP", 1)
            inc("cP", g.affine_select(IDENT[:, :], ONESB[:, :], pattern=[[-1, P]],
                                      compare_op=ALU.is_equal, fill=0.0, base=0,
                                      channel_multiplier=1))
            w(g, "cP", 2)
            inc("cP", g.tensor_copy(IDENTF[:, :], IDENT[:, :]))
            w(g, "qS", qs_om)
            inc("cP", g.tensor_copy(OMB[:, :], OM32[:, :]))
        cp_const = C["cP"]

        @block.tensor
        def _(tn):
            w(tn, "cP", cp_const)
            inc("cT", tn.transpose(PT0[0:64, :], OMB[:, 0:DH], IDENT[:, :]))
            inc("cT", tn.transpose(PT1[0:64, :], OMB[:, DH:2 * DH], IDENT[:, :]))
            inc("cT", dummy_mm(tn))
        ct_om = C["cT"]

        @block.vector
        def _(v):
            w(v, "cT", ct_om - 1)
            v.tensor_copy(OMT2[0:64, 0:P], PT0[0:64, :])
            w(v, "cT", ct_om)
            inc("cV", v.tensor_copy(OMT2[0:64, P:2 * P], PT1[0:64, :]))
        cv_omt = C["cV"]

        @block.gpsimd
        def _(g):
            w(g, "cV", cv_omt)
            inc("qG", g.dma_start(out=OMT2[64:P, :], in_=OMT2[0:64, :]), 16)
        qg_omt2 = C["qG"]

        def emit_ln_to_fm(perm=None):
            vb, sbb, tb = C["tA"], C["tB"], C["tC"]
            ng = NT * KC

            def rstd_ms2(t):
                return sbb + (2 * t + 2 if t <= 1 else 10 * t - 6)

            def ev_ms2(t2, c2):
                return sbb + (10 * t2 + 5 + c2 if t2 <= 14 else 153 + c2)

            pL = C["cP"]

            @block.gpsimd
            def _(g):
                for t in range(NT):
                    w(g, "tA", vb + 3 * t + 3)
                    sl = (t % 2) * D
                    inc("qG", g.dma_start(out=lnb_d[:, sl:sl + D],
                                          in_=XNT[t % 2][:, :]), 16)
                    w(g, "qG", C["qG"])
                    inc("qG", g.dma_start(out=XNT[t % 2][:, :],
                                          in_=lnb_d[:, sl:sl + D]), 16)
                    w(g, "qG", C["qG"])
                    inc("cP", g.tensor_copy(ZT[:, 1:2], ZT[:, 0:1]))

            @block.tensor
            def _(tn):
                w(tn, "cV", cv_omt)
                for t in range(NT):
                    w(tn, "cP", pL + t + 1)
                    for c in range(KC):
                        gidx = t * 8 + c
                        if gidx >= 2:
                            t2, c2 = (gidx - 2) // 8, (gidx - 2) % 8
                            w(tn, "tB", ev_ms2(t2, c2))
                        inc("tC", tn.transpose(
                            PT[gidx % 2][:, :],
                            XNT[t % 2][:, c * P:(c + 1) * P],
                            IDENT[:, :] if perm is None else perm))
                w(tn, "tB", ev_ms2(15, 6))
                inc("tC", dummy_tr(tn))

            @block.vector
            def _(v):
                for t in range(NT):
                    w(v, "qS", qs_x[t])
                    if t >= 2:
                        w(v, "tB", rstd_ms2(t - 2))
                    v.bn_stats(BNB[:, 0:6], X[:, t * D:t * D + 512])
                    inc("tA", v.bn_stats(BNB[:, 6:12], X[:, t * D + 512:(t + 1) * D]))
                    w(v, "tA", vb + 3 * t + 1)
                    inc("tA", v.bn_aggr(MVB[:, 2 * (t % 2):2 * (t % 2) + 2], BNB[:, :]))
                    w(v, "tA", vb + 3 * t + 2)
                    w(v, "tB", rstd_ms2(t))
                    if t >= 2:
                        w(v, "tC", tb + 8 * (t - 2) + 8)
                    inc("tA", v.tensor_scalar(
                        XNT[t % 2], X[:, t * D:(t + 1) * D],
                        MVB[:, 2 * (t % 2):2 * (t % 2) + 1],
                        RSTD[:, t % 2:t % 2 + 1],
                        ALU.subtract, ALU.mult))

            def rstd_ms(t):
                return sbb + (2 * t + 2 if t <= 1 else 10 * t - 6)

            def ev_ms(t2, c2):
                return sbb + (10 * t2 + 5 + c2 if t2 <= 14 else 153 + c2)

            @block.scalar
            def _(s):
                w(s, "cP", cp_const)

                def lnexp(t):
                    w(s, "tA", vb + 3 * t + 2)
                    mvv = MVB[:, 2 * (t % 2) + 1:2 * (t % 2) + 2]
                    inc("tB", s.activation(RSTD[:, t % 2:t % 2 + 1], mvv,
                                           AF.Ln, bias=EPSC[:, 0:1]))
                    w(s, "tB", rstd_ms(t) - 1)
                    inc("tB", s.activation(RSTD[:, t % 2:t % 2 + 1],
                                           RSTD[:, t % 2:t % 2 + 1],
                                           AF.Exp, scale=-0.5))

                lnexp(0)
                lnexp(1)
                for t in range(NT):
                    for c in range(KC):
                        gidx = t * 8 + c
                        w(s, "tC", tb + min(gidx + 2, ng + 1))
                        inc("tB", s.activation(
                            XNF[:, c * TOK + t * P:c * TOK + (t + 1) * P],
                            PT[gidx % 2][:, :], AF.Copy))
                    if t + 2 < NT:
                        lnexp(t + 2)

            assert C["tA"] == vb + 3 * NT
            assert C["tB"] == sbb + 10 * NT
            assert C["tC"] == tb + ng + 1
            return C["tB"]

        def emit_col_weights(wdram, ready_sig, mmdone):
            qb, pb = C["qS"], C["cP"]

            @block.sync
            def _(sync):
                for jc in range(KC):
                    for kc in range(KC):
                        gi = jc * KC + kc
                        if gi >= 2:
                            w(sync, "cP", pb + gi - 1)
                        else:
                            w(sync, "cP", pb)
                        src = AP(wdram.tensor, (kc * P) * D + jc * P,
                                 [[D, P], [1, P]])
                        inc("qS", sync.dma_start(
                            out=WSTJ[:, (gi % 2) * P:(gi % 2 + 1) * P], in_=src), 16)

            @block.gpsimd
            def _(g):
                if ready_sig is not None:
                    w(g, ready_sig[0], ready_sig[1])
                for jc in range(KC):
                    if mmdone is not None and jc >= 2:
                        w(g, "cT", mmdone(jc - 2))
                    for kc in range(KC):
                        gi = jc * KC + kc
                        w(g, "qS", qb + 16 * (gi + 1))
                        inc("cP", g.tensor_copy(
                            WBJ[jc % 2][:, kc * P:(kc + 1) * P],
                            WSTJ[:, (gi % 2) * P:(gi % 2 + 1) * P]))
            assert C["cP"] == pb + 64
            return pb

        def emit_attn(xnf_ready_tB):
            qb0, pb0 = C["qS"], C["cP"]

            @block.sync
            def _(sync):
                for gi in range(KC * KC):
                    if gi >= 2:
                        w(sync, "cP", pb0 + gi - 1)
                    else:
                        w(sync, "cP", cp_const)
                    kc, jj = gi // KC, gi % KC
                    src = AP(wv_d.tensor, (kc * P) * D + jj * P, [[D, P], [1, P]])
                    inc("qS", sync.dma_start(
                        out=WSTJ[:, (gi % 2) * P:(gi % 2 + 1) * P], in_=src), 16)

            @block.gpsimd
            def _(g):
                for gi in range(KC * KC):
                    kc, jj = gi // KC, gi % KC
                    w(g, "qS", qb0 + 16 * (gi + 1))
                    inc("cP", g.tensor_copy(
                        WBV[:, kc * D + jj * P:kc * D + (jj + 1) * P],
                        WSTJ[:, (gi % 2) * P:(gi % 2 + 1) * P]))
            wv_ready = C["cP"]

            tb0, sb0 = C["cT"], C["cS"]
            ngv = NT * 2

            @block.tensor
            def _(tn):
                w(tn, "cP", wv_ready)
                w(tn, "tB", xnf_ready_tB)
                for gidx in range(ngv):
                    t, nt = gidx // 2, gidx % 2
                    if gidx >= 2:
                        w(tn, "cS", sb0 + gidx - 1)
                    pa = PA[gidx % 2]
                    for kc in range(KC):
                        mm = tn.matmul(
                            pa[:, :],
                            XNF[:, kc * TOK + t * P:kc * TOK + (t + 1) * P],
                            WBV[:, kc * D + nt * 512:kc * D + (nt + 1) * 512],
                            start=(kc == 0), stop=(kc == KC - 1))
                    inc("cT", mm)
                w(tn, "cS", sb0 + 31)
                inc("cT", dummy_mm(tn))

            @block.scalar
            def _(s):
                for gidx in range(ngv):
                    t, nt = gidx // 2, gidx % 2
                    w(s, "cT", tb0 + min(gidx + 2, ngv + 1))
                    inc("cS", s.activation(
                        V[:, t * D + nt * 512:t * D + (nt + 1) * 512],
                        PA[gidx % 2][:, :], AF.Copy))
            v_done_cS = C["cS"]
            assert C["cT"] == tb0 + ngv + 1

            tb1 = C["cT"]
            wq_pb = emit_col_weights(wq_d, ("cT", tb1),
                                     mmdone=lambda jc: tb1 + 38 * jc + 4)
            sb1, vb1 = C["cS"], C["cV"]

            def q_drain(jc, g):
                if g < 32 - 1:
                    return tb1 + 38 * jc + 5 + g + 2
                return tb1 + 38 * jc + 38

            @block.tensor
            def _(tn):
                w(tn, "cS", v_done_cS)
                for jc in range(KC):
                    TJ = tb1 + 38 * jc
                    SJ = sb1 + 8 * jc
                    VJ = vb1 + 32 * jc
                    w(tn, "cP", wq_pb + 8 * (jc + 1))
                    w(tn, "cS", SJ)
                    w(tn, "cV", VJ)
                    for nt in range(4):
                        gidx = jc * 4 + nt
                        if gidx >= 2:
                            g2 = gidx - 2
                            w(tn, "cS", sb1 + 8 * (g2 // 4) + 2 * (g2 % 4) + 2)
                        pa = PA[nt % 2]
                        for kc in range(KC):
                            mm = tn.matmul(
                                pa[:, :],
                                WBJ[jc % 2][:, kc * P:(kc + 1) * P],
                                XNF[:, kc * TOK + nt * 512:kc * TOK + (nt + 1) * 512],
                                start=(kc == 0), stop=(kc == KC - 1))
                        inc("cT", mm)
                    inc("cT", dummy_tr(tn))
                    w(tn, "cS", SJ + 8)
                    for g in range(32):
                        hh, t = g // NT, g % NT
                        if g >= 4:
                            w(tn, "cV", VJ + g - 3)
                        elif g >= 2:
                            w(tn, "cS", SJ + 2 * g + 2)
                        inc("cT", tn.matmul(
                            PH4[g % 4][:, 0:1],
                            SQH[64 * hh:64 * hh + 64, t * P:(t + 1) * P],
                            ONESB[64 * hh:64 * hh + 64, 0:1],
                            start=True, stop=True))
                    inc("cT", dummy_tr(tn))

            @block.scalar
            def _(s):
                for jc in range(KC):
                    TJ = tb1 + 38 * jc
                    SJ = sb1 + 8 * jc
                    for nt in range(4):
                        gidx = jc * 4 + nt
                        if nt < 3:
                            w(s, "cT", TJ + nt + 2)
                        else:
                            w(s, "cT", TJ + 5)
                        if jc > 0 and nt == 0:
                            w(s, "cT", TJ)
                        pa = PA[nt % 2]
                        inc("cS", s.activation(
                            Q[:, jc * TOK + nt * 512:jc * TOK + (nt + 1) * 512],
                            pa[:, :], AF.Copy))
                        inc("cS", s.activation(
                            SQH[:, nt * 512:(nt + 1) * 512], pa[:, :], AF.Square))

            @block.vector
            def _(v):
                for jc in range(KC):
                    TJ = tb1 + 38 * jc
                    VJ = vb1 + 32 * jc
                    for g in range(32):
                        hh, t = g // NT, g % NT
                        h = 2 * jc + hh
                        w(v, "cT", q_drain(jc, g))
                        inc("cV", v.tensor_scalar(
                            NQ[:, h * NT + t:h * NT + t + 1],
                            PH4[g % 4][:, 0:1], -0.5, None, ALU.mult))

            assert C["cT"] == tb1 + 38 * KC
            assert C["cS"] == sb1 + 8 * KC
            assert C["cV"] == vb1 + 32 * KC
            nq_done_cV = C["cV"]

            sqr = C["cP"]

            @block.gpsimd
            def _(g):
                w(g, "cV", nq_done_cV)
                inc("qG", g.dma_start(out=cb_d[:, :], in_=NQ[:, :]), 16)
                w(g, "qG", C["qG"])
                inc("qG", g.dma_start(out=NQ[:, :], in_=cb_d[:, :]), 16)
                w(g, "qG", C["qG"])
                inc("cP", g.tensor_copy(ZT[:, 1:2], ZT[:, 0:1]))

            @block.scalar
            def _(s):
                w(s, "cP", sqr + 1)
                inc("cS", s.activation(SQE[:, :], NQ[:, :], AF.Exp))
            sqe_mid_cS = C["cS"]

            @block.gpsimd
            def _(g):
                w(g, "cS", sqe_mid_cS)
                inc("qG", g.dma_start(out=cb_d[:, :], in_=SQE[:, :]), 16)
                w(g, "qG", C["qG"])
                inc("qG", g.dma_start(out=SQE[:, :], in_=cb_d[:, :]), 16)
                w(g, "qG", C["qG"])

            @block.scalar
            def _(s):
                w(s, "qG", C["qG"])
                inc("cS", s.activation(ZT[0:1, 2:3], ZT[0:1, 0:1], AF.Copy))
            sqe_done_cS = C["cS"]

            tb2 = C["cT"]
            sb2, vb2 = C["cS"], C["cV"]
            qbK, pbK = C["qS"], C["cP"]
            if TRACE is not None:
                print("BASES tb2=%d sb2=%d vb2=%d qbK=%d pbK=%d" % (tb2, sb2, vb2, qbK, pbK))
            def k_norm_drain(jc, g):
                if g < 32 - 1:
                    return tb2 + 97 * jc + 5 + g + 2
                return tb2 + 97 * jc + 37 + 1

            OBASE = [0, 6, 12, 21, 27, 36, 42, 51]
            KVPOS = [21, 36, 51, 60]

            def phi_drain(jc, hf, oct_i, i):
                base = tb2 + 97 * jc + 37 + OBASE[2 * hf + oct_i]
                if i < 3:
                    return base + i + 2
                return base + 6

            def kv_drain(jc, hf):
                return tb2 + 97 * jc + 37 + KVPOS[hf]

            def cast_pos(gi):
                jc = gi // KC
                relays_before = 9 * max(0, jc - 1) if jc >= 2 else 0
                return pbK + gi + 1 + relays_before

            def nk_relay_pos(jc2):
                casts_before = 8 * min(KC, jc2 + 2)
                return pbK + casts_before + 9 * jc2 + 1

            def relay_pos(jc2, oct_i):
                casts_before = 8 * min(KC, jc2 + 2)
                return pbK + casts_before + 9 * jc2 + 1 + oct_i + 1

            @block.sync
            def _(sync):
                for jc in range(KC):
                    for kc in range(KC):
                        gi = jc * KC + kc
                        if gi >= 2:
                            w(sync, "cP", cast_pos(gi - 2))
                        else:
                            w(sync, "cP", pbK)
                        src = AP(wk_d.tensor, (kc * P) * D + jc * P,
                                 [[D, P], [1, P]])
                        inc("qS", sync.dma_start(
                            out=WSTJ[:, (gi % 2) * P:(gi % 2 + 1) * P], in_=src), 16)

            @block.gpsimd
            def _(g):
                w(g, "cT", tb2)

                def dma_hop():
                    g.tensor_copy(ZT[:, 1:2], ZT[:, 0:1])

                def emit_relay(jc2):
                    w(g, "cV", vb2 + 41 * jc2 + 33)
                    nkc = NK[:, 32 * jc2:32 * jc2 + 32]
                    inc("qG", g.dma_start(out=cb_d[:, 0:32], in_=nkc), 16)
                    w(g, "qG", C["qG"])
                    inc("qG", g.dma_start(out=nkc, in_=cb_d[:, 0:32]), 16)
                    w(g, "qG", C["qG"])
                    inc("cP", g.tensor_copy(ZT[:, 1:2], ZT[:, 0:1]))
                    for oct_i in range(8):
                        w(g, "cS", sb2 + 40 * jc2 + 8 + oct_i * 4 + 4)
                        half = (oct_i % 2) * 1024
                        inc("qG", g.dma_start(
                            out=lnb_d[:, 0:1024],
                            in_=PKALL[:, half:half + 1024]), 16)
                        w(g, "qG", C["qG"])
                        inc("qG", g.dma_start(
                            out=PKALL[:, half:half + 1024],
                            in_=lnb_d[:, 0:1024]), 16)
                        w(g, "qG", C["qG"])
                        inc("cP", g.tensor_copy(ZT[:, 1:2], ZT[:, 0:1]))

                for jc in range(KC):
                    if jc >= 2:
                        w(g, "cT", tb2 + 97 * (jc - 2) + 4)
                        emit_relay(jc - 2)
                    for kc in range(KC):
                        gi = jc * KC + kc
                        w(g, "qS", qbK + 16 * (gi + 1))
                        inc("cP", g.tensor_copy(
                            WBJ[jc % 2][:, kc * P:(kc + 1) * P],
                            WSTJ[:, (gi % 2) * P:(gi % 2 + 1) * P]))
                emit_relay(KC - 2)
                emit_relay(KC - 1)

            @block.tensor
            def _(tn):
                w(tn, "cV", nq_done_cV)
                w(tn, "qG", qg_omt2)
                for jc in range(KC):
                    TJ = tb2 + 97 * jc
                    SJ = sb2 + 40 * jc
                    VJ = vb2 + 41 * jc
                    w(tn, "cP", cast_pos(jc * KC + KC - 1))
                    w(tn, "cS", SJ)
                    w(tn, "cV", VJ)
                    for nt in range(4):
                        gidx = jc * 4 + nt
                        if gidx >= 2:
                            g2 = gidx - 2
                            w(tn, "cS", sb2 + 40 * (g2 // 4) + 2 * (g2 % 4) + 2)
                        pa = PA[nt % 2]
                        for kc in range(KC):
                            mm = tn.matmul(
                                pa[:, :],
                                WBJ[jc % 2][:, kc * P:(kc + 1) * P],
                                XNF[:, kc * TOK + nt * 512:kc * TOK + (nt + 1) * 512],
                                start=(kc == 0), stop=(kc == KC - 1))
                        inc("cT", mm)
                    inc("cT", dummy_tr(tn))
                    w(tn, "cS", SJ + 8)
                    for g in range(32):
                        hh, t = g // NT, g % NT
                        if g >= 4:
                            w(tn, "cV", VJ + g - 3)
                        elif g >= 2:
                            w(tn, "cS", SJ + 2 * g + 2)
                        inc("cT", tn.matmul(
                            PH4[g % 4][:, 0:1],
                            SQH[64 * hh:64 * hh + 64, t * P:(t + 1) * P],
                            ONESB[64 * hh:64 * hh + 64, 0:1],
                            start=True, stop=True))
                    def emit_oct(o):
                        hf, oct_i = o // 2, o % 2
                        hh = hf // 2
                        half = hf % 2
                        for i in range(4):
                            tt = oct_i * 4 + i
                            t = half * 8 + tt
                            if o == 0:
                                w(tn, "cV", VJ + 29 + i)
                            else:
                                w(tn, "cP", relay_pos(jc, o - 1))
                            inc("cT", tn.matmul(
                                PH4[i][:, 0:M],
                                KBUF[jc % 2][64 * hh:64 * hh + 64,
                                             t * P:(t + 1) * P],
                                OMT2[64 * hh:64 * hh + 64, :],
                                start=True, stop=True))
                        inc("cT", dummy_tr(tn))
                        inc("cT", dummy_tr(tn))

                    def emit_kv(hfk):
                        hh = hfk // 2
                        h = 2 * jc + hh
                        half = hfk % 2
                        w(tn, "cP", relay_pos(jc, 2 * hfk + 1))
                        if hfk >= 1:
                            w(tn, "cV", VJ + 33 + 2 * hfk)
                        for tt in range(8):
                            t = half * 8 + tt
                            tn.matmul(PKV[:, :],
                                      V[:, t * D + h * DH:t * D + (h + 1) * DH],
                                      PKALL[:, tt * M:(tt + 1) * M],
                                      start=(tt == 0), stop=(tt == 7))
                        for tt in range(8):
                            mm = tn.matmul(PPS[:, :], ONESB[:, 0:1],
                                           PKALL[:, tt * M:(tt + 1) * M],
                                           start=(tt == 0), stop=(tt == 7))
                        inc("cT", mm)
                        inc("cT", dummy_tr(tn))
                        inc("cT", dummy_tr(tn))

                    emit_oct(0); emit_oct(1); emit_oct(2); emit_kv(0)
                    emit_oct(3); emit_oct(4); emit_kv(1)
                    emit_oct(5); emit_oct(6); emit_kv(2)
                    emit_oct(7); emit_kv(3)
                inc("cT", dummy_mm(tn))

            @block.scalar
            def _(s):
                for jc in range(KC):
                    TJ = tb2 + 97 * jc
                    SJ = sb2 + 40 * jc
                    VJ = vb2 + 41 * jc
                    for nt in range(4):
                        if nt < 3:
                            w(s, "cT", TJ + nt + 2)
                        else:
                            w(s, "cT", TJ + 5)
                        if jc > 0 and nt == 0:
                            w(s, "cT", TJ)
                        pa = PA[nt % 2]
                        inc("cS", s.activation(
                            KBUF[jc % 2][:, nt * 512:(nt + 1) * 512],
                            pa[:, :], AF.Copy))
                        inc("cS", s.activation(
                            SQH[:, nt * 512:(nt + 1) * 512], pa[:, :], AF.Square))
                    w(s, "cP", nk_relay_pos(jc))
                    for hf in range(4):
                        hh = hf // 2
                        h = 2 * jc + hh
                        half = hf % 2
                        for oct_i in range(2):
                            for i in range(4):
                                tt = oct_i * 4 + i
                                t = half * 8 + tt
                                w(s, "cT", phi_drain(jc, hf, oct_i, i))
                                if hf > 0 and oct_i == 0 and i == 0:
                                    w(s, "cT", kv_drain(jc, hf - 1))
                                inc("cS", s.activation(
                                    PKALL[:, tt * M:(tt + 1) * M],
                                    PH4[i][:, 0:M], AF.Exp,
                                    bias=NK[:, h * NT + t:h * NT + t + 1]))

            @block.vector
            def _(v):
                for jc in range(KC):
                    TJ = tb2 + 97 * jc
                    SJ = sb2 + 40 * jc
                    VJ = vb2 + 41 * jc
                    for g in range(32):
                        hh, t = g // NT, g % NT
                        h = 2 * jc + hh
                        w(v, "cT", k_norm_drain(jc, g))
                        inc("cV", v.tensor_scalar(
                            NK[:, h * NT + t:h * NT + t + 1],
                            PH4[g % 4][:, 0:1], -0.5, None, ALU.mult))
                    inc("cV", v.tensor_copy(ZT[:, 1:2], ZT[:, 0:1]))
                    for hf in range(4):
                        hh = hf // 2
                        h = 2 * jc + hh
                        w(v, "cT", kv_drain(jc, hf))
                        if hf % 2 == 0:
                            v.tensor_copy(KVL[0:64, h * M:(h + 1) * M], PKV[:, :])
                            inc("cV", v.tensor_copy(
                                KVL[64:65, h * M:(h + 1) * M], PPS[:, :]))
                        else:
                            v.tensor_tensor(KVL[0:64, h * M:(h + 1) * M],
                                            KVL[0:64, h * M:(h + 1) * M],
                                            PKV[:, :], ALU.add)
                            inc("cV", v.tensor_tensor(
                                KVL[64:65, h * M:(h + 1) * M],
                                KVL[64:65, h * M:(h + 1) * M],
                                PPS[:, :], ALU.add))
                        inc("cV", v.tensor_copy(ZT[:, 1:2], ZT[:, 0:1]))

            assert C["cT"] == tb2 + 97 * KC + 1
            assert C["cS"] == sb2 + 40 * KC
            assert C["cV"] == vb2 + 41 * KC
            kv_done_cV = C["cV"]

            qg0 = C["qG"]

            @block.gpsimd
            def _(g):
                w(g, "cV", kv_done_cV)
                inc("qG", g.dma_start(out=kvloc_d[:, :], in_=KVL[0:65, :]), 16)
                w(g, "qG", qg0 + 16)
                inc("car", g.collective_compute(
                    "AllReduce", ALU.add,
                    replica_groups=[[0, 1, 2, 3], [4, 5, 6, 7]],
                    ins=[kvloc_d], outs=[kvagg_d]))
                w(g, "car", 1)
                inc("qG", g.dma_start(out=KVL[0:65, :], in_=kvagg_d[:, :]), 16)
            kv_in_cQG = C["qG"]

            tb3, vb3 = C["cT"], C["cV"]

            @block.tensor
            def _(tn):
                w(tn, "qG", kv_in_cQG)
                for g in range(H * 2):
                    h, mh = g // 2, g % 2
                    if g >= 2:
                        w(tn, "cV", vb3 + g - 1)
                    inc("cT", tn.transpose(
                        PT[g % 2][:, :],
                        KVL[:, h * M + mh * P:h * M + (mh + 1) * P],
                        IDENT[:, :]))
                inc("cT", dummy_mm(tn))

            @block.vector
            def _(v):
                for g in range(H * 2):
                    h, mh = g // 2, g % 2
                    w(v, "cT", tb3 + min(g + 2, H * 2 + 1))
                    inc("cV", v.tensor_copy(
                        KVF[:, h * 130 + mh * 65:h * 130 + (mh + 1) * 65],
                        PT[g % 2][0:P, 0:65]))
            assert C["cT"] == tb3 + H * 2 + 1
            assert C["cV"] == vb3 + H * 2
            kvf_done_cV = C["cV"]

            @block.gpsimd
            def _(g):
                w(g, "cV", kvf_done_cV)
                kv_ap = AP(kvagg_d.tensor, 0, [[2080, P], [1, 2080]])
                inc("qG", g.dma_start(out=kv_ap, in_=KVF[:, :]), 16)
                w(g, "qG", C["qG"])
                inc("qG", g.dma_start(out=KVF[:, :], in_=kv_ap), 16)
                w(g, "qG", C["qG"])
                inc("cP", g.tensor_copy(ZT[:, 1:2], ZT[:, 0:1]))
            kvfb_cP = C["cP"]

            tb4, sb4, vb4 = C["cT"], C["cS"], C["cV"]
            pgB = C["cP"]

            @block.gpsimd
            def _(g):
                for h in range(H):
                    w(g, "cS", sb4 + 24 * h + 8)
                    for mh in range(2):
                        qp_ap = AP(kvagg_d.tensor, 0, [[2048, P], [1, 2048]])
                        inc("qG", g.dma_start(out=qp_ap, in_=PQB[mh][:, :]), 16)
                        w(g, "qG", C["qG"])
                        inc("qG", g.dma_start(out=PQB[mh][:, :], in_=qp_ap), 16)
                        w(g, "qG", C["qG"])
                    inc("cP", g.tensor_copy(ZT[:, 1:2], ZT[:, 0:1]))
                    w(g, "cS", sb4 + 24 * h + 24)
                    inc("qG", g.dma_start(out=ozb_d[:, :], in_=OZB[:, :]), 16)
                    w(g, "qG", C["qG"])
                    inc("qG", g.dma_start(out=OZB[:, :], in_=ozb_d[:, :]), 16)
                    w(g, "qG", C["qG"])
                    inc("cP", g.tensor_copy(ZT[:, 1:2], ZT[:, 0:1]))
            ozz = AP(OZB[:, :].tensor, 64, [[16 * 65, P], [65, NT]])
            PO4 = [PK0, PK1, PA0, PA1]

            @block.tensor
            def _(tn):
                w(tn, "cV", kvf_done_cV)
                w(tn, "cP", kvfb_cP)
                w(tn, "cS", sqe_done_cS)
                for h in range(H):
                    TH = tb4 + 28 * h
                    SH = sb4 + 24 * h
                    qrow = 64 * (h % 2)
                    qcol = TOK * (h // 2)
                    if h >= 1:
                        w(tn, "cS", SH)
                    for g in range(8):
                        mh, nt = g // 4, g % 4
                        if g >= 2:
                            w(tn, "cS", SH + g - 1)
                        pa = PA[g % 2]
                        inc("cT", tn.matmul(
                            pa[:, :],
                            OMT2[qrow:qrow + 64, mh * P:(mh + 1) * P],
                            Q[qrow:qrow + 64,
                              qcol + nt * 512:qcol + (nt + 1) * 512],
                            start=True, stop=True))
                    inc("cT", dummy_tr(tn))
                    inc("cT", dummy_tr(tn))
                    w(tn, "cS", SH + 8)
                    w(tn, "cP", pgB + 2 * h + 1)
                    for t in range(NT):
                        if t >= 4:
                            w(tn, "cS", SH + 8 + t - 3)
                        po = PO4[t % 4]
                        tn.matmul(po[:, 0:65],
                                  PQB[0][:, t * P:(t + 1) * P],
                                  KVF[:, h * 130:h * 130 + 65],
                                  start=True, stop=False)
                        inc("cT", tn.matmul(
                            po[:, 0:65],
                            PQB[1][:, t * P:(t + 1) * P],
                            KVF[:, h * 130 + 65:h * 130 + 130],
                            start=False, stop=True))
                    inc("cT", dummy_tr(tn))
                    inc("cT", dummy_tr(tn))

            @block.scalar
            def _(s):
                for h in range(H):
                    TH = tb4 + 28 * h
                    SH = sb4 + 24 * h
                    for g in range(8):
                        mh, nt = g // 4, g % 4
                        w(s, "cT", TH + (g + 2 if g < 7 else 10))
                        inc("cS", s.activation(
                            PQB[mh][:, nt * 512:(nt + 1) * 512],
                            PA[g % 2][:, :], AF.Exp))
                    for t in range(NT):
                        nxt = TH + 10 + t + 2
                        if t == NT - 1:
                            nxt = TH + 28
                        w(s, "cT", nxt)
                        if t == 0 and h >= 1:
                            w(s, "cV", vb4 + 5 * (h - 1) + 5)
                        inc("cS", s.activation(
                            OZB[:, t * 65:(t + 1) * 65], PO4[t % 4][:, 0:65],
                            AF.Copy))

            @block.vector
            def _(v):
                for h in range(H):
                    SH = sb4 + 24 * h
                    VH = vb4 + 5 * h
                    w(v, "cP", pgB + 2 * h + 2)
                    sqs = SQE[:, h * NT:(h + 1) * NT]
                    inc("cV", v.tensor_tensor(ZD[:, 0:NT], ozz, sqs, ALU.mult))
                    w(v, "cV", VH + 1)
                    inc("cV", v.tensor_scalar(ZD[:, NT:2 * NT], ZD[:, 0:NT],
                                              1e-6 * M, None, ALU.add))
                    w(v, "cV", VH + 2)
                    inc("cV", v.reciprocal(ZD[:, 2 * NT:3 * NT], ZD[:, NT:2 * NT]))
                    w(v, "cV", VH + 3)
                    inc("cV", v.tensor_tensor(ZD[:, 3 * NT:4 * NT],
                                              ZD[:, 2 * NT:3 * NT], sqs, ALU.mult))
                    w(v, "cV", VH + 4)
                    for t in range(NT):
                        ins = v.scalar_tensor_tensor(
                            X[:, t * D + h * DH:t * D + (h + 1) * DH],
                            OZB[:, t * 65:t * 65 + 64],
                            ZD[:, 3 * NT + t:3 * NT + t + 1],
                            X[:, t * D + h * DH:t * D + (h + 1) * DH],
                            ALU.mult, ALU.add)
                        if t == NT - 1:
                            inc("cV", ins)

            assert C["cT"] == tb4 + 28 * H
            assert C["cS"] == sb4 + 24 * H
            assert C["cV"] == vb4 + 5 * H
            return C["cV"]


        def emit_ffn(first):
            lnF = emit_ln_to_fm()
            sPrev, vPrev = C["cS"], C["cV"]
            qF, pF = C["qS"], C["cP"]
            WGB = Q[:, 0:KC * D]
            BGB = KVL[:, 0:D]
            LGF = SQF[:, 1024:2048]
            EXB = V[:, 4096:5120]
            WEL = V[:, 5120:6144]
            HEB = [V[:, 0:1024], V[:, 1024:2048]]
            NLM = NQ[:, 0:16]
            SCT = NQ[:, 16:32]
            PCOL = NQ[:, 32:48]
            EIDT = NQ[:, 48:64]
            MX8 = NK[:, 0:8]
            TMPC = NK[:, 8:9]
            MSK = SQE[:, 0:128]
            PH = [PA0, PA1, PK0, PK1]

            def wgcast(c):
                return pF + 2 + c

            rbase = pF + 1 + KC
            pE = rbase + 33

            def wecast(i):
                return pE + i + (i // 8) + 1

            def werelay(u):
                return pE + 9 * u + 9

            NU = 2 * E * KC

            def usrc(u):
                return (u // 8) % 8, u % 8, u // 64

            @block.sync
            def _(sync):
                w(sync, "cV", vPrev)
                inc("qS", sync.dma_start(out=SQF[0:1, 0:1024], in_=bg_d[:]), 16)
                beap = AP(be_d.tensor, 0, [[1, P], [D, E], [P, KC]])
                w(sync, "cV", vPrev)
                inc("qS", sync.dma_start(out=BEBT[:, :], in_=beap,
                                         allow_slow_non_contiguous=True), 16)
                for c in range(KC):
                    if c == 0:
                        w(sync, "cP", pF + 1)
                    elif c >= 2:
                        w(sync, "cP", wgcast(c - 2))
                    else:
                        w(sync, "cP", pF + 1)
                    src = AP(wg_d.tensor, (c * P) * D, [[D, P], [1, D]])
                    inc("qS", sync.dma_start(
                        out=SQF[:, (c % 2) * 1024:(c % 2) * 1024 + 1024],
                        in_=src), 16)
                for i in range(NU * KC):
                    u, c = i // 8, i % 8
                    e, fb, th = usrc(u)
                    if i >= 2:
                        w(sync, "cP", wecast(i - 2))
                    else:
                        w(sync, "cP", pF)
                    src = AP(we_d.tensor, (e * D + c * P) * D + fb * P,
                             [[D, P], [1, P]])
                    inc("qS", sync.dma_start(
                        out=WSTJ[:, (i % 2) * P:(i % 2 + 1) * P], in_=src), 16)

            tR, sR, vR = C["cT"], C["cS"], C["cV"]
            vE = vR + 4 * NT + 1

            @block.gpsimd
            def _(g):
                w(g, "qS", qF + 16)
                inc("cP", g.tensor_copy(BGB[0:1, :], SQF[0:1, 0:1024]))
                for c in range(KC):
                    w(g, "qS", qF + 48 + 16 * c)
                    inc("cP", g.tensor_copy(
                        WGB[:, c * D:(c + 1) * D],
                        SQF[:, (c % 2) * 1024:(c % 2) * 1024 + 1024]))
                for t in range(NT):
                    w(g, "cS", sR + 3 * t + 2)
                    inc("cP", g.tensor_copy(ZT[:, 1:2], ZT[:, 0:1]))
                    w(g, "cV", vR + 4 * t + 1)
                    inc("qG", g.dma_start(out=cb_d[:, 0:1],
                                          in_=NLM[:, t:t + 1],
                                          allow_slow_non_contiguous=True), 16)
                    w(g, "qG", C["qG"])
                    inc("qG", g.dma_start(out=NLM[:, t:t + 1],
                                          in_=cb_d[:, 0:1],
                                          allow_slow_non_contiguous=True), 16)
                    w(g, "qG", C["qG"])
                    inc("cP", g.tensor_copy(ZT[:, 1:2], ZT[:, 0:1]))
                w(g, "cS", sR + 3 * NT)
                inc("qG", g.dma_start(out=cb_d[:, 0:16], in_=SCT[:, :]), 16)
                w(g, "qG", C["qG"])
                inc("qG", g.dma_start(out=SCT[:, :], in_=cb_d[:, 0:16]), 16)
                w(g, "qG", C["qG"])
                inc("cP", g.tensor_copy(ZT[:, 1:2], ZT[:, 0:1]))
                for u in range(NU):
                    for c in range(KC):
                        i = u * KC + c
                        w(g, "qS", qF + 160 + 16 * (i + 1))
                        if u >= 1:
                            w(g, "cT", tR + 64 + 14 * (u - 1) + 2)
                        inc("cP", g.tensor_copy(
                            WEL[:, (i % 8) * P:(i % 8 + 1) * P],
                            WSTJ[:, (i % 2) * P:(i % 2 + 1) * P]))
                    w(g, "cS", sR + 3 * NT + 2 * u + 2)
                    inc("cP", g.tensor_copy(ZT[:, 1:2], ZT[:, 0:1]))

            tE = tR + 4 * NT

            @block.tensor
            def _(tn):
                w(tn, "cS", sPrev)
                w(tn, "cV", vPrev)
                w(tn, "tB", lnF)
                for t in range(NT):
                    for h2 in range(2):
                        if t == 0 and h2 == 0:
                            w(tn, "cP", wgcast(KC - 1))
                        if t >= 1:
                            w(tn, "cS", sR + 3 * (t - 1) + h2 + 1)
                        pa = PA[h2]
                        for c in range(KC):
                            tn.matmul(
                                pa[:, :],
                                XNF[:, c * TOK + t * P:c * TOK + (t + 1) * P],
                                WGB[:, c * D + h2 * 512:c * D + (h2 + 1) * 512],
                                start=(c == 0), stop=False)
                        inc("cT", tn.matmul(
                            pa[:, :], ONESB[0:1, 0:P],
                            BGB[0:1, h2 * 512:(h2 + 1) * 512],
                            start=False, stop=True))
                    inc("cT", dummy_ps(tn))
                    inc("cT", dummy_ps(tn))
                for u in range(NU):
                    e, fb, th = usrc(u)
                    TU = tE + 14 * u
                    for c in range(KC):
                        i = u * KC + c
                        w(tn, "cP", wecast(i))
                        if c == 0:
                            for tg in range(2):
                                if u == 0:
                                    w(tn, "cS", sR + 3 * NT)
                                else:
                                    w(tn, "cS", sR + 3 * NT + 2 * (u - 1) + tg + 1)
                        for tg in range(2):
                            mm = tn.matmul(
                                PA[tg][:, :],
                                WEL[:, (i % 8) * P:(i % 8 + 1) * P],
                                XNF[:, c * TOK + th * 1024 + tg * 512:
                                    c * TOK + th * 1024 + (tg + 1) * 512],
                                start=(c == 0), stop=(c == KC - 1))
                            if c == KC - 1:
                                inc("cT", mm)
                    inc("cT", dummy_ps(tn))
                    inc("cT", dummy_ps(tn))
                    w(tn, "cP", werelay(u))
                    for gg in range(KC):
                        if gg >= 2:
                            w(tn, "cV", vE + KC * u + gg - 1)
                        inc("cT", tn.transpose(
                            PT[gg % 2][:, :],
                            HEB[u % 2][:, gg * P:(gg + 1) * P], IDENT[:, :]))
                    inc("cT", dummy_ps(tn))
                    inc("cT", dummy_ps(tn))

            @block.scalar
            def _(s):
                for t in range(NT):
                    w(s, "cT", tR + 4 * t + 2)
                    inc("cS", s.activation(LGF[:, 0:512], PA0[:, :], AF.Copy))
                    w(s, "cT", tR + 4 * t + 4)
                    inc("cS", s.activation(LGF[:, 512:1024], PA1[:, :], AF.Copy))
                    w(s, "cP", rbase + 2 * t + 2)
                    inc("cS", s.activation(
                        EXB[:, :], LGF[:, :], AF.Exp,
                        bias=NLM[:, t:t + 1],
                        accum_out=SCT[:, t:t + 1]))
                for u in range(NU):
                    e, fb, th = usrc(u)
                    TU = tE + 14 * u
                    if u == 0:
                        w(s, "qS", qF + 32)
                    for tg in range(2):
                        if tg < 1:
                            w(s, "cT", TU + 2)
                        else:
                            w(s, "cT", TU + 4)
                        if u >= 2 and tg == 0:
                            w(s, "cT", tE + 14 * (u - 2) + 14)
                        inc("cS", s.activation(
                            HEB[u % 2][:, tg * 512:(tg + 1) * 512],
                            PA[tg][:, :], AF.Relu,
                            bias=BEBT[:, e * KC + fb:e * KC + fb + 1]))

            @block.vector
            def _(v):
                for t in range(NT):
                    w(v, "cP", rbase + 2 * t + 1)
                    inc("cV", v.reduce_max(NLM[:, t:t + 1], LGF[:, :],
                                           axis=AX.X, negate=True))
                    inc("cV", v.max(MX8[:, :], LGF[:, :]))
                    w(v, "cV", vR + 4 * t + 2)
                    inc("cV", v.max_index(IDXB[:, 0:8], MX8[:, :], LGF[:, :]))
                    w(v, "cV", vR + 4 * t + 3)
                    inc("cV", v.tensor_single_scalar(
                        IDXB[:, 1:2], IDXB[:, 0:1], 7, ALU.bitwise_and))
                    w(v, "cV", vR + 4 * t + 4)
                    v.tensor_copy(EIDT[:, t:t + 1], IDXB[:, 1:2])
                w(v, "cP", pE)
                for t in range(NT):
                    v.reciprocal(PCOL[:, t:t + 1], SCT[:, t:t + 1])
                for t in range(NT):
                    for e in range(E):
                        v.scalar_tensor_tensor(
                            MSK[:, t * 8 + e:t * 8 + e + 1], EIDT[:, t:t + 1],
                            float(e), PCOL[:, t:t + 1],
                            ALU.is_equal, ALU.mult)
                inc("cV", v.tensor_copy(ZT[:, 1:2], ZT[:, 0:1]))
                for u in range(NU):
                    e, fb, th = usrc(u)
                    TU = tE + 14 * u
                    for gg in range(KC):
                        t = th * KC + gg
                        if gg < KC - 1:
                            w(v, "cT", TU + 4 + gg + 2)
                        else:
                            w(v, "cT", TU + 14)
                        inc("cV", v.scalar_tensor_tensor(
                            X[:, t * D + fb * P:t * D + (fb + 1) * P],
                            PT[gg % 2][:, :],
                            MSK[:, t * 8 + e:t * 8 + e + 1],
                            X[:, t * D + fb * P:t * D + (fb + 1) * P],
                            ALU.mult, ALU.add))

            assert C["cT"] == tE + 14 * NU
            assert C["cS"] == sR + 3 * NT + 2 * NU
            assert C["cV"] == vE + KC * NU
            assert C["cP"] == pE + 9 * NU
            return C["cV"]


        def emit_ssm():
            TPI = 6.283185307179586
            REVP = OMT2[:, 0:P]
            prevS, prevV = C["cS"], C["cV"]

            @block.gpsimd
            def _(g):
                w(g, "cS", prevS)
                w(g, "cV", prevV)
                inc("cP", g.affine_select(
                    REVP, ONESB[:, :], pattern=[[1, P]],
                    compare_op=ALU.is_equal, fill=0.0, base=-127,
                    channel_multiplier=1))
            lnS = emit_ln_to_fm(perm=REVP)
            PRB = V[:, 0:4096]
            SSTK = V[:, 4096:8192]
            YB = V[:, 0:2048]
            YG = V[:, 2048:4096]
            UGS = [V[:, 8192:8320], V[:, 8320:8448]]
            UTS = [V[:, 8448:8576], V[:, 8576:8704]]
            PLT = [V[:, 8704:8832], V[:, 8832:8960]]
            KDT = [V[:, 8960:9216], V[:, 9216:9472]]
            WTT = [V[0:64, 9472:9728], V[0:64, 9728:9984]]
            STT = [V[0:64, 10880:10944], V[0:64, 10944:11008]]
            KCH = V[:, 9856:9888]
            KCH0 = V[:, 9888:9889]
            ZBL = V[:, 2048:2176]
            YSB = [V[:, 9984:10112], V[:, 10112:10240]]
            PSB = [SQF[0:64, 0:64], SQF[0:64, 64:128]]
            WTMPA = V[:, 0:1024]
            WTMPB = V[:, 1024:2048]
            CCT = V[:, 10240:12288]
            CCTB = V[:, 12288:14336]
            T1B = V[:, 14336:16384]
            USS = Q[:, :]
            LDTC = NQ[:, 64:65]
            ARE = NQ[:, 96:128]
            AIM = NQ[:, 128:160]
            CRE = NQ[:, 160:192]
            CIM = NQ[:, 192:224]
            DCOL = NQ[:, 66:67]
            EA = NQ[:, 224:256]
            DAR = NK[:, 0:32]
            DAI = NK[:, 32:64]
            YRC = NK[:, 64:96]
            KRC = NK[:, 96:128]
            XRD = NK[:, 128:160]
            COSA = NK[:, 160:192]
            K1C = NK[:, 192:224]
            PI2T = NK[:, 224:256]
            ERE2 = SQE[:, 0:32]
            SINV = SQE[:, 32:64]
            COSV = SQE[:, 64:96]
            LK = SQE[:, 96:160]
            CC2F = SQE[:, 160:224]
            LRE = OZB[:, 1024:1025]
            LREV = SQE[:, 224:256]
            LIMV = NQ[:, 0:32]
            LTB = KVL[:, 1024:1088]
            CCA = KVL[:, 1088:1152]
            CCB = KVL[:, 1152:1216]
            PW = SQF[:, 0:2048]
            CM1 = OZB[:, 0:32]
            CM2 = OZB[:, 32:64]
            H1 = OZB[:, 0:1024]
            H2 = OZB[:, 0:512]
            H3 = NK[:, 0:256]
            H4 = OZB[:, 0:128]
            H5 = OZB[:, 128:192]
            H6 = OZB[:, 192:224]
            KZC = OZB[:, 224:225]

            sP, vP, tP, cPb, qSb, qGb = (C["cS"], C["cV"], C["cT"],
                                         C["cP"], C["qS"], C["qG"])

            @block.sync
            def _(sync):
                for dst, src in [(LDTC, ldt_d[:, :]), (ARE, are_d[:, :]),
                                 (AIM, aim_d[:, :]), (CRE, cre_d[:, :]),
                                 (CIM, cim_d[:, :]), (DCOL, ddb_d[:, :]),
                                 (UIDXS[:, :], uidx_d[:, :]),
                                 (USIDX[:, :], ussidx_d[:, :])]:
                    w(sync, "cV", vP)
                    inc("qS", sync.dma_start(out=dst, in_=src), 16)

            @block.gpsimd
            def _(g):
                w(g, "tB", lnS)
                inc("qG", g.dma_start(out=xg_loc[:, :], in_=XNF[:, :]), 16)
                w(g, "qG", C["qG"])
                inc("car", g.collective_compute(
                    "AllGather", ALU.bypass,
                    replica_groups=[[0, 1, 2, 3, 4, 5, 6, 7]],
                    ins=[xg_loc],
                    outs=[xg_all.rearrange("(r p) w -> r p w", p=P)]))
                w(g, "car", C["car"])
                for sl in range(8):
                    w(g, "qS", qSb + 16 * 8)
                    inc("qG", g.indirect_dma_start(
                        out=USS[:, sl * TOK:(sl + 1) * TOK],
                        out_offset=None,
                        in_=xg2k,
                        in_offset=IndirectOffsetOnAxis(
                            ap=USIDX[:, sl:sl + 1], axis=0)), 16)
                g.memset(PI2T[:, :], TPI / 4.0)
                g.memset(ZBL[:, :], 0.0)
                inc("cP", g.memset(SSTK[:, :], 0.0))
                inc("cP", g.tensor_copy(ZT[:, 1:2], ZT[:, 0:1]))
            gUSS = C["qG"]
            gC = gUSS
            pMS = C["cP"]

            cb0 = C["cS"]

            DTB = OZB[:, 256:288]

            @block.scalar
            def _(s):
                w(s, "qS", qSb + 16 * 6)
                inc("cS", s.activation(NQ[:, 65:66], LDTC, AF.Exp))
                inc("cS", s.activation(EA, ARE, AF.Exp))
                s.activation(ZT[:, 2:3], ZT[:, 0:1], AF.Copy)
                s.activation(ZT[:, 2:3], ZT[:, 0:1], AF.Copy)
                inc("cS", s.activation(DTB, ONESB[:, 0:32], AF.Copy,
                                       scale=NQ[:, 65:66]))

            rel0 = C["cP"]

            @block.gpsimd
            def _(g):
                w(g, "cS", cb0 + 3)
                inc("qG", g.dma_start(out=p_dram[:, 100:132], in_=EA), 16)
                inc("qG", g.dma_start(out=p_dram[:, 132:164], in_=DTB), 16)
                w(g, "qG", C["qG"])
                inc("qG", g.dma_start(out=EA, in_=p_dram[:, 100:132]), 16)
                inc("qG", g.dma_start(out=DTB, in_=p_dram[:, 132:164]), 16)
                w(g, "qG", C["qG"])
                inc("cP", g.tensor_copy(ZT[:, 1:2], ZT[:, 0:1]))

            vb0 = C["cV"]
            vpos = {}

            @block.vector
            def _(v):
                def B(op):
                    inc("cV", op)
                    w(v, "cV", C["cV"])

                w(v, "cP", rel0 + 1)
                for _ in range(6):
                    v.tensor_copy(CM1, EA)
                B(v.scalar_tensor_tensor(DAR, EA, -1.0, DTB, ALU.mult, ALU.mult))
                B(v.tensor_tensor(DAI, AIM, DTB, ALU.mult))
                B(v.tensor_single_scalar(YRC, DAI, 1.0 / TPI, ALU.mult))
                B(v.tensor_single_scalar(KRC, YRC, 0.5, ALU.is_ge))
                B(v.tensor_single_scalar(K1C, YRC, 1.5, ALU.is_ge))
                B(v.tensor_tensor(KRC, KRC, K1C, ALU.add))
                B(v.tensor_single_scalar(K1C, YRC, -0.5, ALU.is_le))
                B(v.tensor_tensor(KRC, KRC, K1C, ALU.subtract))
                B(v.tensor_single_scalar(K1C, YRC, -1.5, ALU.is_le))
                B(v.tensor_tensor(KRC, KRC, K1C, ALU.subtract))
                B(v.scalar_tensor_tensor(XRD, KRC, -TPI, DAI, ALU.mult, ALU.add))
                B(v.scalar_tensor_tensor(COSA, XRD, -1.0, PI2T, ALU.mult, ALU.add))
                B(v.tensor_single_scalar(K1C, COSA, TPI / 2.0, ALU.is_gt))
                if DEBUG:
                    v.tensor_copy(KVL[:, 2256:2288], KRC)
                    v.tensor_copy(KVL[:, 2288:2320], YRC)
                    v.tensor_copy(KVL[:, 2000:2032], EA)
                    v.tensor_copy(KVL[:, 2032:2064], DAR)
                    v.tensor_copy(KVL[:, 2064:2096], DAI)
                    v.tensor_copy(KVL[:, 2096:2128], XRD)
                B(v.scalar_tensor_tensor(COSA, K1C, -TPI, COSA,
                                         ALU.mult, ALU.add))
                vpos["cosa"] = C["cV"]

            rel1 = C["cP"]

            @block.gpsimd
            def _(g):
                w(g, "cV", vpos["cosa"])
                inc("cP", g.tensor_copy(ZT[:, 1:2], ZT[:, 0:1]))

            @block.scalar
            def _(s):
                w(s, "cP", rel1 + 1)
                inc("cS", s.activation(SINV, XRD, AF.Sin))
                inc("cS", s.activation(COSV, COSA, AF.Sin))
                inc("cS", s.activation(ERE2, DAR, AF.Exp))

            rel2 = C["cP"]

            @block.gpsimd
            def _(g):
                w(g, "cS", cb0 + 6)
                inc("qG", g.dma_start(out=p_dram[:, 0:96], in_=SQE[:, 0:96]), 16)
                w(g, "qG", C["qG"])
                inc("qG", g.dma_start(out=SQE[:, 0:96], in_=p_dram[:, 0:96]), 16)
                w(g, "qG", C["qG"])
                inc("cP", g.tensor_copy(ZT[:, 1:2], ZT[:, 0:1]))

            vb1 = C["cV"]
            gW = C["qG"]
            gpos = {}

            def gk_pos(kind, k):
                off = {"k": 4, "pp": 5, "wre": 6, "wim": 7}[kind]
                return gW + 16 * (off + 4 * k)

            @block.vector
            def _(v):
                def B(op):
                    inc("cV", op)
                    w(v, "cV", C["cV"])

                w(v, "cP", rel2 + 1)
                for _ in range(6):
                    v.tensor_copy(CM1, CRE)
                B(v.tensor_tensor(LREV, ERE2, COSV, ALU.mult))
                B(v.tensor_tensor(LIMV, ERE2, SINV, ALU.mult))
                if DEBUG:
                    v.tensor_copy(KVL[:, 2128:2160], LREV)
                    v.tensor_copy(KVL[:, 2160:2192], LIMV)
                    v.tensor_copy(KVL[:, 2192:2224], ERE2)
                    v.tensor_copy(KVL[:, 2224:2256], SINV)
                QR = OZB[:, 288:320]
                QI = OZB[:, 320:352]
                IA2 = OZB[:, 352:384]
                NR = OZB[:, 384:416]
                NI = OZB[:, 416:448]
                AR2 = OZB[:, 448:480]
                B(v.tensor_single_scalar(AR2, EA, -1.0, ALU.mult))
                B(v.tensor_single_scalar(NR, LREV, 1.0, ALU.subtract))
                B(v.tensor_copy(NI, LIMV))
                B(v.tensor_tensor(CM1, AR2, AR2, ALU.mult))
                B(v.tensor_tensor(CM2, AIM, AIM, ALU.mult))
                B(v.tensor_tensor(IA2, CM1, CM2, ALU.add))
                B(v.reciprocal(IA2, IA2))
                B(v.tensor_tensor(CM1, NR, AR2, ALU.mult))
                B(v.tensor_tensor(CM2, NI, AIM, ALU.mult))
                B(v.tensor_tensor(QR, CM1, CM2, ALU.add))
                B(v.tensor_tensor(QR, QR, IA2, ALU.mult))
                B(v.tensor_tensor(CM1, NI, AR2, ALU.mult))
                B(v.tensor_tensor(CM2, NR, AIM, ALU.mult))
                B(v.tensor_tensor(QI, CM1, CM2, ALU.subtract))
                B(v.tensor_tensor(QI, QI, IA2, ALU.mult))
                B(v.tensor_tensor(CM1, CRE, QR, ALU.mult))
                B(v.tensor_tensor(CM2, CIM, QI, ALU.mult))
                B(v.tensor_tensor(CC2F[:, 0:32], CM1, CM2, ALU.subtract))
                B(v.tensor_single_scalar(CC2F[:, 0:32], CC2F[:, 0:32], 2.0,
                                         ALU.mult))
                B(v.tensor_tensor(CM1, CRE, QI, ALU.mult))
                B(v.tensor_tensor(CM2, CIM, QR, ALU.mult))
                B(v.tensor_tensor(CC2F[:, 32:64], CM1, CM2, ALU.add))
                B(v.tensor_single_scalar(CC2F[:, 32:64], CC2F[:, 32:64], -2.0,
                                         ALU.mult))
                B(v.tensor_copy(CCA, CC2F))
                B(v.tensor_copy(CCB[:, 0:32], CC2F[:, 32:64]))
                B(v.tensor_single_scalar(CCB[:, 32:64], CC2F[:, 0:32], -1.0,
                                         ALU.mult))
                B(v.tensor_copy(CCT[:, 0:64], CCA))
                B(v.tensor_copy(CCTB[:, 0:64], CCB))
                wdt = 64
                while wdt < 2048:
                    B(v.tensor_copy(CCT[:, wdt:2 * wdt], CCT[:, 0:wdt]))
                    B(v.tensor_copy(CCTB[:, wdt:2 * wdt], CCTB[:, 0:wdt]))
                    wdt *= 2
                B(v.tensor_tensor(KZC[:, 0:1], CC2F[:, 0:1], CC2F[:, 1:2],
                                  ALU.add))
                for n in range(2, 32):
                    B(v.tensor_tensor(KZC[:, 0:1], KZC[:, 0:1],
                                      CC2F[:, n:n + 1], ALU.add))
                B(v.tensor_copy(KCH0[:, 0:1], KZC[:, 0:1]))
                vpos["k0"] = C["cV"]

                def cmul_into(dre, dim_, sre, sim):
                    B(v.tensor_tensor(CM1, sre, LREV, ALU.mult))
                    B(v.tensor_tensor(CM2, sim, LIMV, ALU.mult))
                    B(v.tensor_tensor(dre, CM1, CM2, ALU.subtract))
                    B(v.tensor_tensor(CM1, sre, LIMV, ALU.mult))
                    B(v.tensor_tensor(CM2, sim, LREV, ALU.mult))
                    B(v.tensor_tensor(dim_, CM1, CM2, ALU.add))

                for k in range(8):
                    if k >= 1:
                        w(v, "qG", gk_pos("pp", k - 1))
                    if k == 0:
                        B(v.tensor_copy(PW[:, 0:32], LREV))
                        B(v.tensor_copy(PW[:, 32:64], LIMV))
                    else:
                        cmul_into(PW[:, 0:32], PW[:, 32:64],
                                  LK[:, 0:32], LK[:, 32:64])
                    for l in range(1, 32):
                        cmul_into(PW[:, l * 64:l * 64 + 32],
                                  PW[:, l * 64 + 32:l * 64 + 64],
                                  PW[:, (l - 1) * 64:(l - 1) * 64 + 32],
                                  PW[:, (l - 1) * 64 + 32:(l - 1) * 64 + 64])
                    B(v.tensor_copy(LK, PW[:, 31 * 64:32 * 64]))
                    nl = 32 if k < 7 else 31
                    if k >= 1:
                        w(v, "qG", gk_pos("wim", k - 1))
                    WV, WO = 16384, 1040
                    B(v.tensor_tensor(T1B, PW, CCT, ALU.mult))
                    sA = AP(T1B.tensor, T1B.offset, [[WV, P], [64, 32], [1, 32]])
                    sBp = AP(T1B.tensor, T1B.offset + 32,
                             [[WV, P], [64, 32], [1, 32]])
                    B(v.tensor_tensor(H1, sA, sBp, ALU.add))
                    if k >= 1:
                        w(v, "qG", gk_pos("wre", k - 1))
                    h1t = AP(H1.tensor, H1.offset,
                             [[WO, P], [1, 32], [32, nl]])
                    B(v.tensor_copy(WTMPA[:, 0:32 * nl], h1t))
                    h1a = AP(H1.tensor, H1.offset, [[WO, P], [32, 32], [1, 16]])
                    h1b = AP(H1.tensor, H1.offset + 16,
                             [[WO, P], [32, 32], [1, 16]])
                    B(v.tensor_tensor(H2, h1a, h1b, ALU.add))
                    h2a = AP(H2.tensor, H2.offset, [[WO, P], [16, 32], [1, 8]])
                    h2b = AP(H2.tensor, H2.offset + 8,
                             [[WO, P], [16, 32], [1, 8]])
                    B(v.tensor_tensor(H3, h2a, h2b, ALU.add))
                    h3a = AP(H3.tensor, H3.offset, [[256, P], [8, 32], [1, 4]])
                    h3b = AP(H3.tensor, H3.offset + 4,
                             [[256, P], [8, 32], [1, 4]])
                    B(v.tensor_tensor(H4, h3a, h3b, ALU.add))
                    h4a = AP(H4.tensor, H4.offset, [[WO, P], [4, 32], [1, 2]])
                    h4b = AP(H4.tensor, H4.offset + 2,
                             [[WO, P], [4, 32], [1, 2]])
                    B(v.tensor_tensor(H5, h4a, h4b, ALU.add))
                    h5a = AP(H5.tensor, H5.offset, [[WO, P], [2, 32], [1, 1]])
                    h5b = AP(H5.tensor, H5.offset + 1,
                             [[WO, P], [2, 32], [1, 1]])
                    B(v.tensor_tensor(H6, h5a, h5b, ALU.add))
                    if k >= 1:
                        w(v, "qG", gk_pos("k", k - 1))
                    B(v.tensor_copy(KCH[:, 0:32], H6[:, 0:32]))
                    vpos[("kwre", k)] = C["cV"]
                    B(v.tensor_tensor(T1B, PW, CCTB, ALU.mult))
                    B(v.tensor_tensor(H1, sA, sBp, ALU.add))
                    B(v.tensor_copy(WTMPB[:, 0:32 * nl], h1t))
                    vpos[("wim", k)] = C["cV"]
                B(v.tensor_copy(LTB, PW[:, 31 * 64:32 * 64]))
                vpos["ltb"] = C["cV"]

            @block.gpsimd
            def _(g):
                w(g, "cV", vpos["k0"])
                inc("qG", g.dma_start(
                    out=AP(k_dram.tensor, 127, [[384, P], [1, 1]]),
                    in_=KCH0[:, 0:1], allow_slow_non_contiguous=True), 16)
                inc("qG", g.dma_start(
                    out=AP(k_dram.tensor, 0, [[384, P], [1, 127]]),
                    in_=ZBL[:, 0:127]), 16)
                inc("qG", g.dma_start(
                    out=AP(w_dram.tensor, 0, [[64 * 256, P], [256, 64]]),
                    in_=CCA, allow_slow_non_contiguous=True), 16)
                for k in range(8):
                    nl = 32 if k < 7 else 31
                    w(g, "cV", vpos[("kwre", k)])
                    inc("qG", g.dma_start(
                        out=AP(k_dram.tensor, 127 + 32 * k + 1,
                               [[384, P], [1, nl]]),
                        in_=KCH[:, 0:nl]), 16)
                    gpos[("k", k)] = C["qG"]
                    jj = 32 * ((k + 4) % 8)
                    w(g, "cV", vpos[("kwre", k)])
                    inc("qG", g.dma_start(
                        out=AP(pp_dram.tensor, jj * 64,
                               [[256 * 64, P], [64, 32], [1, 64]]),
                        in_=PW[:, 0:2048]), 16)
                    gpos[("pp", k)] = C["qG"]
                    w(g, "cV", vpos[("kwre", k)])
                    inc("qG", g.dma_start(
                        out=AP(w_dram.tensor, 32 * k + 1,
                               [[64 * 256, P], [256, 32], [1, nl]]),
                        in_=WTMPA[:, 0:32 * nl]), 16)
                    gpos[("wre", k)] = C["qG"]
                    w(g, "cV", vpos[("wim", k)])
                    inc("qG", g.dma_start(
                        out=AP(w_dram.tensor, (32 * 256) + 32 * k + 1,
                               [[64 * 256, P], [256, 32], [1, nl]]),
                        in_=WTMPB[:, 0:32 * nl]), 16)
                    gpos[("wim", k)] = C["qG"]
                    assert gpos[("wim", k)] == gk_pos("wim", k)
            gK = C["qG"]
            consts_cV = C["cV"]

            t1T, s1S, g1G = C["cT"], C["cS"], C["qG"]

            @block.sync
            def _(sync):
                for ch in range(P):
                    for jb in range(2):
                        if ch >= 2:
                            w(sync, "cT", t1T + 6 * (ch - 2) + 4)
                        else:
                            w(sync, "qG", gK)
                        src = AP(pp_dram.tensor, (ch * 256 + jb * 128) * 64,
                                 [[64, P], [1, 64]])
                        inc("qS", sync.dma_start(
                            out=PLT[ch % 2][:, jb * 64:(jb + 1) * 64],
                            in_=src), 16)

            @block.gpsimd
            def _(g):
                w(g, "qG", gK)
                for ch in range(P):
                    if ch >= 2:
                        w(g, "cT", t1T + 6 * (ch - 2) + 1)
                    else:
                        w(g, "qG", gK)
                    inc("qG", g.indirect_dma_start(
                        out=UGS[ch % 2][:, :], out_offset=None,
                        in_=xg128,
                        in_offset=IndirectOffsetOnAxis(
                            ap=UIDXS[:, ch:ch + 1], axis=0)), 16)
                    if ch >= 1:
                        w(g, "cS", s1S + 2 * (ch - 1) + 2)
                        inc("qG", g.dma_start(
                            out=AP(p_dram.tensor, (ch - 1) * 4096,
                                   [[64, 64], [1, 64]]),
                            in_=PSB[(ch - 1) % 2][:, :]), 16)
                w(g, "cS", s1S + 2 * (P - 1) + 2)
                inc("qG", g.dma_start(
                    out=AP(p_dram.tensor, (P - 1) * 4096,
                           [[64, 64], [1, 64]]),
                    in_=PSB[(P - 1) % 2][:, :]), 16)

            @block.tensor
            def _(tn):
                w(tn, "cV", consts_cV)
                for ch in range(P):
                    w(tn, "qG", g1G + 16 * (2 * ch + 1))
                    if ch >= 2:
                        w(tn, "cS", s1S + 2 * (ch - 2) + 1)
                    inc("cT", tn.transpose(PT[ch % 2][:, :], UGS[ch % 2][:, :],
                                           IDENT[:, :]))
                    inc("cT", dummy_ps(tn))
                    inc("cT", dummy_ps(tn))
                    w(tn, "cS", s1S + 2 * ch + 1)
                    w(tn, "qS", qSb + 16 * (8 + 2 * ch + 2))
                    if ch >= 1:
                        w(tn, "cS", s1S + 2 * (ch - 1) + 2)
                    tn.matmul(PKV[:, 0:64], PLT[ch % 2][:, 0:64],
                              UTS[ch % 2][:, 0:64], start=True, stop=False)
                    inc("cT", tn.matmul(PKV[:, 0:64], PLT[ch % 2][:, 64:128],
                                        UTS[ch % 2][:, 64:128],
                                        start=False, stop=True))
                    inc("cT", dummy_ps(tn))
                    inc("cT", dummy_ps(tn))

            @block.scalar
            def _(s):
                for ch in range(P):
                    w(s, "cT", t1T + 6 * ch + 3)
                    inc("cS", s.activation(UTS[ch % 2][:, :],
                                           PT[ch % 2][:, :], AF.Copy))
                    w(s, "cT", t1T + 6 * ch + 6)
                    if ch >= 2:
                        w(s, "qG", g1G + 16 * (2 * (ch - 2) + 2))
                    inc("cS", s.activation(PSB[ch % 2][:, :],
                                           PKV[:, 0:64], AF.Copy))
            p1_done_qG = C["qG"]

            g2G = C["qG"]

            @block.gpsimd
            def _(g):
                w(g, "qG", p1_done_qG)
                inc("qG", g.dma_start(out=PRB[:, :], in_=p_dram[:, :]), 16)
            rec_in_qG = C["qG"]
            vR2 = C["cV"]

            @block.vector
            def _(v):
                w(v, "qG", rec_in_qG)
                w(v, "cP", pMS)
                A1 = SQF[:, 0:32]
                B1 = SQF[:, 32:64]
                D1 = SQF[:, 64:96]
                WV = 16384

                def scol(buf, call, im):
                    return AP(buf.tensor, buf.offset + 32 * 64 * im + call,
                              [[WV, P], [64, 32]])

                def B2(op):
                    inc("cV", op)
                    w(v, "cV", C["cV"])

                for b in range(2):
                    for c in range(1, 32):
                        call = b * 32 + c
                        B2(v.tensor_tensor(A1, scol(SSTK, call - 1, 0),
                                           LTB[:, 0:32], ALU.mult))
                        B2(v.tensor_tensor(B1, scol(SSTK, call - 1, 1),
                                           LTB[:, 32:64], ALU.mult))
                        B2(v.tensor_tensor(D1, A1, B1, ALU.subtract))
                        B2(v.tensor_tensor(scol(SSTK, call, 0), D1,
                                           scol(PRB, call - 1, 0), ALU.add))
                        B2(v.tensor_tensor(A1, scol(SSTK, call - 1, 0),
                                           LTB[:, 32:64], ALU.mult))
                        B2(v.tensor_tensor(B1, scol(SSTK, call - 1, 1),
                                           LTB[:, 0:32], ALU.mult))
                        B2(v.tensor_tensor(D1, A1, B1, ALU.add))
                        B2(v.tensor_tensor(scol(SSTK, call, 1), D1,
                                           scol(PRB, call - 1, 1), ALU.add))
                inc("cV", v.tensor_copy(ZT[:, 1:2], ZT[:, 0:1]))

            @block.gpsimd
            def _(g):
                w(g, "cV", vR2 + 62 * 8 + 1)
                inc("qG", g.dma_start(out=s_dram[:, :], in_=SSTK[:, :]), 16)
            s_done_qG = C["qG"]

            t2T, s2S, g2G2, q2S = C["cT"], C["cS"], C["qG"], C["qS"]

            def YT2(ch, ib):
                off = 10368 + 128 * (2 * (ch % 2) + ib)
                return V[0:64, off:off + 128]

            @block.sync
            def _(sync):
                for ch in range(P):
                    if ch >= 2:
                        w(sync, "cT", t2T + 11 * (ch - 2) + 11)
                    else:
                        w(sync, "qG", s_done_qG)
                    inc("qS", sync.dma_start(
                        out=KDT[ch % 2][:, 0:128],
                        in_=AP(k_dram.tensor, ch * 384, [[1, P], [1, P]])), 16)
                    inc("qS", sync.dma_start(
                        out=KDT[ch % 2][:, 128:256],
                        in_=AP(k_dram.tensor, ch * 384 + 128,
                               [[1, P], [1, P]])), 16)
                    inc("qS", sync.dma_start(
                        out=WTT[ch % 2][:, :],
                        in_=AP(w_dram.tensor, ch * 64 * 256,
                               [[256, 64], [1, 256]])), 16)
                    inc("qS", sync.dma_start(
                        out=STT[ch % 2][:, :],
                        in_=AP(s_dram.tensor, ch * 4096,
                               [[64, 64], [1, 64]])), 16)

            @block.gpsimd
            def _(g):
                w(g, "qG", s_done_qG)
                for ch in range(P):
                    if ch >= 2:
                        w(g, "cT", t2T + 11 * (ch - 2) + 1)
                    else:
                        w(g, "qG", s_done_qG)
                    inc("qG", g.indirect_dma_start(
                        out=UGS[ch % 2][:, :], out_offset=None,
                        in_=xg128,
                        in_offset=IndirectOffsetOnAxis(
                            ap=UIDXS[:, ch:ch + 1], axis=0)), 16)
                    w(g, "cS", s2S + 5 * ch + 5)
                    for ib in range(2):
                        inc("qG", g.dma_start(
                            out=AP(y_dram.tensor, ch * 16384 + ib * 128,
                                   [[256, 64], [1, 128]]),
                            in_=YT2(ch, ib)), 16)

            @block.tensor
            def _(tn):
                w(tn, "qG", s_done_qG)
                for ch in range(P):
                    w(tn, "qG", g2G2 + 16 * (3 * ch + 1))
                    if ch >= 1:
                        w(tn, "cS", s2S + 5 * (ch - 1) + 5)
                    inc("cT", tn.transpose(PT[ch % 2][:, :], UGS[ch % 2][:, :],
                                           IDENT[:, :]))
                    inc("cT", dummy_ps(tn))
                    inc("cT", dummy_ps(tn))
                    w(tn, "cS", s2S + 5 * ch + 1)
                    w(tn, "qS", q2S + 16 * 4 * (ch + 1))
                    tn.matmul(PK0[:, 0:64], KDT[ch % 2][:, 0:128],
                              UTS[ch % 2][:, 0:64], start=True, stop=False)
                    inc("cT", tn.matmul(PK0[:, 0:64], WTT[ch % 2][0:64, 0:128],
                                        STT[ch % 2][:, :],
                                        start=False, stop=True))
                    tn.matmul(PK1[:, 0:64], KDT[ch % 2][:, 128:256],
                              UTS[ch % 2][:, 0:64], start=True, stop=False)
                    tn.matmul(PK1[:, 0:64], KDT[ch % 2][:, 0:128],
                              UTS[ch % 2][:, 64:128], start=False, stop=False)
                    inc("cT", tn.matmul(PK1[:, 0:64],
                                        WTT[ch % 2][0:64, 128:256],
                                        STT[ch % 2][:, :],
                                        start=False, stop=True))
                    inc("cT", dummy_ps(tn))
                    inc("cT", dummy_ps(tn))
                    w(tn, "cS", s2S + 5 * ch + 3)
                    inc("cT", tn.transpose(PT[ch % 2][0:64, :],
                                           YSB[ch % 2][:, 0:64], IDENT[:, :]))
                    inc("cT", tn.transpose(PT[(ch + 1) % 2][0:64, :],
                                           YSB[ch % 2][:, 64:128], IDENT[:, :]))
                    inc("cT", dummy_ps(tn))
                    inc("cT", dummy_ps(tn))

            @block.scalar
            def _(s):
                for ch in range(P):
                    w(s, "cT", t2T + 11 * ch + 3)
                    inc("cS", s.activation(UTS[ch % 2][:, :],
                                           PT[ch % 2][:, :], AF.Copy))
                    if ch >= 2:
                        w(s, "cT", t2T + 11 * (ch - 2) + 11)
                    w(s, "cT", t2T + 11 * ch + 5)
                    inc("cS", s.activation(YSB[ch % 2][:, 0:64],
                                           PK0[:, 0:64], AF.Copy))
                    w(s, "cT", t2T + 11 * ch + 7)
                    inc("cS", s.activation(YSB[ch % 2][:, 64:128],
                                           PK1[:, 0:64], AF.Copy))
                    w(s, "cT", t2T + 11 * ch + 9)
                    inc("cS", s.activation(YT2(ch, 0), PT[ch % 2][0:64, :],
                                           AF.Copy))
                    w(s, "cT", t2T + 11 * ch + 11)
                    inc("cS", s.activation(YT2(ch, 1), PT[(ch + 1) % 2][0:64, :],
                                           AF.Copy))
            y_done_qG = C["qG"]

            g3G, v3V, s3S = C["qG"], C["cV"], C["cS"]

            @block.gpsimd
            def _(g):
                w(g, "qG", y_done_qG)
                for blk in range(8):
                    if blk >= 2:
                        w(g, "cS", s3S + (blk - 2) + 1)
                    inc("qG", g.dma_start(
                        out=YB[:, 0:2048],
                        in_=y_dram[:, blk * 2048:(blk + 1) * 2048]), 16)
                    w(g, "cS", s3S + blk + 1)
                    inc("qG", g.dma_start(
                        out=yfin_dram[:, blk * 2048:(blk + 1) * 2048],
                        in_=YG[:, 0:2048]), 16)
                w(g, "qG", C["qG"])
                inc("car", g.collective_compute(
                    "AllGather", ALU.bypass,
                    replica_groups=[[0, 1, 2, 3, 4, 5, 6, 7]],
                    ins=[yfin_dram],
                    outs=[yg_all.rearrange("(r p) w -> r p w", p=P)]))
            du_car = C["car"]

            @block.vector
            def _(v):
                for blk in range(8):
                    w(v, "qG", g3G + 16 * (2 * blk + 1))
                    rev = AP(USS.tensor, USS.offset + blk * 2048 + 127,
                             [[P * 128, P], [128, 16], [-1, 128]])
                    inc("cV", v.tensor_scalar(SQF[:, 0:2048], rev, DCOL, None,
                                              ALU.mult))
                    w(v, "cV", C["cV"])
                    inc("cV", v.tensor_tensor(SQF[:, 0:2048], SQF[:, 0:2048],
                                              YB[:, 0:2048], ALU.add))

            @block.scalar
            def _(s):
                for blk in range(8):
                    w(s, "cV", v3V + 2 * blk + 2)
                    inc("cS", s.activation(YG[:, 0:2048], SQF[:, 0:2048],
                                           AF.Gelu))

            t4T, v4V, g4G = C["cT"], C["cV"], C["qG"]

            @block.sync
            def _(sync):
                w(sync, "qG", y_done_qG)
                inc("qS", sync.dma_start(out=UIDXS[:, :], in_=yidx_d[:, :]), 16)
            yidx_qS = C["qS"]

            @block.gpsimd
            def _(g):
                w(g, "car", du_car)
                w(g, "qS", yidx_qS)
                for gidx in range(P):
                    if gidx >= 2:
                        w(g, "cT", t4T + 3 * (gidx - 2) + 1)
                    else:
                        w(g, "qS", yidx_qS)
                    inc("qG", g.indirect_dma_start(
                        out=UGS[gidx % 2][:, :], out_offset=None,
                        in_=yg128,
                        in_offset=IndirectOffsetOnAxis(
                            ap=UIDXS[:, gidx:gidx + 1], axis=0)), 16)

            @block.tensor
            def _(tn):
                for gidx in range(P):
                    w(tn, "qG", g4G + 16 * (gidx + 1))
                    if gidx >= 2:
                        w(tn, "cV", v4V + gidx - 1)
                    inc("cT", tn.transpose(PT[gidx % 2][:, :],
                                           UGS[gidx % 2][:, :], IDENT[:, :]))
                    inc("cT", dummy_ps(tn))
                    inc("cT", dummy_ps(tn))

            @block.vector
            def _(v):
                for gidx in range(P):
                    cbk, t = gidx // 16, gidx % 16
                    w(v, "cT", t4T + 3 * gidx + 3)
                    inc("cV", v.tensor_tensor(
                        X[:, t * D + cbk * P:t * D + (cbk + 1) * P],
                        X[:, t * D + cbk * P:t * D + (cbk + 1) * P],
                        PT[gidx % 2][:, :], ALU.add))
            return C["cV"]

        ln1_tB = emit_ln_to_fm()
        attn_done_cV = emit_attn(ln1_tB)
        x_done = ("cV", attn_done_cV)
        if stage >= 2:
            ffn1_done_cV = emit_ffn(first=True)
            x_done = ("cV", ffn1_done_cV)
        if stage >= 3:
            ssm_done_cV = emit_ssm()
            x_done = ("cV", ssm_done_cV)
        if stage >= 4:
            ffn2_done_cV = emit_ffn(first=False)
            x_done = ("cV", ffn2_done_cV)

        @block.gpsimd
        def _(g):
            w(g, x_done[0], x_done[1])
            xo = xout.rearrange("(t p) d -> t p d", p=P)
            for t in range(NT):
                inc("qG", g.dma_start(out=xo[t], in_=X[:, t * D:(t + 1) * D]), 16)
            if DEBUG:
                for nm, src in [("d_xnf", XNF), ("d_q", Q), ("d_v", V),
                                ("d_kvl", KVL[:, :]), ("d_nq", NQ), ("d_nk", NK),
                                ("d_sqe", SQE)]:
                    sap = src if isinstance(src, AP) else src[:, :]
                    inc("qG", g.dma_start(out=dbg[nm][:, :], in_=sap), 16)
                if stage >= 3:
                    for nm, src in [("d_k", k_dram), ("d_p", p_dram),
                                    ("d_s", s_dram), ("d_y", y_dram),
                                    ("d_yf", yfin_dram), ("d_w", w_dram)]:
                        inc("qG", g.dma_start(out=dbg[nm][:, :], in_=src[:, :]),
                            16)
                    inc("qG", g.dma_start(out=dbg["d_uss"][:, :],
                                          in_=Q[:, :]), 16)
            w(g, "qG", C["qG"])

    return nc


def kernel(**inputs):
    x = np.ascontiguousarray(np.asarray(inputs["L0_attn_a_x"], dtype=np.float32))
    wq = np.ascontiguousarray(np.asarray(inputs["L0_attn_a_Wq"], dtype=np.float32))
    wk = np.ascontiguousarray(np.asarray(inputs["L0_attn_a_Wk"], dtype=np.float32))
    wv = np.ascontiguousarray(np.asarray(inputs["L0_attn_a_Wv"], dtype=np.float32))
    om = np.ascontiguousarray(np.asarray(inputs["L0_attn_a_omega"], dtype=np.float32))
    wg = np.ascontiguousarray(np.asarray(inputs["L0_ffn_Wg"], dtype=np.float32))
    bg = np.ascontiguousarray(np.asarray(inputs["L0_ffn_bg"], dtype=np.float32))
    we = np.ascontiguousarray(
        np.asarray(inputs["L0_ffn_We"], dtype=np.float32).reshape(E * D, D))
    be = np.ascontiguousarray(np.asarray(inputs["L0_ffn_be"], dtype=np.float32))

    nc = build(STAGE)
    in_maps = []
    for r in range(8):
        b, qq = r // 4, r % 4
        in_maps.append({
            "xin": np.ascontiguousarray(x[b, qq * TOK:(qq + 1) * TOK, :]),
            "wq": wq, "wk": wk, "wv": wv, "om": om,
        })
        if STAGE >= 2:
            in_maps[-1].update({"wg": wg, "bg": bg, "we": we, "be": be})
        if STAGE >= 3:
            ch0 = r * 128
            uidx = np.zeros((128, 128), dtype=np.uint32)
            for comb in range(128):
                jb2, call = comb // 64, comb % 64
                bb2, qq2, cq2 = call // 32, (call % 32) // 8, call % 8
                s2 = bb2 * 4 + qq2
                for ch in range(128):
                    uidx[comb, ch] = (128 * s2 + ch) * 128 + r * 16 + cq2 * 2 + jb2
            ussidx = np.zeros((128, 8), dtype=np.uint32)
            for p2 in range(128):
                for s2 in range(8):
                    ussidx[p2, s2] = (128 * s2 + p2) * 8 + r
            yidx = np.zeros((128, 128), dtype=np.uint32)
            for chl in range(128):
                for g2 in range(128):
                    cb2, t2 = g2 // 16, g2 % 16
                    yidx[chl, g2] = (cb2 * 128 + chl) * 128 + (b * 64 + qq * 16 + t2)
            ldtf = np.asarray(inputs["L0_attn_b_log_dt"], dtype=np.float32)
            aref = np.asarray(inputs["L0_attn_b_A_real"], dtype=np.float32)
            aimf = np.asarray(inputs["L0_attn_b_A_imag"], dtype=np.float32)
            cref = np.asarray(inputs["L0_attn_b_C_real"], dtype=np.float32)
            cimf = np.asarray(inputs["L0_attn_b_C_imag"], dtype=np.float32)
            ddf = float(np.asarray(inputs["L0_attn_b_D"], dtype=np.float32))
            in_maps[-1].update({
                "ldt": np.ascontiguousarray(ldtf[ch0:ch0 + 128, None]),
                "are": np.ascontiguousarray(aref[ch0:ch0 + 128]),
                "aim": np.ascontiguousarray(aimf[ch0:ch0 + 128]),
                "cre": np.ascontiguousarray(cref[ch0:ch0 + 128]),
                "cim": np.ascontiguousarray(cimf[ch0:ch0 + 128]),
                "ddb": np.full((128, 1), ddf, dtype=np.float32),
                "uidx": uidx, "ussidx": ussidx, "yidx": yidx,
            })
    res = run_bass_kernel_spmd(nc, in_maps, core_ids=list(range(8)))
    out = np.empty((2, 8192, D), dtype=np.float32)
    for r in range(8):
        b, qq = r // 4, r % 4
        out[b, qq * TOK:(qq + 1) * TOK, :] = np.asarray(res.results[r]["xout"])
    if DEBUG:
        kernel.dbg = [res.results[r] for r in range(8)]
    return out


# version 11
# speedup vs baseline: 1.0140x; 1.0033x over previous
import numpy as np
import concourse.bass as bass
import concourse.mybir as mybir
from concourse.bass_utils import run_bass_kernel_spmd
from concourse.ap import AP
from concourse.bass_core_rust import IndirectOffsetOnAxis
from contextlib import ExitStack

F32 = mybir.dt.float32
BF16 = mybir.dt.bfloat16
U32 = mybir.dt.uint32
AF = mybir.ActivationFunctionType
ALU = mybir.AluOpType
AX = mybir.AxisListType

P = 128
NT = 16
D = 1024
KC = 8
H = 16
DH = 64
M = 256
E = 8
TOK = 2048
STAGE = 4
DEBUG = 0
TRACE = None


def build(stage):
    nc = bass.Bass("TRN2", target_bir_lowering=False)

    xin = nc.dram_tensor("xin", [TOK, D], F32, kind="ExternalInput").ap()
    wq_d = nc.dram_tensor("wq", [D, D], F32, kind="ExternalInput").ap()
    wk_d = nc.dram_tensor("wk", [D, D], F32, kind="ExternalInput").ap()
    wv_d = nc.dram_tensor("wv", [D, D], F32, kind="ExternalInput").ap()
    om_d = nc.dram_tensor("om", [M, DH], F32, kind="ExternalInput").ap()
    xout = nc.dram_tensor("xout", [TOK, D], F32, kind="ExternalOutput").ap()
    dbg = {}
    if DEBUG:
        for nm, shp, dt in [("d_xnf", [P, KC * TOK], BF16), ("d_q", [P, KC * TOK], BF16),
                            ("d_v", [P, NT * D], BF16), ("d_kvl", [P, H * M], BF16),
                            ("d_nq", [P, H * NT], F32), ("d_nk", [P, H * NT], F32),
                            ("d_sqe", [P, H * NT], F32)]:
            dbg[nm] = nc.dram_tensor(nm, shp, dt, kind="ExternalOutput").ap()
        if stage >= 3:
            for nm, shp, dt in [("d_k", [P, 384], BF16), ("d_p", [P, 4096], F32),
                                ("d_s", [P, 4096], BF16), ("d_y", [P, NT * D], BF16),
                                ("d_yf", [P, NT * D], BF16),
                                ("d_uss", [P, NT * D], BF16),
                                ("d_w", [P * 64, 256], BF16)]:
                dbg[nm] = nc.dram_tensor(nm, shp, dt, kind="ExternalOutput").ap()
    kvloc_d = nc.dram_tensor("kvloc_d", [65, H * M], BF16, kind="Internal").ap()
    lnb_d = nc.dram_tensor("lnb_d", [P, 2 * D], BF16, kind="Internal").ap()
    cb_d = nc.dram_tensor("cb_d", [P, 256], F32, kind="Internal").ap()
    ozb_d = nc.dram_tensor("ozb_d", [P, 1040], F32, kind="Internal").ap()
    kvagg_d = nc.dram_tensor("kvagg_d", [65, H * M], BF16, kind="Internal").ap()
    if stage >= 2:
        wg_d = nc.dram_tensor("wg", [D, D], F32, kind="ExternalInput").ap()
        bg_d = nc.dram_tensor("bg", [D], F32, kind="ExternalInput").ap()
        we_d = nc.dram_tensor("we", [E * D, D], F32, kind="ExternalInput").ap()
        be_d = nc.dram_tensor("be", [E, D], F32, kind="ExternalInput").ap()
    if stage >= 3:
        ldt_d = nc.dram_tensor("ldt", [P, 1], F32, kind="ExternalInput").ap()
        are_d = nc.dram_tensor("are", [P, 32], F32, kind="ExternalInput").ap()
        aim_d = nc.dram_tensor("aim", [P, 32], F32, kind="ExternalInput").ap()
        cre_d = nc.dram_tensor("cre", [P, 32], F32, kind="ExternalInput").ap()
        cim_d = nc.dram_tensor("cim", [P, 32], F32, kind="ExternalInput").ap()
        ddb_d = nc.dram_tensor("ddb", [P, 1], F32, kind="ExternalInput").ap()
        uidx_d = nc.dram_tensor("uidx", [P, P], U32, kind="ExternalInput").ap()
        ussidx_d = nc.dram_tensor("ussidx", [P, 8], U32, kind="ExternalInput").ap()
        yidx_d = nc.dram_tensor("yidx", [P, P], U32, kind="ExternalInput").ap()
        xg_loc = nc.dram_tensor("xg_loc", [P, NT * D], BF16, kind="Internal").ap()
        xg_all_t = nc.dram_tensor("xg_all", [8 * P, NT * D], BF16, kind="Internal")
        xg_all = xg_all_t.ap()
        xg2k = xg_all.rearrange("a (b w) -> (a b) w", w=TOK)
        xg128 = xg_all.rearrange("a (b w) -> (a b) w", w=P)
        k_dram = nc.dram_tensor("k_dram", [P, 384], BF16, kind="Internal").ap()
        w_dram = nc.dram_tensor("w_dram", [P * 64, 256], BF16, kind="Internal").ap()
        pp_dram = nc.dram_tensor("pp_dram", [P * 256, 64], BF16, kind="Internal").ap()
        p_dram = nc.dram_tensor("p_dram", [P, 4096], F32, kind="Internal").ap()
        s_dram = nc.dram_tensor("s_dram", [P, 4096], BF16, kind="Internal").ap()
        y_dram = nc.dram_tensor("y_dram", [P, NT * D], BF16, kind="Internal").ap()
        yfin_dram = nc.dram_tensor("yfin", [P, NT * D], BF16, kind="Internal").ap()
        yg_all_t = nc.dram_tensor("yg_all", [8 * P, NT * D], BF16, kind="Internal")
        yg_all = yg_all_t.ap()
        yg128 = yg_all.rearrange("a (b w) -> (a b) w", w=P)

    ctx = ExitStack()
    with ctx:
        sem_names = ["qS", "qG", "cT", "cV", "cS", "cP", "car", "tA", "tB", "tC"]
        SM = {n: ctx.enter_context(nc.semaphore(n)) for n in sem_names}
        C = {n: 0 for n in sem_names}

        def inc(n, instr, k=1):
            if TRACE is not None:
                TRACE.append(("inc", str(instr.ins.engine), n, k,
                              getattr(instr.ins, "name", "?")))
            instr.then_inc(SM[n], k)
            C[n] += k
            return instr

        def w(eng, n, v):
            if TRACE is not None:
                TRACE.append(("wait", str(eng.engine), n, v, "w"))
            if v > 0:
                eng.wait_ge(SM[n], v)

        def sb(name, shape, dt):
            return ctx.enter_context(nc.sbuf_tensor(name, shape, dt))

        X = sb("X", [P, NT * D], F32)
        XNF = sb("XNF", [P, KC * TOK], BF16)
        Q = sb("Q", [P, KC * TOK], BF16)
        V = sb("V", [P, NT * D], BF16)
        AR1 = sb("AR1", [P, 8192], BF16)
        KVL = sb("KVL", [P, H * M], BF16)
        SCR2 = sb("SCR2", [P, TOK], BF16)
        SQF = sb("SQF", [P, TOK], F32)
        OZB = sb("OZB", [P, 16 * 65], F32)
        NQ = sb("NQ", [P, H * NT], F32)
        NK = sb("NK", [P, H * NT], F32)
        SQE = sb("SQE", [P, H * NT], F32)
        WSTJ = sb("WSTJ", [P, 2 * P], F32)
        ONESB = sb("ONESB", [P, P], BF16)
        IDENT = sb("IDENT", [P, P], BF16)
        IDENTF = sb("IDENTF", [P, P], F32)
        OMT2 = sb("OMT2", [P, M], BF16)
        OM32 = sb("OM32", [P, 2 * DH], F32)
        OMB = sb("OMB", [P, 2 * DH], BF16)
        BNB = sb("BNB", [P, 12], F32)
        MVB = sb("MVB", [P, 4], F32)
        MVB2 = sb("MVB2", [P, 2], F32)
        RSTD = sb("RSTD", [P, 4], F32)
        ZT = sb("ZT", [P, 4], F32)
        EPSC = sb("EPSC", [P, 1], F32)
        ZD = sb("ZD", [P, 4 * NT], F32)
        BEBT = sb("BEBT", [P, E * KC], F32)
        IDXB = sb("IDXB", [P, 8], U32)
        UIDXS = sb("UIDXS", [P, P], U32)
        USIDX = sb("USIDX", [P, 8], U32)

        KBUF = [AR1[:, 0:TOK], AR1[:, TOK:2 * TOK]]
        KVF = AR1[:, 0:H * 2 * 65]
        PQB = [AR1[:, 2 * TOK:3 * TOK], AR1[:, 3 * TOK:4 * TOK]]
        WBV = AR1[:, :]
        WBJ = [AR1[:, 4096:4096 + D], AR1[:, 5120:5120 + D]]
        PKALL = AR1[:, 6144:6144 + 8 * M]
        XNT = [SCR2[:, 0:D], SCR2[:, D:2 * D]]
        SQH = SCR2[:, :]

        block = ctx.enter_context(nc.Block())

        PT0 = ctx.enter_context(nc.psum_tensor("PT0", [P, P], BF16))
        PT1 = ctx.enter_context(nc.psum_tensor("PT1", [P, P], BF16))
        PA0 = ctx.enter_context(nc.psum_tensor("PA0", [P, 512], F32))
        PA1 = ctx.enter_context(nc.psum_tensor("PA1", [P, 512], F32))
        PK0 = ctx.enter_context(nc.psum_tensor("PK0", [P, M], F32))
        PK1 = ctx.enter_context(nc.psum_tensor("PK1", [P, M], F32))
        PKV = ctx.enter_context(nc.psum_tensor("PKV", [64, M], F32))
        PPS = ctx.enter_context(nc.psum_tensor("PPS", [1, M], F32))
        PA = [PA0, PA1]
        PT = [PT0, PT1]
        PH4 = [PK0, PK1, PA0, PA1]

        def dummy_mm(tn):
            return tn.matmul(PA0[0:1, 0:1], ONESB[0:1, 0:1], ONESB[0:1, 0:1],
                             start=True, stop=True)

        def dummy_ps(tn):
            return tn.matmul(PPS[0:1, 0:1], ONESB[0:1, 0:1], ONESB[0:1, 0:1],
                             start=True, stop=True)

        def dummy_tr(tn):
            return tn.transpose(PT0[0:1, 0:1], ONESB[0:1, 0:1], IDENT[0:1, 0:1])

        @block.sync
        def _(sync):
            xt = xin.rearrange("(t p) d -> t p d", p=P)
            for t in range(NT):
                w(sync, "cP", 1)
                inc("qS", sync.dma_start(out=X[:, t * D:(t + 1) * D], in_=xt[t]), 16)
            omr = om_d.rearrange("(c p) d -> c p d", p=P)
            for c in range(2):
                w(sync, "cP", 1)
                inc("qS", sync.dma_start(out=OM32[:, c * DH:(c + 1) * DH], in_=omr[c]), 16)
        qs_x = [16 * (NT + 2)] * NT
        qs_om = C["qS"]

        @block.gpsimd
        def _(g):
            g.memset(KVL[64:P, :], 0.0)
            g.memset(ZT[:, :], 0.0)
            g.memset(EPSC[:, :], 1e-5)
            inc("cP", g.memset(ONESB[:, :], 1.0))
            w(g, "cP", 1)
            inc("cP", g.affine_select(IDENT[:, :], ONESB[:, :], pattern=[[-1, P]],
                                      compare_op=ALU.is_equal, fill=0.0, base=0,
                                      channel_multiplier=1))
            w(g, "cP", 2)
            inc("cP", g.tensor_copy(IDENTF[:, :], IDENT[:, :]))
            w(g, "qS", qs_om)
            inc("cP", g.tensor_copy(OMB[:, :], OM32[:, :]))
        cp_const = C["cP"]

        @block.tensor
        def _(tn):
            w(tn, "cP", cp_const)
            inc("cT", tn.transpose(PT0[0:64, :], OMB[:, 0:DH], IDENT[:, :]))
            inc("cT", tn.transpose(PT1[0:64, :], OMB[:, DH:2 * DH], IDENT[:, :]))
            inc("cT", dummy_mm(tn))
        ct_om = C["cT"]

        @block.vector
        def _(v):
            w(v, "cT", ct_om - 1)
            v.tensor_copy(OMT2[0:64, 0:P], PT0[0:64, :])
            w(v, "cT", ct_om)
            inc("cV", v.tensor_copy(OMT2[0:64, P:2 * P], PT1[0:64, :]))
        cv_omt = C["cV"]

        @block.gpsimd
        def _(g):
            w(g, "cV", cv_omt)
            inc("qG", g.dma_start(out=OMT2[64:P, :], in_=OMT2[0:64, :]), 16)
        qg_omt2 = C["qG"]

        def emit_ln_to_fm(perm=None):
            vb, sbb, tb = C["tA"], C["tB"], C["tC"]
            ng = NT * KC

            def rstd_ms2(t):
                return sbb + (2 * t + 2 if t <= 1 else 10 * t - 6)

            def ev_ms2(t2, c2):
                return sbb + (10 * t2 + 5 + c2 if t2 <= 14 else 153 + c2)

            pL = C["cP"]

            @block.gpsimd
            def _(g):
                for t in range(NT):
                    w(g, "tA", vb + 3 * t + 3)
                    sl = (t % 2) * D
                    inc("qG", g.dma_start(out=lnb_d[:, sl:sl + D],
                                          in_=XNT[t % 2][:, :]), 16)
                    w(g, "qG", C["qG"])
                    inc("qG", g.dma_start(out=XNT[t % 2][:, :],
                                          in_=lnb_d[:, sl:sl + D]), 16)
                    w(g, "qG", C["qG"])
                    inc("cP", g.tensor_copy(ZT[:, 1:2], ZT[:, 0:1]))

            @block.tensor
            def _(tn):
                w(tn, "cV", cv_omt)
                for t in range(NT):
                    w(tn, "cP", pL + t + 1)
                    for c in range(KC):
                        gidx = t * 8 + c
                        if gidx >= 2:
                            t2, c2 = (gidx - 2) // 8, (gidx - 2) % 8
                            w(tn, "tB", ev_ms2(t2, c2))
                        inc("tC", tn.transpose(
                            PT[gidx % 2][:, :],
                            XNT[t % 2][:, c * P:(c + 1) * P],
                            IDENT[:, :] if perm is None else perm))
                w(tn, "tB", ev_ms2(15, 6))
                inc("tC", dummy_tr(tn))

            @block.vector
            def _(v):
                for t in range(NT):
                    w(v, "qS", qs_x[t])
                    if t >= 2:
                        w(v, "tB", rstd_ms2(t - 2))
                    v.bn_stats(BNB[:, 0:6], X[:, t * D:t * D + 512])
                    inc("tA", v.bn_stats(BNB[:, 6:12], X[:, t * D + 512:(t + 1) * D]))
                    w(v, "tA", vb + 3 * t + 1)
                    inc("tA", v.bn_aggr(MVB[:, 2 * (t % 2):2 * (t % 2) + 2], BNB[:, :]))
                    w(v, "tA", vb + 3 * t + 2)
                    w(v, "tB", rstd_ms2(t))
                    if t >= 2:
                        w(v, "tC", tb + 8 * (t - 2) + 8)
                    inc("tA", v.tensor_scalar(
                        XNT[t % 2], X[:, t * D:(t + 1) * D],
                        MVB[:, 2 * (t % 2):2 * (t % 2) + 1],
                        RSTD[:, t % 2:t % 2 + 1],
                        ALU.subtract, ALU.mult))

            def rstd_ms(t):
                return sbb + (2 * t + 2 if t <= 1 else 10 * t - 6)

            def ev_ms(t2, c2):
                return sbb + (10 * t2 + 5 + c2 if t2 <= 14 else 153 + c2)

            @block.scalar
            def _(s):
                w(s, "cP", cp_const)

                def lnexp(t):
                    w(s, "tA", vb + 3 * t + 2)
                    mvv = MVB[:, 2 * (t % 2) + 1:2 * (t % 2) + 2]
                    inc("tB", s.activation(RSTD[:, t % 2:t % 2 + 1], mvv,
                                           AF.Ln, bias=EPSC[:, 0:1]))
                    w(s, "tB", rstd_ms(t) - 1)
                    inc("tB", s.activation(RSTD[:, t % 2:t % 2 + 1],
                                           RSTD[:, t % 2:t % 2 + 1],
                                           AF.Exp, scale=-0.5))

                lnexp(0)
                lnexp(1)
                for t in range(NT):
                    for c in range(KC):
                        gidx = t * 8 + c
                        w(s, "tC", tb + min(gidx + 2, ng + 1))
                        inc("tB", s.activation(
                            XNF[:, c * TOK + t * P:c * TOK + (t + 1) * P],
                            PT[gidx % 2][:, :], AF.Copy))
                    if t + 2 < NT:
                        lnexp(t + 2)

            assert C["tA"] == vb + 3 * NT
            assert C["tB"] == sbb + 10 * NT
            assert C["tC"] == tb + ng + 1
            return C["tB"]

        def emit_col_weights(wdram, ready_sig, mmdone):
            qb, pb = C["qS"], C["cP"]

            @block.sync
            def _(sync):
                for jc in range(KC):
                    for kc in range(KC):
                        gi = jc * KC + kc
                        if gi >= 2:
                            w(sync, "cP", pb + gi - 1)
                        else:
                            w(sync, "cP", pb)
                        src = AP(wdram.tensor, (kc * P) * D + jc * P,
                                 [[D, P], [1, P]])
                        inc("qS", sync.dma_start(
                            out=WSTJ[:, (gi % 2) * P:(gi % 2 + 1) * P], in_=src), 16)

            @block.gpsimd
            def _(g):
                if ready_sig is not None:
                    w(g, ready_sig[0], ready_sig[1])
                for jc in range(KC):
                    if mmdone is not None and jc >= 2:
                        w(g, "cT", mmdone(jc - 2))
                    for kc in range(KC):
                        gi = jc * KC + kc
                        w(g, "qS", qb + 16 * max(gi + 1, 2))
                        inc("cP", g.tensor_copy(
                            WBJ[jc % 2][:, kc * P:(kc + 1) * P],
                            WSTJ[:, (gi % 2) * P:(gi % 2 + 1) * P]))
            assert C["cP"] == pb + 64
            return pb

        def emit_attn(xnf_ready_tB):
            qb0, pb0 = C["qS"], C["cP"]

            @block.sync
            def _(sync):
                for gi in range(KC * KC):
                    if gi >= 2:
                        w(sync, "cP", pb0 + gi - 1)
                    else:
                        w(sync, "cP", cp_const)
                    kc, jj = gi // KC, gi % KC
                    src = AP(wv_d.tensor, (kc * P) * D + jj * P, [[D, P], [1, P]])
                    inc("qS", sync.dma_start(
                        out=WSTJ[:, (gi % 2) * P:(gi % 2 + 1) * P], in_=src), 16)

            @block.gpsimd
            def _(g):
                for gi in range(KC * KC):
                    kc, jj = gi // KC, gi % KC
                    w(g, "qS", qb0 + 16 * max(gi + 1, 2))
                    inc("cP", g.tensor_copy(
                        WBV[:, kc * D + jj * P:kc * D + (jj + 1) * P],
                        WSTJ[:, (gi % 2) * P:(gi % 2 + 1) * P]))
            wv_ready = C["cP"]

            tb0, sb0 = C["cT"], C["cS"]
            ngv = NT * 2

            @block.tensor
            def _(tn):
                w(tn, "cP", wv_ready)
                w(tn, "tB", xnf_ready_tB)
                for gidx in range(ngv):
                    t, nt = gidx // 2, gidx % 2
                    if gidx >= 2:
                        w(tn, "cS", sb0 + gidx - 1)
                    pa = PA[gidx % 2]
                    for kc in range(KC):
                        mm = tn.matmul(
                            pa[:, :],
                            XNF[:, kc * TOK + t * P:kc * TOK + (t + 1) * P],
                            WBV[:, kc * D + nt * 512:kc * D + (nt + 1) * 512],
                            start=(kc == 0), stop=(kc == KC - 1))
                    inc("cT", mm)
                w(tn, "cS", sb0 + 31)
                inc("cT", dummy_mm(tn))

            @block.scalar
            def _(s):
                for gidx in range(ngv):
                    t, nt = gidx // 2, gidx % 2
                    w(s, "cT", tb0 + min(gidx + 2, ngv + 1))
                    inc("cS", s.activation(
                        V[:, t * D + nt * 512:t * D + (nt + 1) * 512],
                        PA[gidx % 2][:, :], AF.Copy))
            v_done_cS = C["cS"]
            assert C["cT"] == tb0 + ngv + 1

            tb1 = C["cT"]
            wq_pb = emit_col_weights(wq_d, ("cT", tb1),
                                     mmdone=lambda jc: tb1 + 38 * jc + 4)
            sb1, vb1 = C["cS"], C["cV"]

            def q_drain(jc, g):
                if g < 32 - 1:
                    return tb1 + 38 * jc + 5 + g + 2
                return tb1 + 38 * jc + 38

            @block.tensor
            def _(tn):
                w(tn, "cS", v_done_cS)
                for jc in range(KC):
                    TJ = tb1 + 38 * jc
                    SJ = sb1 + 8 * jc
                    VJ = vb1 + 32 * jc
                    w(tn, "cP", wq_pb + 8 * (jc + 1))
                    w(tn, "cS", SJ)
                    w(tn, "cV", VJ)
                    for nt in range(4):
                        gidx = jc * 4 + nt
                        if gidx >= 2:
                            g2 = gidx - 2
                            w(tn, "cS", sb1 + 8 * (g2 // 4) + 2 * (g2 % 4) + 2)
                        pa = PA[nt % 2]
                        for kc in range(KC):
                            mm = tn.matmul(
                                pa[:, :],
                                WBJ[jc % 2][:, kc * P:(kc + 1) * P],
                                XNF[:, kc * TOK + nt * 512:kc * TOK + (nt + 1) * 512],
                                start=(kc == 0), stop=(kc == KC - 1))
                        inc("cT", mm)
                    inc("cT", dummy_tr(tn))
                    w(tn, "cS", SJ + 8)
                    for g in range(32):
                        hh, t = g // NT, g % NT
                        if g >= 4:
                            w(tn, "cV", VJ + g - 3)
                        elif g >= 2:
                            w(tn, "cS", SJ + 2 * g + 2)
                        inc("cT", tn.matmul(
                            PH4[g % 4][:, 0:1],
                            SQH[64 * hh:64 * hh + 64, t * P:(t + 1) * P],
                            ONESB[64 * hh:64 * hh + 64, 0:1],
                            start=True, stop=True))
                    inc("cT", dummy_tr(tn))

            @block.scalar
            def _(s):
                for jc in range(KC):
                    TJ = tb1 + 38 * jc
                    SJ = sb1 + 8 * jc
                    for nt in range(4):
                        gidx = jc * 4 + nt
                        if nt < 3:
                            w(s, "cT", TJ + nt + 2)
                        else:
                            w(s, "cT", TJ + 5)
                        if jc > 0 and nt == 0:
                            w(s, "cT", TJ)
                        pa = PA[nt % 2]
                        inc("cS", s.activation(
                            Q[:, jc * TOK + nt * 512:jc * TOK + (nt + 1) * 512],
                            pa[:, :], AF.Copy))
                        inc("cS", s.activation(
                            SQH[:, nt * 512:(nt + 1) * 512], pa[:, :], AF.Square))

            @block.vector
            def _(v):
                for jc in range(KC):
                    TJ = tb1 + 38 * jc
                    VJ = vb1 + 32 * jc
                    for g in range(32):
                        hh, t = g // NT, g % NT
                        h = 2 * jc + hh
                        w(v, "cT", q_drain(jc, g))
                        inc("cV", v.tensor_scalar(
                            NQ[:, h * NT + t:h * NT + t + 1],
                            PH4[g % 4][:, 0:1], -0.5, None, ALU.mult))

            assert C["cT"] == tb1 + 38 * KC
            assert C["cS"] == sb1 + 8 * KC
            assert C["cV"] == vb1 + 32 * KC
            nq_done_cV = C["cV"]

            sqr = C["cP"]

            @block.gpsimd
            def _(g):
                w(g, "cV", nq_done_cV)
                inc("qG", g.dma_start(out=cb_d[:, :], in_=NQ[:, :]), 16)
                w(g, "qG", C["qG"])
                inc("qG", g.dma_start(out=NQ[:, :], in_=cb_d[:, :]), 16)
                w(g, "qG", C["qG"])
                inc("cP", g.tensor_copy(ZT[:, 1:2], ZT[:, 0:1]))

            @block.scalar
            def _(s):
                w(s, "cP", sqr + 1)
                inc("cS", s.activation(SQE[:, :], NQ[:, :], AF.Exp))
            sqe_mid_cS = C["cS"]

            @block.gpsimd
            def _(g):
                w(g, "cS", sqe_mid_cS)
                inc("qG", g.dma_start(out=cb_d[:, :], in_=SQE[:, :]), 16)
                w(g, "qG", C["qG"])
                inc("qG", g.dma_start(out=SQE[:, :], in_=cb_d[:, :]), 16)
                w(g, "qG", C["qG"])

            @block.scalar
            def _(s):
                w(s, "qG", C["qG"])
                inc("cS", s.activation(ZT[0:1, 2:3], ZT[0:1, 0:1], AF.Copy))
            sqe_done_cS = C["cS"]

            tb2 = C["cT"]
            sb2, vb2 = C["cS"], C["cV"]
            qbK, pbK = C["qS"], C["cP"]
            if TRACE is not None:
                print("BASES tb2=%d sb2=%d vb2=%d qbK=%d pbK=%d" % (tb2, sb2, vb2, qbK, pbK))
            def k_norm_drain(jc, g):
                if g < 32 - 1:
                    return tb2 + 97 * jc + 5 + g + 2
                return tb2 + 97 * jc + 37 + 1

            OBASE = [0, 6, 12, 21, 27, 36, 42, 51]
            KVPOS = [21, 36, 51, 60]

            def phi_drain(jc, hf, oct_i, i):
                base = tb2 + 97 * jc + 37 + OBASE[2 * hf + oct_i]
                if i < 3:
                    return base + i + 2
                return base + 6

            def kv_drain(jc, hf):
                return tb2 + 97 * jc + 37 + KVPOS[hf]

            def cast_pos(gi):
                jc = gi // KC
                relays_before = 9 * max(0, jc - 1) if jc >= 2 else 0
                return pbK + gi + 1 + relays_before

            def nk_relay_pos(jc2):
                casts_before = 8 * min(KC, jc2 + 2)
                return pbK + casts_before + 9 * jc2 + 1

            def relay_pos(jc2, oct_i):
                casts_before = 8 * min(KC, jc2 + 2)
                return pbK + casts_before + 9 * jc2 + 1 + oct_i + 1

            @block.sync
            def _(sync):
                for jc in range(KC):
                    for kc in range(KC):
                        gi = jc * KC + kc
                        if gi >= 2:
                            w(sync, "cP", cast_pos(gi - 2))
                        else:
                            w(sync, "cP", pbK)
                        src = AP(wk_d.tensor, (kc * P) * D + jc * P,
                                 [[D, P], [1, P]])
                        inc("qS", sync.dma_start(
                            out=WSTJ[:, (gi % 2) * P:(gi % 2 + 1) * P], in_=src), 16)

            @block.gpsimd
            def _(g):
                w(g, "cT", tb2)

                def dma_hop():
                    g.tensor_copy(ZT[:, 1:2], ZT[:, 0:1])

                def emit_relay(jc2):
                    w(g, "cV", vb2 + 41 * jc2 + 33)
                    nkc = NK[:, 32 * jc2:32 * jc2 + 32]
                    inc("qG", g.dma_start(out=cb_d[:, 0:32], in_=nkc), 16)
                    w(g, "qG", C["qG"])
                    inc("qG", g.dma_start(out=nkc, in_=cb_d[:, 0:32]), 16)
                    w(g, "qG", C["qG"])
                    inc("cP", g.tensor_copy(ZT[:, 1:2], ZT[:, 0:1]))
                    for oct_i in range(8):
                        w(g, "cS", sb2 + 40 * jc2 + 8 + oct_i * 4 + 4)
                        half = (oct_i % 2) * 1024
                        inc("qG", g.dma_start(
                            out=lnb_d[:, 0:1024],
                            in_=PKALL[:, half:half + 1024]), 16)
                        w(g, "qG", C["qG"])
                        inc("qG", g.dma_start(
                            out=PKALL[:, half:half + 1024],
                            in_=lnb_d[:, 0:1024]), 16)
                        w(g, "qG", C["qG"])
                        inc("cP", g.tensor_copy(ZT[:, 1:2], ZT[:, 0:1]))

                for jc in range(KC):
                    if jc >= 2:
                        w(g, "cT", tb2 + 97 * (jc - 2) + 4)
                        emit_relay(jc - 2)
                    for kc in range(KC):
                        gi = jc * KC + kc
                        w(g, "qS", qbK + 16 * max(gi + 1, 2))
                        inc("cP", g.tensor_copy(
                            WBJ[jc % 2][:, kc * P:(kc + 1) * P],
                            WSTJ[:, (gi % 2) * P:(gi % 2 + 1) * P]))
                emit_relay(KC - 2)
                emit_relay(KC - 1)

            @block.tensor
            def _(tn):
                w(tn, "cV", nq_done_cV)
                w(tn, "qG", qg_omt2)
                for jc in range(KC):
                    TJ = tb2 + 97 * jc
                    SJ = sb2 + 40 * jc
                    VJ = vb2 + 41 * jc
                    w(tn, "cP", cast_pos(jc * KC + KC - 1))
                    w(tn, "cS", SJ)
                    w(tn, "cV", VJ)
                    for nt in range(4):
                        gidx = jc * 4 + nt
                        if gidx >= 2:
                            g2 = gidx - 2
                            w(tn, "cS", sb2 + 40 * (g2 // 4) + 2 * (g2 % 4) + 2)
                        pa = PA[nt % 2]
                        for kc in range(KC):
                            mm = tn.matmul(
                                pa[:, :],
                                WBJ[jc % 2][:, kc * P:(kc + 1) * P],
                                XNF[:, kc * TOK + nt * 512:kc * TOK + (nt + 1) * 512],
                                start=(kc == 0), stop=(kc == KC - 1))
                        inc("cT", mm)
                    inc("cT", dummy_tr(tn))
                    w(tn, "cS", SJ + 8)
                    for g in range(32):
                        hh, t = g // NT, g % NT
                        if g >= 4:
                            w(tn, "cV", VJ + g - 3)
                        elif g >= 2:
                            w(tn, "cS", SJ + 2 * g + 2)
                        inc("cT", tn.matmul(
                            PH4[g % 4][:, 0:1],
                            SQH[64 * hh:64 * hh + 64, t * P:(t + 1) * P],
                            ONESB[64 * hh:64 * hh + 64, 0:1],
                            start=True, stop=True))
                    def emit_oct(o):
                        hf, oct_i = o // 2, o % 2
                        hh = hf // 2
                        half = hf % 2
                        for i in range(4):
                            tt = oct_i * 4 + i
                            t = half * 8 + tt
                            if o == 0:
                                w(tn, "cV", VJ + 29 + i)
                            else:
                                w(tn, "cP", relay_pos(jc, o - 1))
                            inc("cT", tn.matmul(
                                PH4[i][:, 0:M],
                                KBUF[jc % 2][64 * hh:64 * hh + 64,
                                             t * P:(t + 1) * P],
                                OMT2[64 * hh:64 * hh + 64, :],
                                start=True, stop=True))
                        inc("cT", dummy_tr(tn))
                        inc("cT", dummy_tr(tn))

                    def emit_kv(hfk):
                        hh = hfk // 2
                        h = 2 * jc + hh
                        half = hfk % 2
                        w(tn, "cP", relay_pos(jc, 2 * hfk + 1))
                        if hfk >= 1:
                            w(tn, "cV", VJ + 33 + 2 * hfk)
                        for tt in range(8):
                            t = half * 8 + tt
                            tn.matmul(PKV[:, :],
                                      V[:, t * D + h * DH:t * D + (h + 1) * DH],
                                      PKALL[:, tt * M:(tt + 1) * M],
                                      start=(tt == 0), stop=(tt == 7))
                        for tt in range(8):
                            mm = tn.matmul(PPS[:, :], ONESB[:, 0:1],
                                           PKALL[:, tt * M:(tt + 1) * M],
                                           start=(tt == 0), stop=(tt == 7))
                        inc("cT", mm)
                        inc("cT", dummy_tr(tn))
                        inc("cT", dummy_tr(tn))

                    emit_oct(0); emit_oct(1); emit_oct(2); emit_kv(0)
                    emit_oct(3); emit_oct(4); emit_kv(1)
                    emit_oct(5); emit_oct(6); emit_kv(2)
                    emit_oct(7); emit_kv(3)
                inc("cT", dummy_mm(tn))

            @block.scalar
            def _(s):
                for jc in range(KC):
                    TJ = tb2 + 97 * jc
                    SJ = sb2 + 40 * jc
                    VJ = vb2 + 41 * jc
                    for nt in range(4):
                        if nt < 3:
                            w(s, "cT", TJ + nt + 2)
                        else:
                            w(s, "cT", TJ + 5)
                        if jc > 0 and nt == 0:
                            w(s, "cT", TJ)
                        pa = PA[nt % 2]
                        inc("cS", s.activation(
                            KBUF[jc % 2][:, nt * 512:(nt + 1) * 512],
                            pa[:, :], AF.Copy))
                        inc("cS", s.activation(
                            SQH[:, nt * 512:(nt + 1) * 512], pa[:, :], AF.Square))
                    w(s, "cP", nk_relay_pos(jc))
                    for hf in range(4):
                        hh = hf // 2
                        h = 2 * jc + hh
                        half = hf % 2
                        for oct_i in range(2):
                            for i in range(4):
                                tt = oct_i * 4 + i
                                t = half * 8 + tt
                                w(s, "cT", phi_drain(jc, hf, oct_i, i))
                                if hf > 0 and oct_i == 0 and i == 0:
                                    w(s, "cT", kv_drain(jc, hf - 1))
                                inc("cS", s.activation(
                                    PKALL[:, tt * M:(tt + 1) * M],
                                    PH4[i][:, 0:M], AF.Exp,
                                    bias=NK[:, h * NT + t:h * NT + t + 1]))

            @block.vector
            def _(v):
                for jc in range(KC):
                    TJ = tb2 + 97 * jc
                    SJ = sb2 + 40 * jc
                    VJ = vb2 + 41 * jc
                    for g in range(32):
                        hh, t = g // NT, g % NT
                        h = 2 * jc + hh
                        w(v, "cT", k_norm_drain(jc, g))
                        inc("cV", v.tensor_scalar(
                            NK[:, h * NT + t:h * NT + t + 1],
                            PH4[g % 4][:, 0:1], -0.5, None, ALU.mult))
                    inc("cV", v.tensor_copy(ZT[:, 1:2], ZT[:, 0:1]))
                    for hf in range(4):
                        hh = hf // 2
                        h = 2 * jc + hh
                        w(v, "cT", kv_drain(jc, hf))
                        if hf % 2 == 0:
                            v.tensor_copy(KVL[0:64, h * M:(h + 1) * M], PKV[:, :])
                            inc("cV", v.tensor_copy(
                                KVL[64:65, h * M:(h + 1) * M], PPS[:, :]))
                        else:
                            v.tensor_tensor(KVL[0:64, h * M:(h + 1) * M],
                                            KVL[0:64, h * M:(h + 1) * M],
                                            PKV[:, :], ALU.add)
                            inc("cV", v.tensor_tensor(
                                KVL[64:65, h * M:(h + 1) * M],
                                KVL[64:65, h * M:(h + 1) * M],
                                PPS[:, :], ALU.add))
                        inc("cV", v.tensor_copy(ZT[:, 1:2], ZT[:, 0:1]))

            assert C["cT"] == tb2 + 97 * KC + 1
            assert C["cS"] == sb2 + 40 * KC
            assert C["cV"] == vb2 + 41 * KC
            kv_done_cV = C["cV"]

            qg0 = C["qG"]

            @block.gpsimd
            def _(g):
                w(g, "cV", kv_done_cV)
                inc("qG", g.dma_start(out=kvloc_d[:, :], in_=KVL[0:65, :]), 16)
                w(g, "qG", qg0 + 16)
                inc("car", g.collective_compute(
                    "AllReduce", ALU.add,
                    replica_groups=[[0, 1, 2, 3], [4, 5, 6, 7]],
                    ins=[kvloc_d], outs=[kvagg_d]))
                w(g, "car", 1)
                inc("qG", g.dma_start(out=KVL[0:65, :], in_=kvagg_d[:, :]), 16)
            kv_in_cQG = C["qG"]

            tb3, vb3 = C["cT"], C["cV"]

            @block.tensor
            def _(tn):
                w(tn, "qG", kv_in_cQG)
                for g in range(H * 2):
                    h, mh = g // 2, g % 2
                    if g >= 2:
                        w(tn, "cV", vb3 + g - 1)
                    inc("cT", tn.transpose(
                        PT[g % 2][:, :],
                        KVL[:, h * M + mh * P:h * M + (mh + 1) * P],
                        IDENT[:, :]))
                inc("cT", dummy_mm(tn))

            @block.vector
            def _(v):
                for g in range(H * 2):
                    h, mh = g // 2, g % 2
                    w(v, "cT", tb3 + min(g + 2, H * 2 + 1))
                    inc("cV", v.tensor_copy(
                        KVF[:, h * 130 + mh * 65:h * 130 + (mh + 1) * 65],
                        PT[g % 2][0:P, 0:65]))
            assert C["cT"] == tb3 + H * 2 + 1
            assert C["cV"] == vb3 + H * 2
            kvf_done_cV = C["cV"]

            @block.gpsimd
            def _(g):
                w(g, "cV", kvf_done_cV)
                kv_ap = AP(kvagg_d.tensor, 0, [[2080, P], [1, 2080]])
                inc("qG", g.dma_start(out=kv_ap, in_=KVF[:, :]), 16)
                w(g, "qG", C["qG"])
                inc("qG", g.dma_start(out=KVF[:, :], in_=kv_ap), 16)
                w(g, "qG", C["qG"])
                inc("cP", g.tensor_copy(ZT[:, 1:2], ZT[:, 0:1]))
            kvfb_cP = C["cP"]

            tb4, sb4, vb4 = C["cT"], C["cS"], C["cV"]
            pgB = C["cP"]

            @block.gpsimd
            def _(g):
                for h in range(H):
                    w(g, "cS", sb4 + 24 * h + 8)
                    for mh in range(2):
                        qp_ap = AP(kvagg_d.tensor, 0, [[2048, P], [1, 2048]])
                        inc("qG", g.dma_start(out=qp_ap, in_=PQB[mh][:, :]), 16)
                        w(g, "qG", C["qG"])
                        inc("qG", g.dma_start(out=PQB[mh][:, :], in_=qp_ap), 16)
                        w(g, "qG", C["qG"])
                    inc("cP", g.tensor_copy(ZT[:, 1:2], ZT[:, 0:1]))
                    w(g, "cS", sb4 + 24 * h + 24)
                    inc("qG", g.dma_start(out=ozb_d[:, :], in_=OZB[:, :]), 16)
                    w(g, "qG", C["qG"])
                    inc("qG", g.dma_start(out=OZB[:, :], in_=ozb_d[:, :]), 16)
                    w(g, "qG", C["qG"])
                    inc("cP", g.tensor_copy(ZT[:, 1:2], ZT[:, 0:1]))
            ozz = AP(OZB[:, :].tensor, 64, [[16 * 65, P], [65, NT]])
            PO4 = [PK0, PK1, PA0, PA1]

            @block.tensor
            def _(tn):
                w(tn, "cV", kvf_done_cV)
                w(tn, "cP", kvfb_cP)
                w(tn, "cS", sqe_done_cS)
                for h in range(H):
                    TH = tb4 + 28 * h
                    SH = sb4 + 24 * h
                    qrow = 64 * (h % 2)
                    qcol = TOK * (h // 2)
                    if h >= 1:
                        w(tn, "cS", SH)
                    for g in range(8):
                        mh, nt = g // 4, g % 4
                        if g >= 2:
                            w(tn, "cS", SH + g - 1)
                        pa = PA[g % 2]
                        inc("cT", tn.matmul(
                            pa[:, :],
                            OMT2[qrow:qrow + 64, mh * P:(mh + 1) * P],
                            Q[qrow:qrow + 64,
                              qcol + nt * 512:qcol + (nt + 1) * 512],
                            start=True, stop=True))
                    inc("cT", dummy_tr(tn))
                    inc("cT", dummy_tr(tn))
                    w(tn, "cS", SH + 8)
                    w(tn, "cP", pgB + 2 * h + 1)
                    for t in range(NT):
                        if t >= 4:
                            w(tn, "cS", SH + 8 + t - 3)
                        po = PO4[t % 4]
                        tn.matmul(po[:, 0:65],
                                  PQB[0][:, t * P:(t + 1) * P],
                                  KVF[:, h * 130:h * 130 + 65],
                                  start=True, stop=False)
                        inc("cT", tn.matmul(
                            po[:, 0:65],
                            PQB[1][:, t * P:(t + 1) * P],
                            KVF[:, h * 130 + 65:h * 130 + 130],
                            start=False, stop=True))
                    inc("cT", dummy_tr(tn))
                    inc("cT", dummy_tr(tn))

            @block.scalar
            def _(s):
                for h in range(H):
                    TH = tb4 + 28 * h
                    SH = sb4 + 24 * h
                    for g in range(8):
                        mh, nt = g // 4, g % 4
                        w(s, "cT", TH + (g + 2 if g < 7 else 10))
                        inc("cS", s.activation(
                            PQB[mh][:, nt * 512:(nt + 1) * 512],
                            PA[g % 2][:, :], AF.Exp))
                    for t in range(NT):
                        nxt = TH + 10 + t + 2
                        if t == NT - 1:
                            nxt = TH + 28
                        w(s, "cT", nxt)
                        if t == 0 and h >= 1:
                            w(s, "cV", vb4 + 5 * (h - 1) + 5)
                        inc("cS", s.activation(
                            OZB[:, t * 65:(t + 1) * 65], PO4[t % 4][:, 0:65],
                            AF.Copy))

            @block.vector
            def _(v):
                for h in range(H):
                    SH = sb4 + 24 * h
                    VH = vb4 + 5 * h
                    w(v, "cP", pgB + 2 * h + 2)
                    sqs = SQE[:, h * NT:(h + 1) * NT]
                    inc("cV", v.tensor_tensor(ZD[:, 0:NT], ozz, sqs, ALU.mult))
                    w(v, "cV", VH + 1)
                    inc("cV", v.tensor_scalar(ZD[:, NT:2 * NT], ZD[:, 0:NT],
                                              1e-6 * M, None, ALU.add))
                    w(v, "cV", VH + 2)
                    inc("cV", v.reciprocal(ZD[:, 2 * NT:3 * NT], ZD[:, NT:2 * NT]))
                    w(v, "cV", VH + 3)
                    inc("cV", v.tensor_tensor(ZD[:, 3 * NT:4 * NT],
                                              ZD[:, 2 * NT:3 * NT], sqs, ALU.mult))
                    w(v, "cV", VH + 4)
                    for t in range(NT):
                        ins = v.scalar_tensor_tensor(
                            X[:, t * D + h * DH:t * D + (h + 1) * DH],
                            OZB[:, t * 65:t * 65 + 64],
                            ZD[:, 3 * NT + t:3 * NT + t + 1],
                            X[:, t * D + h * DH:t * D + (h + 1) * DH],
                            ALU.mult, ALU.add)
                        if t == NT - 1:
                            inc("cV", ins)

            assert C["cT"] == tb4 + 28 * H
            assert C["cS"] == sb4 + 24 * H
            assert C["cV"] == vb4 + 5 * H
            return C["cV"]


        def emit_ffn(first):
            lnF = emit_ln_to_fm()
            sPrev, vPrev = C["cS"], C["cV"]
            qF, pF = C["qS"], C["cP"]
            WGB = Q[:, 0:KC * D]
            BGB = KVL[:, 0:D]
            LGF = SQF[:, 1024:2048]
            EXB = V[:, 4096:5120]
            WEL = V[:, 5120:6144]
            HEB = [V[:, 0:1024], V[:, 1024:2048]]
            NLM = NQ[:, 0:16]
            SCT = NQ[:, 16:32]
            PCOL = NQ[:, 32:48]
            EIDT = NQ[:, 48:64]
            MX8 = NK[:, 0:8]
            TMPC = NK[:, 8:9]
            MSK = SQE[:, 0:128]
            PH = [PA0, PA1, PK0, PK1]

            def wgcast(c):
                return pF + 2 + c

            rbase = pF + 1 + KC
            pE = rbase + 33

            def wecast(i):
                return pE + i + (i // 8) + 1

            def werelay(u):
                return pE + 9 * u + 9

            NU = 2 * E * KC

            def usrc(u):
                return (u // 8) % 8, u % 8, u // 64

            @block.sync
            def _(sync):
                w(sync, "cV", vPrev)
                inc("qS", sync.dma_start(out=SQF[0:1, 0:1024], in_=bg_d[:]), 16)
                beap = AP(be_d.tensor, 0, [[1, P], [D, E], [P, KC]])
                w(sync, "cV", vPrev)
                inc("qS", sync.dma_start(out=BEBT[:, :], in_=beap,
                                         allow_slow_non_contiguous=True), 16)
                for c in range(KC):
                    if c == 0:
                        w(sync, "cP", pF + 1)
                    elif c >= 2:
                        w(sync, "cP", wgcast(c - 2))
                    else:
                        w(sync, "cP", pF + 1)
                    src = AP(wg_d.tensor, (c * P) * D, [[D, P], [1, D]])
                    inc("qS", sync.dma_start(
                        out=SQF[:, (c % 2) * 1024:(c % 2) * 1024 + 1024],
                        in_=src), 16)
                for i in range(NU * KC):
                    u, c = i // 8, i % 8
                    e, fb, th = usrc(u)
                    if i >= 2:
                        w(sync, "cP", wecast(i - 2))
                    else:
                        w(sync, "cP", pF)
                    src = AP(we_d.tensor, (e * D + c * P) * D + fb * P,
                             [[D, P], [1, P]])
                    inc("qS", sync.dma_start(
                        out=WSTJ[:, (i % 2) * P:(i % 2 + 1) * P], in_=src), 16)

            tR, sR, vR = C["cT"], C["cS"], C["cV"]
            vE = vR + 4 * NT + 1

            @block.gpsimd
            def _(g):
                w(g, "qS", qF + 16)
                inc("cP", g.tensor_copy(BGB[0:1, :], SQF[0:1, 0:1024]))
                for c in range(KC):
                    w(g, "qS", qF + 48 + 16 * c)
                    inc("cP", g.tensor_copy(
                        WGB[:, c * D:(c + 1) * D],
                        SQF[:, (c % 2) * 1024:(c % 2) * 1024 + 1024]))
                for t in range(NT):
                    w(g, "cS", sR + 3 * t + 2)
                    inc("cP", g.tensor_copy(ZT[:, 1:2], ZT[:, 0:1]))
                    w(g, "cV", vR + 4 * t + 1)
                    inc("qG", g.dma_start(out=cb_d[:, 0:1],
                                          in_=NLM[:, t:t + 1],
                                          allow_slow_non_contiguous=True), 16)
                    w(g, "qG", C["qG"])
                    inc("qG", g.dma_start(out=NLM[:, t:t + 1],
                                          in_=cb_d[:, 0:1],
                                          allow_slow_non_contiguous=True), 16)
                    w(g, "qG", C["qG"])
                    inc("cP", g.tensor_copy(ZT[:, 1:2], ZT[:, 0:1]))
                w(g, "cS", sR + 3 * NT)
                inc("qG", g.dma_start(out=cb_d[:, 0:16], in_=SCT[:, :]), 16)
                w(g, "qG", C["qG"])
                inc("qG", g.dma_start(out=SCT[:, :], in_=cb_d[:, 0:16]), 16)
                w(g, "qG", C["qG"])
                inc("cP", g.tensor_copy(ZT[:, 1:2], ZT[:, 0:1]))
                for u in range(NU):
                    for c in range(KC):
                        i = u * KC + c
                        w(g, "qS", qF + 160 + 16 * max(i + 1, 2))
                        if u >= 1:
                            w(g, "cT", tR + 64 + 14 * (u - 1) + 2)
                        inc("cP", g.tensor_copy(
                            WEL[:, (i % 8) * P:(i % 8 + 1) * P],
                            WSTJ[:, (i % 2) * P:(i % 2 + 1) * P]))
                    w(g, "cS", sR + 3 * NT + 2 * u + 2)
                    inc("cP", g.tensor_copy(ZT[:, 1:2], ZT[:, 0:1]))

            tE = tR + 4 * NT

            @block.tensor
            def _(tn):
                w(tn, "cS", sPrev)
                w(tn, "cV", vPrev)
                w(tn, "tB", lnF)
                for t in range(NT):
                    for h2 in range(2):
                        if t == 0 and h2 == 0:
                            w(tn, "cP", wgcast(KC - 1))
                        if t >= 1:
                            w(tn, "cS", sR + 3 * (t - 1) + h2 + 1)
                        pa = PA[h2]
                        for c in range(KC):
                            tn.matmul(
                                pa[:, :],
                                XNF[:, c * TOK + t * P:c * TOK + (t + 1) * P],
                                WGB[:, c * D + h2 * 512:c * D + (h2 + 1) * 512],
                                start=(c == 0), stop=False)
                        inc("cT", tn.matmul(
                            pa[:, :], ONESB[0:1, 0:P],
                            BGB[0:1, h2 * 512:(h2 + 1) * 512],
                            start=False, stop=True))
                    inc("cT", dummy_ps(tn))
                    inc("cT", dummy_ps(tn))
                for u in range(NU):
                    e, fb, th = usrc(u)
                    TU = tE + 14 * u
                    for c in range(KC):
                        i = u * KC + c
                        w(tn, "cP", wecast(i))
                        if c == 0:
                            for tg in range(2):
                                if u == 0:
                                    w(tn, "cS", sR + 3 * NT)
                                else:
                                    w(tn, "cS", sR + 3 * NT + 2 * (u - 1) + tg + 1)
                        for tg in range(2):
                            mm = tn.matmul(
                                PA[tg][:, :],
                                WEL[:, (i % 8) * P:(i % 8 + 1) * P],
                                XNF[:, c * TOK + th * 1024 + tg * 512:
                                    c * TOK + th * 1024 + (tg + 1) * 512],
                                start=(c == 0), stop=(c == KC - 1))
                            if c == KC - 1:
                                inc("cT", mm)
                    inc("cT", dummy_ps(tn))
                    inc("cT", dummy_ps(tn))
                    w(tn, "cP", werelay(u))
                    for gg in range(KC):
                        if gg >= 2:
                            w(tn, "cV", vE + KC * u + gg - 1)
                        inc("cT", tn.transpose(
                            PT[gg % 2][:, :],
                            HEB[u % 2][:, gg * P:(gg + 1) * P], IDENT[:, :]))
                    inc("cT", dummy_ps(tn))
                    inc("cT", dummy_ps(tn))

            @block.scalar
            def _(s):
                for t in range(NT):
                    w(s, "cT", tR + 4 * t + 2)
                    inc("cS", s.activation(LGF[:, 0:512], PA0[:, :], AF.Copy))
                    w(s, "cT", tR + 4 * t + 4)
                    inc("cS", s.activation(LGF[:, 512:1024], PA1[:, :], AF.Copy))
                    w(s, "cP", rbase + 2 * t + 2)
                    inc("cS", s.activation(
                        EXB[:, :], LGF[:, :], AF.Exp,
                        bias=NLM[:, t:t + 1],
                        accum_out=SCT[:, t:t + 1]))
                for u in range(NU):
                    e, fb, th = usrc(u)
                    TU = tE + 14 * u
                    if u == 0:
                        w(s, "qS", qF + 32)
                    for tg in range(2):
                        if tg < 1:
                            w(s, "cT", TU + 2)
                        else:
                            w(s, "cT", TU + 4)
                        if u >= 2 and tg == 0:
                            w(s, "cT", tE + 14 * (u - 2) + 14)
                        inc("cS", s.activation(
                            HEB[u % 2][:, tg * 512:(tg + 1) * 512],
                            PA[tg][:, :], AF.Relu,
                            bias=BEBT[:, e * KC + fb:e * KC + fb + 1]))

            @block.vector
            def _(v):
                for t in range(NT):
                    w(v, "cP", rbase + 2 * t + 1)
                    inc("cV", v.reduce_max(NLM[:, t:t + 1], LGF[:, :],
                                           axis=AX.X, negate=True))
                    inc("cV", v.max(MX8[:, :], LGF[:, :]))
                    w(v, "cV", vR + 4 * t + 2)
                    inc("cV", v.max_index(IDXB[:, 0:8], MX8[:, :], LGF[:, :]))
                    w(v, "cV", vR + 4 * t + 3)
                    inc("cV", v.tensor_single_scalar(
                        IDXB[:, 1:2], IDXB[:, 0:1], 7, ALU.bitwise_and))
                    w(v, "cV", vR + 4 * t + 4)
                    v.tensor_copy(EIDT[:, t:t + 1], IDXB[:, 1:2])
                w(v, "cP", pE)
                for t in range(NT):
                    v.reciprocal(PCOL[:, t:t + 1], SCT[:, t:t + 1])
                for t in range(NT):
                    for e in range(E):
                        v.scalar_tensor_tensor(
                            MSK[:, t * 8 + e:t * 8 + e + 1], EIDT[:, t:t + 1],
                            float(e), PCOL[:, t:t + 1],
                            ALU.is_equal, ALU.mult)
                inc("cV", v.tensor_copy(ZT[:, 1:2], ZT[:, 0:1]))
                for u in range(NU):
                    e, fb, th = usrc(u)
                    TU = tE + 14 * u
                    for gg in range(KC):
                        t = th * KC + gg
                        if gg < KC - 1:
                            w(v, "cT", TU + 4 + gg + 2)
                        else:
                            w(v, "cT", TU + 14)
                        inc("cV", v.scalar_tensor_tensor(
                            X[:, t * D + fb * P:t * D + (fb + 1) * P],
                            PT[gg % 2][:, :],
                            MSK[:, t * 8 + e:t * 8 + e + 1],
                            X[:, t * D + fb * P:t * D + (fb + 1) * P],
                            ALU.mult, ALU.add))

            assert C["cT"] == tE + 14 * NU
            assert C["cS"] == sR + 3 * NT + 2 * NU
            assert C["cV"] == vE + KC * NU
            assert C["cP"] == pE + 9 * NU
            return C["cV"]


        def emit_ssm():
            TPI = 6.283185307179586
            REVP = OMT2[:, 0:P]
            prevS, prevV = C["cS"], C["cV"]

            @block.gpsimd
            def _(g):
                w(g, "cS", prevS)
                w(g, "cV", prevV)
                inc("cP", g.affine_select(
                    REVP, ONESB[:, :], pattern=[[1, P]],
                    compare_op=ALU.is_equal, fill=0.0, base=-127,
                    channel_multiplier=1))
            lnS = emit_ln_to_fm(perm=REVP)
            PRB = V[:, 0:4096]
            SSTK = V[:, 4096:8192]
            YB = V[:, 0:2048]
            YG = V[:, 2048:4096]
            UGS = [V[:, 8192:8320], V[:, 8320:8448]]
            UTS = [V[:, 8448:8576], V[:, 8576:8704]]
            PLT = [V[:, 8704:8832], V[:, 8832:8960]]
            KDT = [V[:, 8960:9216], V[:, 9216:9472]]
            WTT = [V[0:64, 9472:9728], V[0:64, 9728:9984]]
            STT = [V[0:64, 10880:10944], V[0:64, 10944:11008]]
            KCH = V[:, 9856:9888]
            KCH0 = V[:, 9888:9889]
            ZBL = V[:, 2048:2176]
            YSB = [V[:, 9984:10112], V[:, 10112:10240]]
            PSB = [SQF[0:64, 0:64], SQF[0:64, 64:128]]
            WTMPA = V[:, 0:1024]
            WTMPB = V[:, 1024:2048]
            CCT = V[:, 10240:12288]
            CCTB = V[:, 12288:14336]
            T1B = V[:, 14336:16384]
            USS = Q[:, :]
            LDTC = NQ[:, 64:65]
            ARE = NQ[:, 96:128]
            AIM = NQ[:, 128:160]
            CRE = NQ[:, 160:192]
            CIM = NQ[:, 192:224]
            DCOL = NQ[:, 66:67]
            EA = NQ[:, 224:256]
            DAR = NK[:, 0:32]
            DAI = NK[:, 32:64]
            YRC = NK[:, 64:96]
            KRC = NK[:, 96:128]
            XRD = NK[:, 128:160]
            COSA = NK[:, 160:192]
            K1C = NK[:, 192:224]
            PI2T = NK[:, 224:256]
            ERE2 = SQE[:, 0:32]
            SINV = SQE[:, 32:64]
            COSV = SQE[:, 64:96]
            LK = SQE[:, 96:160]
            CC2F = SQE[:, 160:224]
            LRE = OZB[:, 1024:1025]
            LREV = SQE[:, 224:256]
            LIMV = NQ[:, 0:32]
            LTB = KVL[:, 1024:1088]
            CCA = KVL[:, 1088:1152]
            CCB = KVL[:, 1152:1216]
            PW = SQF[:, 0:2048]
            CM1 = OZB[:, 0:32]
            CM2 = OZB[:, 32:64]
            H1 = OZB[:, 0:1024]
            H2 = OZB[:, 0:512]
            H3 = NK[:, 0:256]
            H4 = OZB[:, 0:128]
            H5 = OZB[:, 128:192]
            H6 = OZB[:, 192:224]
            KZC = OZB[:, 224:225]

            sP, vP, tP, cPb, qSb, qGb = (C["cS"], C["cV"], C["cT"],
                                         C["cP"], C["qS"], C["qG"])

            @block.sync
            def _(sync):
                for dst, src in [(LDTC, ldt_d[:, :]), (ARE, are_d[:, :]),
                                 (AIM, aim_d[:, :]), (CRE, cre_d[:, :]),
                                 (CIM, cim_d[:, :]), (DCOL, ddb_d[:, :]),
                                 (UIDXS[:, :], uidx_d[:, :]),
                                 (USIDX[:, :], ussidx_d[:, :])]:
                    w(sync, "cV", vP)
                    inc("qS", sync.dma_start(out=dst, in_=src), 16)

            @block.gpsimd
            def _(g):
                w(g, "tB", lnS)
                inc("qG", g.dma_start(out=xg_loc[:, :], in_=XNF[:, :]), 16)
                w(g, "qG", C["qG"])
                inc("car", g.collective_compute(
                    "AllGather", ALU.bypass,
                    replica_groups=[[0, 1, 2, 3, 4, 5, 6, 7]],
                    ins=[xg_loc],
                    outs=[xg_all.rearrange("(r p) w -> r p w", p=P)]))
                w(g, "car", C["car"])
                for sl in range(8):
                    w(g, "qS", qSb + 16 * 8)
                    inc("qG", g.indirect_dma_start(
                        out=USS[:, sl * TOK:(sl + 1) * TOK],
                        out_offset=None,
                        in_=xg2k,
                        in_offset=IndirectOffsetOnAxis(
                            ap=USIDX[:, sl:sl + 1], axis=0)), 16)
                g.memset(PI2T[:, :], TPI / 4.0)
                g.memset(ZBL[:, :], 0.0)
                inc("cP", g.memset(SSTK[:, :], 0.0))
                inc("cP", g.tensor_copy(ZT[:, 1:2], ZT[:, 0:1]))
            gUSS = C["qG"]
            gC = gUSS
            pMS = C["cP"]

            cb0 = C["cS"]

            DTB = OZB[:, 256:288]

            @block.scalar
            def _(s):
                w(s, "qS", qSb + 16 * 6)
                inc("cS", s.activation(NQ[:, 65:66], LDTC, AF.Exp))
                inc("cS", s.activation(EA, ARE, AF.Exp))
                s.activation(ZT[:, 2:3], ZT[:, 0:1], AF.Copy)
                s.activation(ZT[:, 2:3], ZT[:, 0:1], AF.Copy)
                inc("cS", s.activation(DTB, ONESB[:, 0:32], AF.Copy,
                                       scale=NQ[:, 65:66]))

            rel0 = C["cP"]

            @block.gpsimd
            def _(g):
                w(g, "cS", cb0 + 3)
                inc("qG", g.dma_start(out=p_dram[:, 100:132], in_=EA), 16)
                inc("qG", g.dma_start(out=p_dram[:, 132:164], in_=DTB), 16)
                w(g, "qG", C["qG"])
                inc("qG", g.dma_start(out=EA, in_=p_dram[:, 100:132]), 16)
                inc("qG", g.dma_start(out=DTB, in_=p_dram[:, 132:164]), 16)
                w(g, "qG", C["qG"])
                inc("cP", g.tensor_copy(ZT[:, 1:2], ZT[:, 0:1]))

            vb0 = C["cV"]
            vpos = {}

            @block.vector
            def _(v):
                def B(op):
                    inc("cV", op)
                    w(v, "cV", C["cV"])

                w(v, "cP", rel0 + 1)
                for _ in range(6):
                    v.tensor_copy(CM1, EA)
                B(v.scalar_tensor_tensor(DAR, EA, -1.0, DTB, ALU.mult, ALU.mult))
                B(v.tensor_tensor(DAI, AIM, DTB, ALU.mult))
                B(v.tensor_single_scalar(YRC, DAI, 1.0 / TPI, ALU.mult))
                B(v.tensor_single_scalar(KRC, YRC, 0.5, ALU.is_ge))
                B(v.tensor_single_scalar(K1C, YRC, 1.5, ALU.is_ge))
                B(v.tensor_tensor(KRC, KRC, K1C, ALU.add))
                B(v.tensor_single_scalar(K1C, YRC, -0.5, ALU.is_le))
                B(v.tensor_tensor(KRC, KRC, K1C, ALU.subtract))
                B(v.tensor_single_scalar(K1C, YRC, -1.5, ALU.is_le))
                B(v.tensor_tensor(KRC, KRC, K1C, ALU.subtract))
                B(v.scalar_tensor_tensor(XRD, KRC, -TPI, DAI, ALU.mult, ALU.add))
                B(v.scalar_tensor_tensor(COSA, XRD, -1.0, PI2T, ALU.mult, ALU.add))
                B(v.tensor_single_scalar(K1C, COSA, TPI / 2.0, ALU.is_gt))
                if DEBUG:
                    v.tensor_copy(KVL[:, 2256:2288], KRC)
                    v.tensor_copy(KVL[:, 2288:2320], YRC)
                    v.tensor_copy(KVL[:, 2000:2032], EA)
                    v.tensor_copy(KVL[:, 2032:2064], DAR)
                    v.tensor_copy(KVL[:, 2064:2096], DAI)
                    v.tensor_copy(KVL[:, 2096:2128], XRD)
                B(v.scalar_tensor_tensor(COSA, K1C, -TPI, COSA,
                                         ALU.mult, ALU.add))
                vpos["cosa"] = C["cV"]

            rel1 = C["cP"]

            @block.gpsimd
            def _(g):
                w(g, "cV", vpos["cosa"])
                inc("cP", g.tensor_copy(ZT[:, 1:2], ZT[:, 0:1]))

            @block.scalar
            def _(s):
                w(s, "cP", rel1 + 1)
                inc("cS", s.activation(SINV, XRD, AF.Sin))
                inc("cS", s.activation(COSV, COSA, AF.Sin))
                inc("cS", s.activation(ERE2, DAR, AF.Exp))

            rel2 = C["cP"]

            @block.gpsimd
            def _(g):
                w(g, "cS", cb0 + 6)
                inc("qG", g.dma_start(out=p_dram[:, 0:96], in_=SQE[:, 0:96]), 16)
                w(g, "qG", C["qG"])
                inc("qG", g.dma_start(out=SQE[:, 0:96], in_=p_dram[:, 0:96]), 16)
                w(g, "qG", C["qG"])
                inc("cP", g.tensor_copy(ZT[:, 1:2], ZT[:, 0:1]))

            vb1 = C["cV"]
            gW = C["qG"]
            gpos = {}

            def gk_pos(kind, k):
                off = {"k": 4, "pp": 5, "wre": 6, "wim": 7}[kind]
                return gW + 16 * (off + 4 * k)

            @block.vector
            def _(v):
                def B(op):
                    inc("cV", op)
                    w(v, "cV", C["cV"])

                w(v, "cP", rel2 + 1)
                for _ in range(6):
                    v.tensor_copy(CM1, CRE)
                B(v.tensor_tensor(LREV, ERE2, COSV, ALU.mult))
                B(v.tensor_tensor(LIMV, ERE2, SINV, ALU.mult))
                if DEBUG:
                    v.tensor_copy(KVL[:, 2128:2160], LREV)
                    v.tensor_copy(KVL[:, 2160:2192], LIMV)
                    v.tensor_copy(KVL[:, 2192:2224], ERE2)
                    v.tensor_copy(KVL[:, 2224:2256], SINV)
                QR = OZB[:, 288:320]
                QI = OZB[:, 320:352]
                IA2 = OZB[:, 352:384]
                NR = OZB[:, 384:416]
                NI = OZB[:, 416:448]
                AR2 = OZB[:, 448:480]
                B(v.tensor_single_scalar(AR2, EA, -1.0, ALU.mult))
                B(v.tensor_single_scalar(NR, LREV, 1.0, ALU.subtract))
                B(v.tensor_copy(NI, LIMV))
                B(v.tensor_tensor(CM1, AR2, AR2, ALU.mult))
                B(v.tensor_tensor(CM2, AIM, AIM, ALU.mult))
                B(v.tensor_tensor(IA2, CM1, CM2, ALU.add))
                B(v.reciprocal(IA2, IA2))
                B(v.tensor_tensor(CM1, NR, AR2, ALU.mult))
                B(v.tensor_tensor(CM2, NI, AIM, ALU.mult))
                B(v.tensor_tensor(QR, CM1, CM2, ALU.add))
                B(v.tensor_tensor(QR, QR, IA2, ALU.mult))
                B(v.tensor_tensor(CM1, NI, AR2, ALU.mult))
                B(v.tensor_tensor(CM2, NR, AIM, ALU.mult))
                B(v.tensor_tensor(QI, CM1, CM2, ALU.subtract))
                B(v.tensor_tensor(QI, QI, IA2, ALU.mult))
                B(v.tensor_tensor(CM1, CRE, QR, ALU.mult))
                B(v.tensor_tensor(CM2, CIM, QI, ALU.mult))
                B(v.tensor_tensor(CC2F[:, 0:32], CM1, CM2, ALU.subtract))
                B(v.tensor_single_scalar(CC2F[:, 0:32], CC2F[:, 0:32], 2.0,
                                         ALU.mult))
                B(v.tensor_tensor(CM1, CRE, QI, ALU.mult))
                B(v.tensor_tensor(CM2, CIM, QR, ALU.mult))
                B(v.tensor_tensor(CC2F[:, 32:64], CM1, CM2, ALU.add))
                B(v.tensor_single_scalar(CC2F[:, 32:64], CC2F[:, 32:64], -2.0,
                                         ALU.mult))
                B(v.tensor_copy(CCA, CC2F))
                B(v.tensor_copy(CCB[:, 0:32], CC2F[:, 32:64]))
                B(v.tensor_single_scalar(CCB[:, 32:64], CC2F[:, 0:32], -1.0,
                                         ALU.mult))
                B(v.tensor_copy(CCT[:, 0:64], CCA))
                B(v.tensor_copy(CCTB[:, 0:64], CCB))
                wdt = 64
                while wdt < 2048:
                    B(v.tensor_copy(CCT[:, wdt:2 * wdt], CCT[:, 0:wdt]))
                    B(v.tensor_copy(CCTB[:, wdt:2 * wdt], CCTB[:, 0:wdt]))
                    wdt *= 2
                B(v.tensor_tensor(KZC[:, 0:1], CC2F[:, 0:1], CC2F[:, 1:2],
                                  ALU.add))
                for n in range(2, 32):
                    B(v.tensor_tensor(KZC[:, 0:1], KZC[:, 0:1],
                                      CC2F[:, n:n + 1], ALU.add))
                B(v.tensor_copy(KCH0[:, 0:1], KZC[:, 0:1]))
                vpos["k0"] = C["cV"]

                def cmul_into(dre, dim_, sre, sim):
                    B(v.tensor_tensor(CM1, sre, LREV, ALU.mult))
                    B(v.tensor_tensor(CM2, sim, LIMV, ALU.mult))
                    B(v.tensor_tensor(dre, CM1, CM2, ALU.subtract))
                    B(v.tensor_tensor(CM1, sre, LIMV, ALU.mult))
                    B(v.tensor_tensor(CM2, sim, LREV, ALU.mult))
                    B(v.tensor_tensor(dim_, CM1, CM2, ALU.add))

                for k in range(8):
                    if k >= 1:
                        w(v, "qG", gk_pos("pp", k - 1))
                    if k == 0:
                        B(v.tensor_copy(PW[:, 0:32], LREV))
                        B(v.tensor_copy(PW[:, 32:64], LIMV))
                    else:
                        cmul_into(PW[:, 0:32], PW[:, 32:64],
                                  LK[:, 0:32], LK[:, 32:64])
                    for l in range(1, 32):
                        cmul_into(PW[:, l * 64:l * 64 + 32],
                                  PW[:, l * 64 + 32:l * 64 + 64],
                                  PW[:, (l - 1) * 64:(l - 1) * 64 + 32],
                                  PW[:, (l - 1) * 64 + 32:(l - 1) * 64 + 64])
                    B(v.tensor_copy(LK, PW[:, 31 * 64:32 * 64]))
                    nl = 32 if k < 7 else 31
                    if k >= 1:
                        w(v, "qG", gk_pos("wim", k - 1))
                    WV, WO = 16384, 1040
                    B(v.tensor_tensor(T1B, PW, CCT, ALU.mult))
                    sA = AP(T1B.tensor, T1B.offset, [[WV, P], [64, 32], [1, 32]])
                    sBp = AP(T1B.tensor, T1B.offset + 32,
                             [[WV, P], [64, 32], [1, 32]])
                    B(v.tensor_tensor(H1, sA, sBp, ALU.add))
                    if k >= 1:
                        w(v, "qG", gk_pos("wre", k - 1))
                    h1t = AP(H1.tensor, H1.offset,
                             [[WO, P], [1, 32], [32, nl]])
                    B(v.tensor_copy(WTMPA[:, 0:32 * nl], h1t))
                    h1a = AP(H1.tensor, H1.offset, [[WO, P], [32, 32], [1, 16]])
                    h1b = AP(H1.tensor, H1.offset + 16,
                             [[WO, P], [32, 32], [1, 16]])
                    B(v.tensor_tensor(H2, h1a, h1b, ALU.add))
                    h2a = AP(H2.tensor, H2.offset, [[WO, P], [16, 32], [1, 8]])
                    h2b = AP(H2.tensor, H2.offset + 8,
                             [[WO, P], [16, 32], [1, 8]])
                    B(v.tensor_tensor(H3, h2a, h2b, ALU.add))
                    h3a = AP(H3.tensor, H3.offset, [[256, P], [8, 32], [1, 4]])
                    h3b = AP(H3.tensor, H3.offset + 4,
                             [[256, P], [8, 32], [1, 4]])
                    B(v.tensor_tensor(H4, h3a, h3b, ALU.add))
                    h4a = AP(H4.tensor, H4.offset, [[WO, P], [4, 32], [1, 2]])
                    h4b = AP(H4.tensor, H4.offset + 2,
                             [[WO, P], [4, 32], [1, 2]])
                    B(v.tensor_tensor(H5, h4a, h4b, ALU.add))
                    h5a = AP(H5.tensor, H5.offset, [[WO, P], [2, 32], [1, 1]])
                    h5b = AP(H5.tensor, H5.offset + 1,
                             [[WO, P], [2, 32], [1, 1]])
                    B(v.tensor_tensor(H6, h5a, h5b, ALU.add))
                    if k >= 1:
                        w(v, "qG", gk_pos("k", k - 1))
                    B(v.tensor_copy(KCH[:, 0:32], H6[:, 0:32]))
                    vpos[("kwre", k)] = C["cV"]
                    B(v.tensor_tensor(T1B, PW, CCTB, ALU.mult))
                    B(v.tensor_tensor(H1, sA, sBp, ALU.add))
                    B(v.tensor_copy(WTMPB[:, 0:32 * nl], h1t))
                    vpos[("wim", k)] = C["cV"]
                B(v.tensor_copy(LTB, PW[:, 31 * 64:32 * 64]))
                vpos["ltb"] = C["cV"]

            @block.gpsimd
            def _(g):
                w(g, "cV", vpos["k0"])
                inc("qG", g.dma_start(
                    out=AP(k_dram.tensor, 127, [[384, P], [1, 1]]),
                    in_=KCH0[:, 0:1], allow_slow_non_contiguous=True), 16)
                inc("qG", g.dma_start(
                    out=AP(k_dram.tensor, 0, [[384, P], [1, 127]]),
                    in_=ZBL[:, 0:127]), 16)
                inc("qG", g.dma_start(
                    out=AP(w_dram.tensor, 0, [[64 * 256, P], [256, 64]]),
                    in_=CCA, allow_slow_non_contiguous=True), 16)
                for k in range(8):
                    nl = 32 if k < 7 else 31
                    w(g, "cV", vpos[("kwre", k)])
                    inc("qG", g.dma_start(
                        out=AP(k_dram.tensor, 127 + 32 * k + 1,
                               [[384, P], [1, nl]]),
                        in_=KCH[:, 0:nl]), 16)
                    gpos[("k", k)] = C["qG"]
                    jj = 32 * ((k + 4) % 8)
                    w(g, "cV", vpos[("kwre", k)])
                    inc("qG", g.dma_start(
                        out=AP(pp_dram.tensor, jj * 64,
                               [[256 * 64, P], [64, 32], [1, 64]]),
                        in_=PW[:, 0:2048]), 16)
                    gpos[("pp", k)] = C["qG"]
                    w(g, "cV", vpos[("kwre", k)])
                    inc("qG", g.dma_start(
                        out=AP(w_dram.tensor, 32 * k + 1,
                               [[64 * 256, P], [256, 32], [1, nl]]),
                        in_=WTMPA[:, 0:32 * nl]), 16)
                    gpos[("wre", k)] = C["qG"]
                    w(g, "cV", vpos[("wim", k)])
                    inc("qG", g.dma_start(
                        out=AP(w_dram.tensor, (32 * 256) + 32 * k + 1,
                               [[64 * 256, P], [256, 32], [1, nl]]),
                        in_=WTMPB[:, 0:32 * nl]), 16)
                    gpos[("wim", k)] = C["qG"]
                    assert gpos[("wim", k)] == gk_pos("wim", k)
            gK = C["qG"]
            consts_cV = C["cV"]

            t1T, s1S, g1G = C["cT"], C["cS"], C["qG"]

            @block.sync
            def _(sync):
                for ch in range(P):
                    for jb in range(2):
                        if ch >= 2:
                            w(sync, "cT", t1T + 6 * (ch - 2) + 4)
                        else:
                            w(sync, "qG", gK)
                        src = AP(pp_dram.tensor, (ch * 256 + jb * 128) * 64,
                                 [[64, P], [1, 64]])
                        inc("qS", sync.dma_start(
                            out=PLT[ch % 2][:, jb * 64:(jb + 1) * 64],
                            in_=src), 16)

            @block.gpsimd
            def _(g):
                w(g, "qG", gK)
                for ch in range(P):
                    if ch >= 2:
                        w(g, "cT", t1T + 6 * (ch - 2) + 1)
                    else:
                        w(g, "qG", gK)
                    inc("qG", g.indirect_dma_start(
                        out=UGS[ch % 2][:, :], out_offset=None,
                        in_=xg128,
                        in_offset=IndirectOffsetOnAxis(
                            ap=UIDXS[:, ch:ch + 1], axis=0)), 16)
                    if ch >= 1:
                        w(g, "cS", s1S + 2 * (ch - 1) + 2)
                        inc("qG", g.dma_start(
                            out=AP(p_dram.tensor, (ch - 1) * 4096,
                                   [[64, 64], [1, 64]]),
                            in_=PSB[(ch - 1) % 2][:, :]), 16)
                w(g, "cS", s1S + 2 * (P - 1) + 2)
                inc("qG", g.dma_start(
                    out=AP(p_dram.tensor, (P - 1) * 4096,
                           [[64, 64], [1, 64]]),
                    in_=PSB[(P - 1) % 2][:, :]), 16)

            @block.tensor
            def _(tn):
                w(tn, "cV", consts_cV)
                for ch in range(P):
                    w(tn, "qG", g1G + 16 * (2 * ch + 1))
                    if ch >= 2:
                        w(tn, "cS", s1S + 2 * (ch - 2) + 1)
                    inc("cT", tn.transpose(PT[ch % 2][:, :], UGS[ch % 2][:, :],
                                           IDENT[:, :]))
                    inc("cT", dummy_ps(tn))
                    inc("cT", dummy_ps(tn))
                    w(tn, "cS", s1S + 2 * ch + 1)
                    w(tn, "qS", qSb + 16 * (8 + 2 * ch + 2))
                    if ch >= 1:
                        w(tn, "cS", s1S + 2 * (ch - 1) + 2)
                    tn.matmul(PKV[:, 0:64], PLT[ch % 2][:, 0:64],
                              UTS[ch % 2][:, 0:64], start=True, stop=False)
                    inc("cT", tn.matmul(PKV[:, 0:64], PLT[ch % 2][:, 64:128],
                                        UTS[ch % 2][:, 64:128],
                                        start=False, stop=True))
                    inc("cT", dummy_ps(tn))
                    inc("cT", dummy_ps(tn))

            @block.scalar
            def _(s):
                for ch in range(P):
                    w(s, "cT", t1T + 6 * ch + 3)
                    inc("cS", s.activation(UTS[ch % 2][:, :],
                                           PT[ch % 2][:, :], AF.Copy))
                    w(s, "cT", t1T + 6 * ch + 6)
                    if ch >= 2:
                        w(s, "qG", g1G + 16 * (2 * (ch - 2) + 2))
                    inc("cS", s.activation(PSB[ch % 2][:, :],
                                           PKV[:, 0:64], AF.Copy))
            p1_done_qG = C["qG"]

            g2G = C["qG"]

            @block.gpsimd
            def _(g):
                w(g, "qG", p1_done_qG)
                inc("qG", g.dma_start(out=PRB[:, :], in_=p_dram[:, :]), 16)
            rec_in_qG = C["qG"]
            vR2 = C["cV"]

            @block.vector
            def _(v):
                w(v, "qG", rec_in_qG)
                w(v, "cP", pMS)
                A1 = SQF[:, 0:32]
                B1 = SQF[:, 32:64]
                D1 = SQF[:, 64:96]
                WV = 16384

                def scol(buf, call, im):
                    return AP(buf.tensor, buf.offset + 32 * 64 * im + call,
                              [[WV, P], [64, 32]])

                def B2(op):
                    inc("cV", op)
                    w(v, "cV", C["cV"])

                for b in range(2):
                    for c in range(1, 32):
                        call = b * 32 + c
                        B2(v.tensor_tensor(A1, scol(SSTK, call - 1, 0),
                                           LTB[:, 0:32], ALU.mult))
                        B2(v.tensor_tensor(B1, scol(SSTK, call - 1, 1),
                                           LTB[:, 32:64], ALU.mult))
                        B2(v.tensor_tensor(D1, A1, B1, ALU.subtract))
                        B2(v.tensor_tensor(scol(SSTK, call, 0), D1,
                                           scol(PRB, call - 1, 0), ALU.add))
                        B2(v.tensor_tensor(A1, scol(SSTK, call - 1, 0),
                                           LTB[:, 32:64], ALU.mult))
                        B2(v.tensor_tensor(B1, scol(SSTK, call - 1, 1),
                                           LTB[:, 0:32], ALU.mult))
                        B2(v.tensor_tensor(D1, A1, B1, ALU.add))
                        B2(v.tensor_tensor(scol(SSTK, call, 1), D1,
                                           scol(PRB, call - 1, 1), ALU.add))
                inc("cV", v.tensor_copy(ZT[:, 1:2], ZT[:, 0:1]))

            @block.gpsimd
            def _(g):
                w(g, "cV", vR2 + 62 * 8 + 1)
                inc("qG", g.dma_start(out=s_dram[:, :], in_=SSTK[:, :]), 16)
            s_done_qG = C["qG"]

            t2T, s2S, g2G2, q2S = C["cT"], C["cS"], C["qG"], C["qS"]

            def YT2(ch, ib):
                off = 10368 + 128 * (2 * (ch % 2) + ib)
                return V[0:64, off:off + 128]

            @block.sync
            def _(sync):
                for ch in range(P):
                    if ch >= 2:
                        w(sync, "cT", t2T + 11 * (ch - 2) + 11)
                    else:
                        w(sync, "qG", s_done_qG)
                    inc("qS", sync.dma_start(
                        out=KDT[ch % 2][:, 0:128],
                        in_=AP(k_dram.tensor, ch * 384, [[1, P], [1, P]])), 16)
                    inc("qS", sync.dma_start(
                        out=KDT[ch % 2][:, 128:256],
                        in_=AP(k_dram.tensor, ch * 384 + 128,
                               [[1, P], [1, P]])), 16)
                    inc("qS", sync.dma_start(
                        out=WTT[ch % 2][:, :],
                        in_=AP(w_dram.tensor, ch * 64 * 256,
                               [[256, 64], [1, 256]])), 16)
                    inc("qS", sync.dma_start(
                        out=STT[ch % 2][:, :],
                        in_=AP(s_dram.tensor, ch * 4096,
                               [[64, 64], [1, 64]])), 16)

            @block.gpsimd
            def _(g):
                w(g, "qG", s_done_qG)
                for ch in range(P):
                    if ch >= 2:
                        w(g, "cT", t2T + 11 * (ch - 2) + 1)
                    else:
                        w(g, "qG", s_done_qG)
                    inc("qG", g.indirect_dma_start(
                        out=UGS[ch % 2][:, :], out_offset=None,
                        in_=xg128,
                        in_offset=IndirectOffsetOnAxis(
                            ap=UIDXS[:, ch:ch + 1], axis=0)), 16)
                    w(g, "cS", s2S + 5 * ch + 5)
                    for ib in range(2):
                        inc("qG", g.dma_start(
                            out=AP(y_dram.tensor, ch * 16384 + ib * 128,
                                   [[256, 64], [1, 128]]),
                            in_=YT2(ch, ib)), 16)

            @block.tensor
            def _(tn):
                w(tn, "qG", s_done_qG)
                for ch in range(P):
                    w(tn, "qG", g2G2 + 16 * (3 * ch + 1))
                    if ch >= 1:
                        w(tn, "cS", s2S + 5 * (ch - 1) + 5)
                    inc("cT", tn.transpose(PT[ch % 2][:, :], UGS[ch % 2][:, :],
                                           IDENT[:, :]))
                    inc("cT", dummy_ps(tn))
                    inc("cT", dummy_ps(tn))
                    w(tn, "cS", s2S + 5 * ch + 1)
                    w(tn, "qS", q2S + 16 * 4 * (ch + 1))
                    tn.matmul(PK0[:, 0:64], KDT[ch % 2][:, 0:128],
                              UTS[ch % 2][:, 0:64], start=True, stop=False)
                    inc("cT", tn.matmul(PK0[:, 0:64], WTT[ch % 2][0:64, 0:128],
                                        STT[ch % 2][:, :],
                                        start=False, stop=True))
                    tn.matmul(PK1[:, 0:64], KDT[ch % 2][:, 128:256],
                              UTS[ch % 2][:, 0:64], start=True, stop=False)
                    tn.matmul(PK1[:, 0:64], KDT[ch % 2][:, 0:128],
                              UTS[ch % 2][:, 64:128], start=False, stop=False)
                    inc("cT", tn.matmul(PK1[:, 0:64],
                                        WTT[ch % 2][0:64, 128:256],
                                        STT[ch % 2][:, :],
                                        start=False, stop=True))
                    inc("cT", dummy_ps(tn))
                    inc("cT", dummy_ps(tn))
                    w(tn, "cS", s2S + 5 * ch + 3)
                    inc("cT", tn.transpose(PT[ch % 2][0:64, :],
                                           YSB[ch % 2][:, 0:64], IDENT[:, :]))
                    inc("cT", tn.transpose(PT[(ch + 1) % 2][0:64, :],
                                           YSB[ch % 2][:, 64:128], IDENT[:, :]))
                    inc("cT", dummy_ps(tn))
                    inc("cT", dummy_ps(tn))

            @block.scalar
            def _(s):
                for ch in range(P):
                    w(s, "cT", t2T + 11 * ch + 3)
                    inc("cS", s.activation(UTS[ch % 2][:, :],
                                           PT[ch % 2][:, :], AF.Copy))
                    if ch >= 2:
                        w(s, "cT", t2T + 11 * (ch - 2) + 11)
                    w(s, "cT", t2T + 11 * ch + 5)
                    inc("cS", s.activation(YSB[ch % 2][:, 0:64],
                                           PK0[:, 0:64], AF.Copy))
                    w(s, "cT", t2T + 11 * ch + 7)
                    inc("cS", s.activation(YSB[ch % 2][:, 64:128],
                                           PK1[:, 0:64], AF.Copy))
                    w(s, "cT", t2T + 11 * ch + 9)
                    inc("cS", s.activation(YT2(ch, 0), PT[ch % 2][0:64, :],
                                           AF.Copy))
                    w(s, "cT", t2T + 11 * ch + 11)
                    inc("cS", s.activation(YT2(ch, 1), PT[(ch + 1) % 2][0:64, :],
                                           AF.Copy))
            y_done_qG = C["qG"]

            g3G, v3V, s3S = C["qG"], C["cV"], C["cS"]

            @block.gpsimd
            def _(g):
                w(g, "qG", y_done_qG)
                for blk in range(8):
                    if blk >= 2:
                        w(g, "cS", s3S + (blk - 2) + 1)
                    inc("qG", g.dma_start(
                        out=YB[:, 0:2048],
                        in_=y_dram[:, blk * 2048:(blk + 1) * 2048]), 16)
                    w(g, "cS", s3S + blk + 1)
                    inc("qG", g.dma_start(
                        out=yfin_dram[:, blk * 2048:(blk + 1) * 2048],
                        in_=YG[:, 0:2048]), 16)
                w(g, "qG", C["qG"])
                inc("car", g.collective_compute(
                    "AllGather", ALU.bypass,
                    replica_groups=[[0, 1, 2, 3, 4, 5, 6, 7]],
                    ins=[yfin_dram],
                    outs=[yg_all.rearrange("(r p) w -> r p w", p=P)]))
            du_car = C["car"]

            @block.vector
            def _(v):
                for blk in range(8):
                    w(v, "qG", g3G + 16 * (2 * blk + 1))
                    rev = AP(USS.tensor, USS.offset + blk * 2048 + 127,
                             [[P * 128, P], [128, 16], [-1, 128]])
                    inc("cV", v.tensor_scalar(SQF[:, 0:2048], rev, DCOL, None,
                                              ALU.mult))
                    w(v, "cV", C["cV"])
                    inc("cV", v.tensor_tensor(SQF[:, 0:2048], SQF[:, 0:2048],
                                              YB[:, 0:2048], ALU.add))

            @block.scalar
            def _(s):
                for blk in range(8):
                    w(s, "cV", v3V + 2 * blk + 2)
                    inc("cS", s.activation(YG[:, 0:2048], SQF[:, 0:2048],
                                           AF.Gelu))

            t4T, v4V, g4G = C["cT"], C["cV"], C["qG"]

            @block.sync
            def _(sync):
                w(sync, "qG", y_done_qG)
                inc("qS", sync.dma_start(out=UIDXS[:, :], in_=yidx_d[:, :]), 16)
            yidx_qS = C["qS"]

            @block.gpsimd
            def _(g):
                w(g, "car", du_car)
                w(g, "qS", yidx_qS)
                for gidx in range(P):
                    if gidx >= 2:
                        w(g, "cT", t4T + 3 * (gidx - 2) + 1)
                    else:
                        w(g, "qS", yidx_qS)
                    inc("qG", g.indirect_dma_start(
                        out=UGS[gidx % 2][:, :], out_offset=None,
                        in_=yg128,
                        in_offset=IndirectOffsetOnAxis(
                            ap=UIDXS[:, gidx:gidx + 1], axis=0)), 16)

            @block.tensor
            def _(tn):
                for gidx in range(P):
                    w(tn, "qG", g4G + 16 * (gidx + 1))
                    if gidx >= 2:
                        w(tn, "cV", v4V + gidx - 1)
                    inc("cT", tn.transpose(PT[gidx % 2][:, :],
                                           UGS[gidx % 2][:, :], IDENT[:, :]))
                    inc("cT", dummy_ps(tn))
                    inc("cT", dummy_ps(tn))

            @block.vector
            def _(v):
                for gidx in range(P):
                    cbk, t = gidx // 16, gidx % 16
                    w(v, "cT", t4T + 3 * gidx + 3)
                    inc("cV", v.tensor_tensor(
                        X[:, t * D + cbk * P:t * D + (cbk + 1) * P],
                        X[:, t * D + cbk * P:t * D + (cbk + 1) * P],
                        PT[gidx % 2][:, :], ALU.add))
            return C["cV"]

        ln1_tB = emit_ln_to_fm()
        attn_done_cV = emit_attn(ln1_tB)
        x_done = ("cV", attn_done_cV)
        if stage >= 2:
            ffn1_done_cV = emit_ffn(first=True)
            x_done = ("cV", ffn1_done_cV)
        if stage >= 3:
            ssm_done_cV = emit_ssm()
            x_done = ("cV", ssm_done_cV)
        if stage >= 4:
            ffn2_done_cV = emit_ffn(first=False)
            x_done = ("cV", ffn2_done_cV)

        @block.gpsimd
        def _(g):
            w(g, x_done[0], x_done[1])
            xo = xout.rearrange("(t p) d -> t p d", p=P)
            for t in range(NT):
                inc("qG", g.dma_start(out=xo[t], in_=X[:, t * D:(t + 1) * D]), 16)
            if DEBUG:
                for nm, src in [("d_xnf", XNF), ("d_q", Q), ("d_v", V),
                                ("d_kvl", KVL[:, :]), ("d_nq", NQ), ("d_nk", NK),
                                ("d_sqe", SQE)]:
                    sap = src if isinstance(src, AP) else src[:, :]
                    inc("qG", g.dma_start(out=dbg[nm][:, :], in_=sap), 16)
                if stage >= 3:
                    for nm, src in [("d_k", k_dram), ("d_p", p_dram),
                                    ("d_s", s_dram), ("d_y", y_dram),
                                    ("d_yf", yfin_dram), ("d_w", w_dram)]:
                        inc("qG", g.dma_start(out=dbg[nm][:, :], in_=src[:, :]),
                            16)
                    inc("qG", g.dma_start(out=dbg["d_uss"][:, :],
                                          in_=Q[:, :]), 16)
            w(g, "qG", C["qG"])

    return nc


def kernel(**inputs):
    x = np.ascontiguousarray(np.asarray(inputs["L0_attn_a_x"], dtype=np.float32))
    wq = np.ascontiguousarray(np.asarray(inputs["L0_attn_a_Wq"], dtype=np.float32))
    wk = np.ascontiguousarray(np.asarray(inputs["L0_attn_a_Wk"], dtype=np.float32))
    wv = np.ascontiguousarray(np.asarray(inputs["L0_attn_a_Wv"], dtype=np.float32))
    om = np.ascontiguousarray(np.asarray(inputs["L0_attn_a_omega"], dtype=np.float32))
    wg = np.ascontiguousarray(np.asarray(inputs["L0_ffn_Wg"], dtype=np.float32))
    bg = np.ascontiguousarray(np.asarray(inputs["L0_ffn_bg"], dtype=np.float32))
    we = np.ascontiguousarray(
        np.asarray(inputs["L0_ffn_We"], dtype=np.float32).reshape(E * D, D))
    be = np.ascontiguousarray(np.asarray(inputs["L0_ffn_be"], dtype=np.float32))

    nc = build(STAGE)
    in_maps = []
    for r in range(8):
        b, qq = r // 4, r % 4
        in_maps.append({
            "xin": np.ascontiguousarray(x[b, qq * TOK:(qq + 1) * TOK, :]),
            "wq": wq, "wk": wk, "wv": wv, "om": om,
        })
        if STAGE >= 2:
            in_maps[-1].update({"wg": wg, "bg": bg, "we": we, "be": be})
        if STAGE >= 3:
            ch0 = r * 128
            uidx = np.zeros((128, 128), dtype=np.uint32)
            for comb in range(128):
                jb2, call = comb // 64, comb % 64
                bb2, qq2, cq2 = call // 32, (call % 32) // 8, call % 8
                s2 = bb2 * 4 + qq2
                for ch in range(128):
                    uidx[comb, ch] = (128 * s2 + ch) * 128 + r * 16 + cq2 * 2 + jb2
            ussidx = np.zeros((128, 8), dtype=np.uint32)
            for p2 in range(128):
                for s2 in range(8):
                    ussidx[p2, s2] = (128 * s2 + p2) * 8 + r
            yidx = np.zeros((128, 128), dtype=np.uint32)
            for chl in range(128):
                for g2 in range(128):
                    cb2, t2 = g2 // 16, g2 % 16
                    yidx[chl, g2] = (cb2 * 128 + chl) * 128 + (b * 64 + qq * 16 + t2)
            ldtf = np.asarray(inputs["L0_attn_b_log_dt"], dtype=np.float32)
            aref = np.asarray(inputs["L0_attn_b_A_real"], dtype=np.float32)
            aimf = np.asarray(inputs["L0_attn_b_A_imag"], dtype=np.float32)
            cref = np.asarray(inputs["L0_attn_b_C_real"], dtype=np.float32)
            cimf = np.asarray(inputs["L0_attn_b_C_imag"], dtype=np.float32)
            ddf = float(np.asarray(inputs["L0_attn_b_D"], dtype=np.float32))
            in_maps[-1].update({
                "ldt": np.ascontiguousarray(ldtf[ch0:ch0 + 128, None]),
                "are": np.ascontiguousarray(aref[ch0:ch0 + 128]),
                "aim": np.ascontiguousarray(aimf[ch0:ch0 + 128]),
                "cre": np.ascontiguousarray(cref[ch0:ch0 + 128]),
                "cim": np.ascontiguousarray(cimf[ch0:ch0 + 128]),
                "ddb": np.full((128, 1), ddf, dtype=np.float32),
                "uidx": uidx, "ussidx": ussidx, "yidx": yidx,
            })
    res = run_bass_kernel_spmd(nc, in_maps, core_ids=list(range(8)))
    out = np.empty((2, 8192, D), dtype=np.float32)
    for r in range(8):
        b, qq = r // 4, r % 4
        out[b, qq * TOK:(qq + 1) * TOK, :] = np.asarray(res.results[r]["xout"])
    if DEBUG:
        kernel.dbg = [res.results[r] for r in range(8)]
    return out
